# version 13
# baseline (speedup 1.0000x reference)
import numpy as np
import concourse.bass as bass
import concourse.mybir as mybir
from concourse.bass_utils import run_bass_kernel_spmd




class Buf:
    __slots__ = ("name", "last_write", "readers")

    def __init__(self, name=""):
        self.name = name
        self.last_write = None
        self.readers = []


class Prog:
    def __init__(self, nc):
        self.nc = nc
        self.sems = {}
        self.cnt = {}
        self.mark = {}
        self.bg = []

    def pump(self, k):
        for _ in range(min(k, len(self.bg))):
            self.bg.pop(0)()

    def flush_bg(self):
        while self.bg:
            self.bg.pop(0)()

    def sem(self, key):
        if key not in self.sems:
            self.sems[key] = self.nc.alloc_semaphore(key)
            self.cnt[key] = 0
        return self.sems[key]

    def _collect(self, reads, writes):
        need = {}

        def add(tok):
            if tok is None:
                return
            k, v = tok
            if need.get(k, 0) < v:
                need[k] = v

        for b in reads:
            add(b.last_write)
        for b in writes:
            add(b.last_write)
            for t in b.readers:
                add(t)
        return need

    def _wait(self, engine, eng, need):
        m = self.mark.setdefault(engine, {})
        for k in sorted(need):
            v = need[k]
            if m.get(k, 0) < v:
                eng.wait_ge(self.sem(k), v)
                m[k] = v

    def _finish(self, inst, engine, dma, reads, writes, key=None):
        if key is None:
            key = engine + ("_dma" if dma else "_op")
        inc = 16 if dma else 1
        s = self.sem(key)
        self.cnt[key] += inc
        inst.then_inc(s, inc)
        tok = (key, self.cnt[key])
        for b in writes:
            b.last_write = tok
            b.readers = []
        for b in reads:
            b.readers.append(tok)
        return tok

    def op(self, engine, method, *args, reads=(), writes=(), dma_key=None, **kw):
        eng = getattr(self.nc, engine)
        self._wait(engine, eng, self._collect(reads, writes))
        inst = getattr(eng, method)(*args, **kw)
        dma = method.startswith("dma_start")
        return self._finish(inst, engine, dma, reads, writes,
                            key=dma_key if dma else None)

    def mm_group(self, psum_writes, mms, engine="tensor"):
        eng = getattr(self.nc, engine)
        need = self._collect([], psum_writes)
        n = len(mms)
        insts = []
        for i, (lhsT, rhs, out, rbufs) in enumerate(mms):
            nd = dict(need) if i == 0 else {}
            for b in rbufs:
                t = b.last_write
                if t is not None and nd.get(t[0], 0) < t[1]:
                    nd[t[0]] = t[1]
            self._wait(engine, eng, nd)
            insts.append(
                eng.matmul(out, lhsT, rhs, start=(i == 0), stop=(i == n - 1))
            )
        tok = self._finish(insts[-1], engine, False, [], psum_writes)
        for _, _, _, rbufs in mms:
            for b in rbufs:
                if tok not in b.readers:
                    b.readers.append(tok)
        return tok

    def barrier_token(self, engine):
        key = engine + "_op"
        return (key, self.cnt.get(key, 0))

    def wait_all(self, engine, toks):
        eng = getattr(self.nc, engine)
        need = {}
        for t in toks:
            if t and need.get(t[0], 0) < t[1]:
                need[t[0]] = t[1]
        self._wait(engine, eng, need)



F32 = mybir.dt.float32
BF16 = mybir.dt.bfloat16
AF = mybir.ActivationFunctionType
ALU = mybir.AluOpType
AX = mybir.AxisListType

D = 2048
EC = D // 128


class Region:
    def __init__(self, nc, name, nbytes):
        self.nc = nc
        self.name = name
        self.t = nc.alloc_sbuf_tensor(name, [128, nbytes], mybir.dt.uint8)
        self.addr = nc.lookup_mloc(self.t).addr
        self.nbytes = nbytes
        self.live = []

    def place(self, name, shape, dtype, rel):
        per_part = 1
        for s in shape[1:]:
            per_part *= s
        per_part *= mybir.dt.size(dtype)
        assert rel + per_part <= self.nbytes, \
            f"{name}: {rel}+{per_part} > {self.nbytes} in {self.name}"
        h = self.nc.virtual_sbuf_tensor(name, list(shape), dtype)
        self.nc._tensor(h.name, list(h.shape), h.dtype, type="SB")
        mloc = self.nc.lookup_mloc(h)
        mloc.allocated = True
        mloc.addr = self.addr + rel
        return h

    def transition(self, bufs_in):
        toks = []
        for b in self.live:
            if b.last_write is not None:
                toks.append(b.last_write)
            toks.extend(b.readers)
        for b in bufs_in:
            b.readers.extend(toks)
        self.live = list(bufs_in)


class Ctx:

    def __init__(self, p, nc):
        self.p = p
        self.nc = nc
        a = nc.alloc_sbuf_tensor
        self.B = {}
        self.ident = a("c_ident", [128, 128], BF16)
        self.ones_col = a("c_ones", [1, 128], F32)
        self.cpool = a("c_pool", [128, 1], F32)
        nc.const_aps.register(F32, 1e-5, self.cpool[:, 0:1])
        self.xt = [a(f"w_x{i}", [128, D], F32) for i in range(2)]
        self.xn = a("w_xn", [128, D], BF16)
        self.xf = [a(f"w_xf{i}", [128, D], BF16) for i in range(2)]
        self.st = [a(f"w_st{i}", [128, 32], F32) for i in range(2)]
        self.stage = [a(f"w_stg{i}", [128, 1024], F32) for i in range(2)]
        self.wbounce = [a(f"w_wb{i}", [128, 1024], BF16) for i in range(2)]
        self.ps_tr = nc.alloc_psum_tensor("ps_tr", [128, D], BF16)
        self.ps_y = nc.alloc_psum_tensor("ps_y", [128, D], F32)
        self.ps_h = nc.alloc_psum_tensor("ps_h", [128, 512], F32)
        self.ps_b = nc.alloc_psum_tensor("ps_b", [128, 512], BF16)
        for name in ["ident", "ones_col", "cpool", "xn",
                     "ps_tr", "ps_h", "ps_b"]:
            self.B[name] = Buf(name)
        for q in range(4):
            self.B[f"ps_q{q}"] = Buf(f"ps_q{q}")
        for i in range(2):
            for name in ["xt", "xf", "st", "stage", "wbounce"]:
                self.B[f"{name}{i}"] = Buf(f"{name}{i}")
        self.RA = Region(nc, "regA", 64 * 1024)
        self.RB = Region(nc, "regB", 64 * 1024)
        self.RC = Region(nc, "regC", 36 * 1024)

    def emit_consts(self):
        p = self.p
        B = self.B
        p.op("gpsimd", "memset", self.ident[:], 0.0, writes=[B["ident"]])
        p.op("gpsimd", "affine_select", out=self.ident[:], in_=self.ident[:],
             compare_op=ALU.not_equal, fill=1.0, base=0, pattern=[[-1, 128]],
             channel_multiplier=1, reads=[B["ident"]], writes=[B["ident"]])
        p.op("gpsimd", "memset", self.ones_col[:], 1.0, writes=[B["ones_col"]])
        p.op("gpsimd", "memset", self.cpool[:, 0:1], 1e-5, writes=[B["cpool"]])

    def buf(self, name):
        if name not in self.B:
            self.B[name] = Buf(name)
        return self.B[name]

    def filler(self, tag=None):
        self.p.op("tensor", "matmul", self.ps_y[0:1, 1792:1856],
                  self.ident[:, 0:1], self.ident[:, 0:64],
                  start=True, stop=True,
                  reads=[self.B["ident"]], writes=[self.B["ps_q3"]])

    def psq(self, c0, c1):
        return [self.B[f"ps_q{q}"] for q in range(4)
                if q * 512 < c1 and (q + 1) * 512 > c0]


def ln_to_fm(ctx, par, src_ap, src_buf, want_fm=True, dma_engine="sync"):
    p = ctx.p
    xt, xn, xf, st = ctx.xt[par], ctx.xn, ctx.xf[par], ctx.st[par]
    ps = ctx.ps_tr
    bxt, bxn, bxf, bst = (ctx.B[f"xt{par}"], ctx.B["xn"],
                          ctx.B[f"xf{par}"], ctx.B[f"st{par}"])
    bps = ctx.B["ps_tr"]

    p.op(dma_engine, "dma_start", out=xt[:], in_=src_ap,
         reads=[src_buf], writes=[bxt], dma_key=f"xld{par}")
    p.pump(3)
    bnst = st[:, 0:24].rearrange("p (c s) -> p c s", c=4)
    mean, var = st[:, 24:25], st[:, 25:26]
    negmean, rstd, tmp = st[:, 26:27], st[:, 27:28], st[:, 28:29]
    for c in range(4):
        p.op("vector", "bn_stats", bnst[:, c], xt[:, c * 512:(c + 1) * 512],
             reads=[bxt], writes=[bst])
    p.op("vector", "bn_aggr", st[:, 24:26], bnst[:], reads=[bst], writes=[bst])
    p.op("vector", "tensor_scalar", out=negmean, in0=mean, scalar1=-1.0,
         scalar2=None, op0=ALU.mult, reads=[bst], writes=[bst])
    p.op("scalar", "activation", tmp, var, AF.Ln, bias=1e-5, scale=1.0,
         reads=[bst, ctx.B["cpool"]], writes=[bst])
    p.op("scalar", "activation", rstd, tmp, AF.Exp, bias=0.0, scale=-0.5,
         reads=[bst], writes=[bst])
    p.op("vector", "tensor_scalar", out=xn[:], in0=xt[:], scalar1=negmean,
         scalar2=rstd, op0=ALU.add, op1=ALU.mult,
         reads=[bxt, bst], writes=[bxn])
    out = {"xt": bxt, "xn": bxn}
    if want_fm:
        for ec in range(EC):
            sl = slice(ec * 128, (ec + 1) * 128)
            p.op("tensor", "transpose", ps[:, sl], xn[:, sl], ctx.ident[:],
                 reads=[bxn, ctx.B["ident"]], writes=[bps])
        p.op("vector", "tensor_copy", xf[:], ps[:], reads=[bps], writes=[bxf])
        out["xf"] = bxf
    return out


def _stream_cast(ctx, dst_slices_and_aps, buf, scale=None):
    p = ctx.p
    for i, (dst_sl, src_ap) in enumerate(dst_slices_and_aps):
        par = i % 2
        bj = buf[i] if isinstance(buf, list) else buf
        stg, bstg = ctx.stage[par], ctx.B[f"stage{par}"]
        n = src_ap.shape[-1]
        p.op("sync", "dma_start", out=stg[:, 0:n], in_=src_ap,
             writes=[bstg], dma_key=f"wstg{par}")
        if scale is None:
            p.op("vector", "tensor_copy", dst_sl, stg[:, 0:n],
                 reads=[bstg], writes=[bj])
        else:
            p.op("vector", "tensor_scalar", out=dst_sl, in0=stg[:, 0:n],
                 scalar1=scale, scalar2=None, op0=ALU.mult,
                 reads=[bstg], writes=[bj])


def emit_wcache(ctx, jobs, seg_buf):
    p = ctx.p
    for i, (dst_sl, src_ap) in enumerate(jobs):
        par = i % 2
        stg, bstg = ctx.stage[par], ctx.B[f"stage{par}"]
        wb, bwb = ctx.wbounce[par], ctx.B[f"wbounce{par}"]
        n = src_ap.shape[-1]
        p.op("sync", "dma_start", out=stg[:, 0:n], in_=src_ap,
             writes=[bstg], dma_key=f"wstg{par}")
        p.op("scalar", "copy", wb[:, 0:n], stg[:, 0:n],
             reads=[bstg], writes=[bwb])
        p.op("gpsimd", "dma_start", out=dst_sl, in_=wb[:, 0:n],
             reads=[bwb], writes=[seg_buf], dma_key=f"wcst{par}")


def mk_wseg(ctx, name, shape, jobs, eager=True):
    nc = ctx.nc
    ap_ = nc.dram_tensor("wc_" + name, shape, BF16, kind="Internal").ap()
    b = Buf("wc_" + name)
    if eager:
        emit_wcache(ctx, [(dst(ap_), src) for dst, src in jobs], b)
    else:
        for dst, src in jobs:
            ctx.p.bg.append(
                lambda d=dst, sr=src: emit_wcache(ctx, [(d(ap_), sr)], b))
    return (ap_, b)



def alloc_l0ffn(ctx):
    RC = ctx.RC
    ctx.l0_AT = RC.place("l0_AT", [128, 16 * 512], BF16, 0)
    ctx.l0_BT = RC.place("l0_BT", [128, 4 * 2048], BF16, 16384)
    ctx.l0_WrT = RC.place("l0_WrT", [128, 16 * 8], BF16, 32768)
    ctx.l0_brbc = RC.place("l0_brbc", [128, 8], F32, 33056)
    ctx.l0_lg = RC.place("l0_lg", [128, 24], F32, 33408)
    ctx.l0_mw = RC.place("l0_mw", [128, 8], F32, 33504)
    ctx.l0_hs = RC.place("l0_hs", [128, 512], BF16, 33568)
    ctx.l0_hf = RC.place("l0_hf", [128, 512], BF16, 34592)
    ctx.l0_brrow = ctx.nc.alloc_sbuf_tensor("l0_brrow", [1, 8], F32)
    for n in ["l0_AT", "l0_BT", "l0_WrT", "l0_brbc", "l0_lg", "l0_mw",
              "l0_hs", "l0_hf", "l0_brrow"]:
        ctx.B[n] = Buf(n)


def l0ffn_bufs(ctx):
    return [ctx.B[n] for n in ["l0_AT", "l0_BT", "l0_WrT", "l0_brbc",
                               "l0_lg", "l0_mw", "l0_hs", "l0_hf"]]


def load_l0ffn_weights(ctx, wc, ap_br):
    p = ctx.p
    B = ctx.B
    p.op("sync", "dma_start",
         out=ctx.l0_AT[:].rearrange("p (e r) -> p e r", e=16),
         in_=wc["l0_AT"][0], reads=[wc["l0_AT"][1]],
         writes=[B["l0_AT"]], dma_key="wld0")
    p.op("sync", "dma_start",
         out=ctx.l0_BT[:].rearrange("p (e r) -> p e r", e=4),
         in_=wc["l0_BT"][0], reads=[wc["l0_BT"][1]],
         writes=[B["l0_BT"]], dma_key="wld1")
    p.op("sync", "dma_start",
         out=ctx.l0_WrT[:].rearrange("p (e r) -> p e r", e=16),
         in_=wc["l0_Wr"][0], reads=[wc["l0_Wr"][1]],
         writes=[B["l0_WrT"]], dma_key="wld2")
    p.op("sync", "dma_start", out=ctx.l0_brrow[:], in_=ap_br,
         writes=[B["l0_brrow"]], dma_key="wbr")
    p.op("tensor", "matmul", ctx.ps_h[:, 0:8], ctx.ones_col[:],
         ctx.l0_brrow[:], start=True, stop=True,
         reads=[ctx.B["ones_col"], B["l0_brrow"]], writes=[B["ps_h"]])
    p.op("vector", "tensor_copy", ctx.l0_brbc[:], ctx.ps_h[:, 0:8],
         reads=[B["ps_h"]], writes=[B["l0_brbc"]])


def emit_l0ffn(ctx, src, dst, ntiles):
    p = ctx.p
    B = ctx.B
    src_name, src_aps = src
    dst_name, dst_aps = dst
    scale = 0.5 / 64.0
    lg, mw, hs, hf = ctx.l0_lg, ctx.l0_mw, ctx.l0_hs, ctx.l0_hf
    blg, bmw, bhs, bhf = B["l0_lg"], B["l0_mw"], B["l0_hs"], B["l0_hf"]

    for t in range(ntiles):
        par = t % 2
        r = ln_to_fm(ctx, par, src_aps[t], ctx.buf(f"{src_name}_t{t}"))
        bxt, bxf = r["xt"], r["xf"]
        xf = ctx.xf[par]
        ph, bph = ctx.ps_h, B["ps_h"]
        mms = [(xf[:, ec * 128:(ec + 1) * 128],
                ctx.l0_WrT[:, ec * 8:(ec + 1) * 8],
                ph[:, 0:8], [bxf, B["l0_WrT"]]) for ec in range(EC)]
        p.mm_group([bph], mms)
        p.op("vector", "tensor_tensor", out=lg[:, 0:8], in0=ph[:, 0:8],
             in1=ctx.l0_brbc[:, 0:8], op=ALU.add,
             reads=[bph, B["l0_brbc"]], writes=[blg])
        m1, negm1, esum, recip = (lg[:, 8:9], lg[:, 9:10], lg[:, 10:11],
                                  lg[:, 11:12])
        p.op("vector", "reduce_max", negm1, lg[:, 0:8], axis=AX.X, negate=True,
             reads=[blg], writes=[blg])
        p.op("vector", "tensor_scalar", out=m1, in0=negm1, scalar1=-1.0,
             scalar2=None, op0=ALU.mult, reads=[blg], writes=[blg])
        p.op("scalar", "activation", lg[:, 12:20], lg[:, 0:8], AF.Exp,
             bias=negm1, scale=1.0, accum_out=esum,
             reads=[blg], writes=[blg])
        p.op("vector", "reciprocal", recip, esum, reads=[blg], writes=[blg])
        p.op("vector", "tensor_scalar", out=mw[:], in0=lg[:, 0:8], scalar1=m1,
             scalar2=recip, op0=ALU.is_ge, op1=ALU.mult,
             reads=[blg], writes=[bmw])
        mms = [(xf[:, ec * 128:(ec + 1) * 128],
                ctx.l0_AT[:, ec * 512:(ec + 1) * 512],
                ph[:, 0:512], [bxf, B["l0_AT"]]) for ec in range(EC)]
        p.mm_group([bph], mms)
        for e in range(8):
            p.op("vector", "tensor_scalar", out=hs[:, e * 64:(e + 1) * 64],
                 in0=ph[:, e * 64:(e + 1) * 64], scalar1=mw[:, e:e + 1],
                 scalar2=None, op0=ALU.mult,
                 reads=[bph, bmw], writes=[bhs])
        for j in range(4):
            sl = slice(j * 128, (j + 1) * 128)
            p.op("tensor", "transpose", ctx.ps_b[:, sl], hs[:, sl],
                 ctx.ident[:], reads=[bhs, ctx.B["ident"]],
                 writes=[B["ps_b"]])
        p.op("vector", "tensor_copy", hf[:], ctx.ps_b[:],
             reads=[B["ps_b"]], writes=[bhf])
        py = ctx.ps_y
        for o in range(4):
            mms = [(hf[:, j * 128:(j + 1) * 128],
                    ctx.l0_BT[:, j * 2048 + o * 512: j * 2048 + (o + 1) * 512],
                    py[:, o * 512:(o + 1) * 512],
                    [bhf, B["l0_BT"]]) for j in range(4)]
            p.mm_group([B[f"ps_q{o}"]], mms)
        p.op("vector", "scalar_tensor_tensor", out=ctx.xt[par][:], in0=py[:],
             scalar=scale, in1=ctx.xt[par][:], op0=ALU.mult, op1=ALU.add,
             reads=ctx.psq(0, 2048) + [bxt], writes=[bxt])
        p.op("gpsimd", "dma_start", out=dst_aps[t], in_=ctx.xt[par][:],
             reads=[bxt], writes=[ctx.buf(f"{dst_name}_t{t}")],
             dma_key=f"xst{par}")



def _alibi_slopes16():
    start = 2.0 ** -0.5
    return [start ** (i + 1) for i in range(16)]


def alloc_l0attn(ctx, ntb):
    nc = ctx.nc
    RA, RB, RC = ctx.RA, ctx.RB, ctx.RC
    ctx.Afm = RA.place("l0a_Afm", [128, 16 * 2048], BF16, 0)
    ctx.Bv = RB.place("l0a_Bv", [128, 16 * 2048], BF16, 0)
    ctx.Wh = RC.place("l0a_Wh", [128, 16 * 1024], BF16, 0)
    ctx.s_sb = [RC.place(f"l0a_s{i}", [128, 2048], F32, i * 8192)
                for i in range(2)]
    ctx.PP = RC.place("l0a_PP", [128, 2048], BF16, 16384)
    ctx.PT = RC.place("l0a_PTs", [128, 2048], BF16, 20480)
    ctx.zfull = RC.place("l0a_z", [128, 2048], BF16, 24576)
    ctx.causal = RC.place("l0a_causal", [128, 128], F32, 28672)
    ctx.att_st = RC.place("l0a_ast", [128, 16], F32, 29184)
    ctx.iota_row = RC.place("l0a_iota", [128, D], mybir.dt.float16,
                            29760)
    ctx.zf_fm = RC.place("l0a_zf", [128, 2048], BF16, 0)
    ctx.ktmp = ctx.xn[:, 0:1024]
    ctx.qstg = ctx.xn[:, 1024:2048]
    ctx.xn_d = nc.dram_tensor("l0a_xn_d", [ntb, 128, 2048], BF16,
                              kind="Internal").ap()
    ctx.q_d = nc.dram_tensor("l0a_q_d", [ntb, 128, 2048], BF16,
                             kind="Internal").ap()
    ctx.z_d = nc.dram_tensor("l0a_z_d", [ntb, 128, 2048], BF16,
                             kind="Internal").ap()
    for n in ["Afm", "Bv", "Wh", "s_sb0", "s_sb1", "PP", "PT", "zfull",
              "zf_fm", "causal", "att_st", "iota_row"]:
        ctx.B[n] = Buf(n)
    ctx.B["ktmp"] = ctx.B["xn"]


def l0attn_bufs_phA(ctx):
    return [ctx.B[n] for n in ["Wh"]]


def l0attn_bufs_phB(ctx):
    return [ctx.B[n] for n in ["s_sb0", "s_sb1", "PP", "PT", "zfull",
                               "causal", "att_st", "iota_row"]]


def l0attn_bufs_phC(ctx):
    return [ctx.B[n] for n in ["zfull", "zf_fm"]]


def emit_l0attn_consts(ctx):
    p = ctx.p
    p.op("gpsimd", "memset", ctx.causal[:], 0.0, writes=[ctx.B["causal"]])
    p.op("gpsimd", "affine_select", out=ctx.causal[:], in_=ctx.causal[:],
         compare_op=ALU.is_ge, fill=-1e9, base=0, pattern=[[-1, 128]],
         channel_multiplier=1, reads=[ctx.B["causal"]],
         writes=[ctx.B["causal"]])


def emit_l0attn(ctx, src, dst, w, b_tiles_list):
    p = ctx.p
    B = ctx.B
    src_name, src_aps = src
    dst_name, dst_aps = dst
    slopes = _alibi_slopes16()
    Afm_v = ctx.Afm[:].rearrange("p (h k) -> p h k", h=16)
    Bv_v = ctx.Bv[:].rearrange("p (t c) -> p t c", t=16)

    for b, tiles in enumerate(b_tiles_list):
        ntb = len(tiles)
        for i, t in enumerate(tiles):
            par = i % 2
            r = ln_to_fm(ctx, par, src_aps[t], ctx.buf(f"{src_name}_t{t}"))
            p.op("gpsimd", "dma_start", out=ctx.xn_d[i], in_=ctx.xf[par][:],
                 reads=[r["xf"]], writes=[ctx.buf(f"xnd_{i}")],
                 dma_key=f"xnd{par}")
        ctx.RA.transition([B["Afm"]])
        ctx.RB.transition([B["Bv"]])
        ctx.RC.transition(l0attn_bufs_phA(ctx))
        for mat in ["k", "q", "v"]:
            qscale = (1.0 / (128.0 ** 0.5)) if mat == "q" else None
            for half in range(2):
                p.op("sync", "dma_start",
                     out=ctx.Wh[:].rearrange("p (e c) -> p e c", e=16),
                     in_=w[mat][0][:, half], reads=[w[mat][1]],
                     writes=[B["Wh"]], dma_key="whld")
                for i in range(ntb):
                    par = i % 2
                    xfp, bxf = ctx.xf[par], ctx.B[f"xf{par}"]
                    p.op("sync", "dma_start", out=xfp[:], in_=ctx.xn_d[i],
                         reads=[ctx.buf(f"xnd_{i}")], writes=[bxf],
                         dma_key=f"xld{par}")
                    p.pump(4)
                    for o in range(2):
                        q = (2 * i + o) % 4
                        mms = [(xfp[:, ec * 128:(ec + 1) * 128],
                                ctx.Wh[:, ec * 1024 + o * 512:
                                       ec * 1024 + (o + 1) * 512],
                                ctx.ps_y[:, q * 512:(q + 1) * 512],
                                [bxf, B["Wh"]]) for ec in range(EC)]
                        p.mm_group([B[f"ps_q{q}"]], mms)
                        psl = ctx.ps_y[:, q * 512:(q + 1) * 512]
                        prd = [B[f"ps_q{q}"]]
                        if mat == "v":
                            p.op("scalar", "copy",
                                 ctx.Bv[:, i * 2048 + half * 1024 + o * 512:
                                        i * 2048 + half * 1024 +
                                        (o + 1) * 512],
                                 psl, reads=prd, writes=[B["Bv"]])
                        elif qscale is None:
                            p.op("scalar", "copy",
                                 ctx.ktmp[:, o * 512:(o + 1) * 512], psl,
                                 reads=prd, writes=[B["ktmp"]])
                        else:
                            p.op("scalar", "mul",
                                 ctx.ktmp[:, o * 512:(o + 1) * 512], psl,
                                 qscale, reads=prd, writes=[B["ktmp"]])
                    if mat == "v":
                        continue
                    for j in range(8):
                        sl = slice(j * 128, (j + 1) * 128)
                        p.op("tensor", "transpose", ctx.ps_tr[:, sl],
                             ctx.ktmp[:, sl], ctx.ident[:],
                             reads=[B["ktmp"], B["ident"]],
                             writes=[B["ps_tr"]])
                    if mat == "k":
                        p.op("vector", "tensor_copy",
                             Afm_v[:, half * 8:(half + 1) * 8,
                                   i * 128:(i + 1) * 128],
                             ctx.ps_tr[:, 0:1024].rearrange(
                                 "p (h k) -> p h k", h=8),
                             reads=[B["ps_tr"]], writes=[B["Afm"]])
                    else:
                        p.op("vector", "tensor_copy", ctx.qstg,
                             ctx.ps_tr[:, 0:1024],
                             reads=[B["ps_tr"]], writes=[B["xn"]])
                        p.op("gpsimd", "dma_start",
                             out=ctx.q_d[i][:, half * 1024:(half + 1) * 1024],
                             in_=ctx.qstg,
                             reads=[B["xn"]], writes=[ctx.buf(f"qd_d{i}")],
                             dma_key="qst")
        ctx.RC.transition(l0attn_bufs_phB(ctx))
        p.op("gpsimd", "iota", ctx.iota_row[:], [[1, D]], base=0,
             channel_multiplier=0, allow_small_or_imprecise_dtypes=True,
             writes=[B["iota_row"]])
        emit_l0attn_consts(ctx)
        iters = [(qtl, h) for qtl in range(ntb) for h in range(16)]
        n_it = len(iters)


        def qd_slot(qtl):
            return ctx.xf[qtl % 2], ctx.B[f"xf{qtl % 2}"]

        def s_qload(qtl):
            qd, bqd = qd_slot(qtl)
            p.op("sync", "dma_start", out=qd[:], in_=ctx.q_d[qtl],
                 reads=[ctx.buf(f"qd_d{qtl}")], writes=[bqd],
                 dma_key=f"qld{qtl % 2}")

        def chunks_of(i):
            qtl, h = iters[i]
            kend = (qtl + 1) * 128
            out = []
            for ci, c0 in enumerate(range(0, kend, 1024)):
                c1 = min(c0 + 1024, kend)
                pairbase = (ci if kend > 1024 else i % 2) * 1024
                out.append((c0, c1, pairbase))
            return out

        def s1_scores(i):
            qtl, h = iters[i]
            qd, bqd = qd_slot(qtl)
            p.pump(1)
            if h == 0:
                s_qload(qtl)
            for c0, c1, pb in chunks_of(i):
                for s0 in range(c0, c1, 512):
                    s1 = min(s0 + 512, c1)
                    pcol = pb + (s0 - c0)
                    p.op("tensor", "matmul",
                         ctx.ps_y[:, pcol:pcol + (s1 - s0)],
                         qd[:, h * 128:(h + 1) * 128],
                         ctx.Afm[:, h * 2048 + s0:h * 2048 + s1],
                         start=True, stop=True,
                         reads=[bqd, B["Afm"]],
                         writes=ctx.psq(pcol, pcol + (s1 - s0)))

        def s2a_evac(i):
            qtl, h = iters[i]
            kend = (qtl + 1) * 128
            qb = qtl * 128
            ss, bss = ctx.s_sb[i % 2], B[f"s_sb{i % 2}"]
            for c0, c1, pb in chunks_of(i):
                if qb >= c1:
                    p.op("scalar", "copy", ss[:, c0:c1],
                         ctx.ps_y[:, pb:pb + (c1 - c0)],
                         reads=ctx.psq(pb, pb + (c1 - c0)), writes=[bss])
                    continue
                if qb > c0:
                    p.op("scalar", "copy", ss[:, c0:qb],
                         ctx.ps_y[:, pb:pb + (qb - c0)],
                         reads=ctx.psq(pb, pb + (qb - c0)), writes=[bss])
                p.op("vector", "scalar_tensor_tensor", out=ss[:, qb:kend],
                     in0=ctx.ps_y[:, pb + (qb - c0):pb + (c1 - c0)],
                     scalar=1.0, in1=ctx.causal[:],
                     op0=ALU.mult, op1=ALU.add,
                     reads=ctx.psq(pb + (qb - c0), pb + (c1 - c0)) +
                     [B["causal"]], writes=[bss])

        def s2b_stats(i):
            qtl, h = iters[i]
            kend = (qtl + 1) * 128
            ss, bss = ctx.s_sb[i % 2], B[f"s_sb{i % 2}"]
            ast = ctx.att_st[:, 8 * (i % 2):8 * (i % 2) + 8]
            p.op("vector", "scalar_tensor_tensor", out=ss[:, 0:kend],
                 in0=ctx.iota_row[:, 0:kend], scalar=slopes[h],
                 in1=ss[:, 0:kend], op0=ALU.mult, op1=ALU.add,
                 reads=[B["iota_row"], bss], writes=[bss])
            p.op("vector", "reduce_max", ast[:, 0:1], ss[:, 0:kend],
                 axis=AX.X, negate=True,
                 reads=[bss], writes=[B["att_st"]])

        def s3_exp(i):
            qtl, h = iters[i]
            kend = (qtl + 1) * 128
            ss, bss = ctx.s_sb[i % 2], B[f"s_sb{i % 2}"]
            ast = ctx.att_st[:, 8 * (i % 2):8 * (i % 2) + 8]
            p.op("scalar", "activation", ctx.PP[:, 0:kend], ss[:, 0:kend],
                 AF.Exp, bias=ast[:, 0:1], accum_out=ast[:, 1:2],
                 reads=[bss, B["att_st"]],
                 writes=[B["PP"], B["att_st"]])

        def s4567_tail(i):
            qtl, h = iters[i]
            kend = (qtl + 1) * 128
            nkc = qtl + 1
            ast = ctx.att_st[:, 8 * (i % 2):8 * (i % 2) + 8]
            for kc in range(nkc):
                sl = slice(kc * 128, (kc + 1) * 128)
                p.op("tensor", "transpose", ctx.ps_tr[:, sl],
                     ctx.PP[:, sl], ctx.ident[:],
                     reads=[B["PP"], B["ident"]], writes=[B["ps_tr"]])
            p.op("vector", "reciprocal", ast[:, 2:3], ast[:, 1:2],
                 reads=[B["att_st"]], writes=[B["att_st"]])
            p.op("scalar", "copy", ctx.PT[:, 0:kend],
                 ctx.ps_tr[:, 0:kend],
                 reads=[B["ps_tr"]], writes=[B["PT"]])
            mms = [(ctx.PT[:, kc * 128:(kc + 1) * 128],
                    Bv_v[:, kc, h * 128:(h + 1) * 128],
                    ctx.ps_h[:, 0:128], [B["PT"], B["Bv"]])
                   for kc in range(nkc)]
            p.mm_group([B["ps_h"]], mms)
            p.op("vector", "tensor_scalar",
                 out=ctx.zfull[:, h * 128:(h + 1) * 128],
                 in0=ctx.ps_h[:, 0:128], scalar1=ast[:, 2:3], scalar2=None,
                 op0=ALU.mult, reads=[B["ps_h"], B["att_st"]],
                 writes=[B["zfull"]])
            if h == 15:
                p.op("gpsimd", "dma_start", out=ctx.z_d[qtl],
                     in_=ctx.zfull[:],
                     reads=[B["zfull"]], writes=[ctx.buf(f"zd_{qtl}")],
                     dma_key=f"zst{qtl % 2}")

        s1_scores(0)
        s2a_evac(0)
        s2b_stats(0)
        s1_scores(1)
        for i in range(n_it):
            s3_exp(i)
            if i + 1 < n_it:
                s2a_evac(i + 1)
                s2b_stats(i + 1)
            if i + 2 < n_it:
                s1_scores(i + 2)
            s4567_tail(i)
        p.flush_bg()
        ctx.RC.transition(l0attn_bufs_phC(ctx))
        p.op("sync", "dma_start",
             out=Afm_v[:], in_=w["wo"][0], reads=[w["wo"][1]],
             writes=[B["Afm"]], dma_key="whld")
        for qtl in range(ntb):
            t = tiles[qtl]
            par = qtl % 2
            p.op("sync", "dma_start", out=ctx.zfull[:], in_=ctx.z_d[qtl],
                 reads=[ctx.buf(f"zd_{qtl}")], writes=[B["zfull"]],
                 dma_key="zld")
            for hc in range(EC):
                sl = slice(hc * 128, (hc + 1) * 128)
                p.op("tensor", "transpose", ctx.ps_tr[:, sl],
                     ctx.zfull[:, sl], ctx.ident[:],
                     reads=[B["zfull"], B["ident"]], writes=[B["ps_tr"]])
            p.op("vector", "tensor_copy", ctx.zf_fm[:], ctx.ps_tr[:],
                 reads=[B["ps_tr"]], writes=[B["zf_fm"]])
            for o in range(4):
                mms = [(ctx.zf_fm[:, hc * 128:(hc + 1) * 128],
                        Afm_v[:, hc, o * 512:(o + 1) * 512],
                        ctx.ps_y[:, o * 512:(o + 1) * 512],
                        [B["zf_fm"], B["Afm"]]) for hc in range(EC)]
                p.mm_group([B[f"ps_q{o}"]], mms)
            xt, bxt = ctx.xt[par], ctx.B[f"xt{par}"]
            p.op("sync", "dma_start", out=xt[:], in_=src_aps[t],
                 reads=[ctx.buf(f"{src_name}_t{t}")], writes=[bxt],
                 dma_key=f"xld{par}")
            p.op("vector", "tensor_tensor", out=xt[:], in0=ctx.ps_y[:],
                 in1=xt[:], op=ALU.add, reads=ctx.psq(0, 2048) + [bxt],
                 writes=[bxt])
            p.op("gpsimd", "dma_start", out=dst_aps[t], in_=xt[:],
                 reads=[bxt], writes=[ctx.buf(f"{dst_name}_t{t}")],
                 dma_key=f"xst{par}")



def alloc_l1ffn(ctx, ntiles):
    nc = ctx.nc
    RA, RB, RC = ctx.RA, ctx.RB, ctx.RC
    ctx.w13 = RA.place("l1f_w13", [128, 4 * 16 * 512], BF16, 0)
    ctx.w2p = RB.place("l1f_w2", [128, 2 * 4 * 2048], BF16, 0)
    ctx.l1_rwT = RC.place("l1f_rwT", [128, 16 * 8], BF16, 0)
    ctx.l1_lg = RC.place("l1f_lg", [128, 48], F32, 256)
    ctx.l1_sl = [RC.place(f"l1f_sl{i}", [128, 512], BF16, 512 + i * 1024)
                 for i in range(2)]
    ctx.l1_h = [RC.place(f"l1f_h{i}", [128, 512], BF16, 2560 + i * 1024)
                for i in range(2)]
    ctx.l1_hf = [RC.place(f"l1f_hf{i}", [128, 2 * 512], BF16,
                          4608 + i * 2048) for i in range(2)]
    ctx.l1_mw = RC.place("l1f_mw", [128, 8 * ntiles], F32, 8704)
    ctx.l1_xn_d = nc.dram_tensor("l1f_xn_d", [ntiles, 128, 2048], BF16,
                                 kind="Internal").ap()
    for n in ["l1_rwT", "l1_lg", "l1_mw", "l1_sl0", "l1_sl1", "l1_h0",
              "l1_h1", "l1_hf0", "l1_hf1"]:
        ctx.B[n] = Buf(n)
    for i in range(4):
        ctx.B[f"w13_{i}"] = Buf(f"w13_{i}")
    for i in range(2):
        ctx.B[f"w2p_{i}"] = Buf(f"w2p_{i}")


def l1ffn_bufs_p0(ctx):
    return [ctx.B[n] for n in ["l1_rwT", "l1_lg", "l1_mw"]]


def l1ffn_bufs_px(ctx):
    return [ctx.B[n] for n in ["l1_rwT", "l1_lg", "l1_mw", "l1_sl0",
                               "l1_sl1", "l1_h0", "l1_h1", "l1_hf0",
                               "l1_hf1"]]


def w13_bufs(ctx):
    return [ctx.B[f"w13_{i}"] for i in range(4)]


def w2p_bufs(ctx):
    return [ctx.B[f"w2p_{i}"] for i in range(2)]


def emit_l1ffn(ctx, src, work, dst, w, ntiles):
    p = ctx.p
    B = ctx.B
    src_name, src_aps = src
    work_name, work_aps = work
    dst_name, dst_aps = dst
    lg = ctx.l1_lg
    l, mask1, l2 = lg[:, 0:8], lg[:, 8:16], lg[:, 16:24]
    mw1, mw2 = lg[:, 24:32], lg[:, 32:40]
    negm1, m1, negm2, m2 = (lg[:, 40:41], lg[:, 41:42], lg[:, 42:43],
                            lg[:, 43:44])
    dd, e2, onep, w1s, w2s = (lg[:, 44:45], lg[:, 45:46], lg[:, 46:47],
                              lg[:, 47:48], lg[:, 43:44])

    def router_tile(t, xf, bxf):
        mms = [(xf[:, ec * 128:(ec + 1) * 128],
                ctx.l1_rwT[:, ec * 8:(ec + 1) * 8],
                ctx.ps_h[:, 0:8], [bxf, B["l1_rwT"]]) for ec in range(EC)]
        p.mm_group([B["ps_h"]], mms)
        blg = B["l1_lg"]
        p.op("vector", "tensor_copy", l, ctx.ps_h[:, 0:8],
             reads=[B["ps_h"]], writes=[blg])
        p.op("vector", "reduce_max", negm1, l, axis=AX.X, negate=True,
             reads=[blg], writes=[blg])
        p.op("vector", "tensor_scalar", out=m1, in0=negm1, scalar1=-1.0,
             scalar2=None, op0=ALU.mult, reads=[blg], writes=[blg])
        p.op("vector", "tensor_scalar", out=mask1, in0=l, scalar1=m1,
             scalar2=None, op0=ALU.is_ge, reads=[blg], writes=[blg])
        p.op("vector", "scalar_tensor_tensor", out=l2, in0=mask1,
             scalar=-1e9, in1=l, op0=ALU.mult, op1=ALU.add,
             reads=[blg], writes=[blg])
        p.op("vector", "reduce_max", negm2, l2, axis=AX.X, negate=True,
             reads=[blg], writes=[blg])
        p.op("vector", "tensor_tensor", out=dd, in0=negm2, in1=negm1,
             op=ALU.subtract, reads=[blg], writes=[blg])
        p.op("scalar", "activation", e2, dd, AF.Exp, bias=0.0, scale=-1.0,
             reads=[blg], writes=[blg])
        p.op("vector", "tensor_scalar", out=onep, in0=e2, scalar1=1.0,
             scalar2=None, op0=ALU.add, reads=[blg], writes=[blg])
        p.op("vector", "reciprocal", w1s, onep, reads=[blg], writes=[blg])
        p.op("vector", "tensor_scalar", out=mw1, in0=l, scalar1=m1,
             scalar2=w1s, op0=ALU.is_ge, op1=ALU.mult,
             reads=[blg], writes=[blg])
        p.op("vector", "tensor_scalar", out=m2, in0=negm2, scalar1=-1.0,
             scalar2=None, op0=ALU.mult, reads=[blg], writes=[blg])
        p.op("vector", "tensor_scalar", out=mw2, in0=l2, scalar1=m2,
             scalar2=None, op0=ALU.is_ge, reads=[blg], writes=[blg])
        p.op("vector", "tensor_tensor", out=m2, in0=e2, in1=w1s,
             op=ALU.mult, reads=[blg], writes=[blg])
        p.op("vector", "tensor_scalar", out=mw2, in0=mw2, scalar1=m2,
             scalar2=None, op0=ALU.mult, reads=[blg], writes=[blg])
        p.op("vector", "tensor_tensor", out=ctx.l1_mw[:, t * 8:(t + 1) * 8],
             in0=mw1, in1=mw2, op=ALU.add,
             reads=[blg], writes=[B["l1_mw"]])

    ctx.RA.transition(w13_bufs(ctx))
    ctx.RB.transition(w2p_bufs(ctx))
    ctx.RC.transition(l1ffn_bufs_px(ctx))
    rwT_v = ctx.l1_rwT[:].rearrange("p (e r) -> p e r", e=16)
    p.op("sync", "dma_start", out=rwT_v[:], in_=w["rwT"][0],
         reads=[w["rwT"][1]], writes=[B["l1_rwT"]], dma_key="wldr")
    w13_v = ctx.w13[:].rearrange("p (s e c) -> p s e c", s=4, e=16)
    w2_v = ctx.w2p[:].rearrange("p (s f c) -> p s f c", s=2, f=4)
    for pr in range(4):
        e0, e1 = 2 * pr, 2 * pr + 1
        for si, (mat, e) in enumerate([("w1", e0), ("w3", e0),
                                       ("w1", e1), ("w3", e1)]):
            p.op("sync", "dma_start", out=w13_v[:, si],
                 in_=w[mat][0][:, e], reads=[w[mat][1]],
                 writes=[B[f"w13_{si}"]], dma_key=f"wld13_{si}")
        for si, e in enumerate([e0, e1]):
            p.op("sync", "dma_start", out=w2_v[:, si],
                 in_=w["w2"][0][:, e], reads=[w["w2"][1]],
                 writes=[B[f"w2p_{si}"]], dma_key=f"wldw2_{si}")
        first = pr == 0
        last = pr == 3
        rd = src_aps if first else work_aps
        rdname = src_name if first else work_name
        wr = dst_aps if last else work_aps
        wrname = dst_name if last else work_name
        for t in range(ntiles):
            par = t % 2
            xf, bxf = ctx.xf[par], ctx.B[f"xf{par}"]
            if first:
                r = ln_to_fm(ctx, par, src_aps[t],
                             ctx.buf(f"{src_name}_t{t}"))
                p.op("gpsimd", "dma_start", out=ctx.l1_xn_d[t], in_=xf[:],
                     reads=[r["xf"]], writes=[ctx.buf(f"l1xnd_{t}")],
                     dma_key=f"xnd{par}")
                router_tile(t, xf, bxf)
            else:
                p.op("sync", "dma_start", out=xf[:], in_=ctx.l1_xn_d[t],
                     reads=[ctx.buf(f"l1xnd_{t}")], writes=[bxf],
                     dma_key=f"xld{par}")
            for s, e in enumerate([e0, e1]):
                x1ps = ctx.ps_h[:, 0:512] if s == 0 else ctx.ps_y[:, 512:1024]
                bx1 = B["ps_h"] if s == 0 else B["ps_q1"]
                x3ps = ctx.ps_y[:, 0:512] if s == 0 else ctx.ps_y[:,
                                                                 1024:1536]
                bx3 = B["ps_q0"] if s == 0 else B["ps_q2"]
                mms = [(xf[:, ec * 128:(ec + 1) * 128],
                        w13_v[:, 2 * s, ec], x1ps,
                        [bxf, B[f"w13_{2 * s}"]]) for ec in range(EC)]
                p.mm_group([bx1], mms)
                mms = [(xf[:, ec * 128:(ec + 1) * 128],
                        w13_v[:, 2 * s + 1, ec], x3ps,
                        [bxf, B[f"w13_{2 * s + 1}"]]) for ec in range(EC)]
                p.mm_group([bx3], mms)
                slt, bslt = ctx.l1_sl[par], B[f"l1_sl{par}"]
                ht, bht = ctx.l1_h[par], B[f"l1_h{par}"]
                p.op("scalar", "activation", slt[:], x1ps,
                     AF.Silu, reads=[bx1], writes=[bslt])
                p.op("vector", "scalar_tensor_tensor", out=ht[:],
                     in0=x3ps,
                     scalar=ctx.l1_mw[:, t * 8 + (e0 + s):t * 8 + (e0 + s) + 1],
                     in1=slt[:], op0=ALU.mult, op1=ALU.mult,
                     reads=[bx3, B["l1_mw"], bslt],
                     writes=[bht])
                btr = B["ps_b"] if s == 0 else B["ps_tr"]
                for j in range(4):
                    sl = slice(j * 128, (j + 1) * 128)
                    p.op("tensor", "transpose",
                         ctx.ps_b[:, sl] if s == 0 else ctx.ps_tr[:, sl],
                         ht[:, sl], ctx.ident[:],
                         reads=[bht, ctx.B["ident"]],
                         writes=[btr])
                p.op("vector", "tensor_copy",
                     ctx.l1_hf[par][:, s * 512:(s + 1) * 512],
                     ctx.ps_b[:, 0:512] if s == 0 else ctx.ps_tr[:, 0:512],
                     reads=[btr], writes=[B[f"l1_hf{par}"]])
            for o in range(4):
                mms = []
                for s in range(2):
                    for f in range(4):
                        mms.append((
                            ctx.l1_hf[par][:, s * 512 + f * 128:
                                           s * 512 + (f + 1) * 128],
                            w2_v[:, s, f, o * 512:(o + 1) * 512],
                            ctx.ps_y[:, o * 512:(o + 1) * 512],
                            [B[f"l1_hf{par}"], B[f"w2p_{s}"]]))
                p.mm_group([B[f"ps_q{o}"]], mms)
            xt, bxt = ctx.xt[par], ctx.B[f"xt{par}"]
            if not first:
                p.op("sync", "dma_start", out=xt[:], in_=rd[t],
                     reads=[ctx.buf(f"{rdname}_t{t}")], writes=[bxt],
                     dma_key=f"rmw{par}")
            p.op("vector", "scalar_tensor_tensor", out=xt[:],
                 in0=ctx.ps_y[:], scalar=0.5, in1=xt[:],
                 op0=ALU.mult, op1=ALU.add,
                 reads=ctx.psq(0, 2048) + [bxt], writes=[bxt])
            p.op("gpsimd", "dma_start", out=wr[t], in_=xt[:],
                 reads=[bxt], writes=[ctx.buf(f"{wrname}_t{t}")],
                 dma_key=f"xst{par}")



def alloc_l1attn(ctx, ntb):
    nc = ctx.nc
    RA, RB, RC = ctx.RA, ctx.RB, ctx.RC
    ctx.l1a_Q = RA.place("l1a_Q", [128, 16 * 2048], BF16, 0)
    ctx.l1a_PT = RB.place("l1a_PT", [128, 16 * 2048], BF16, 0)
    ctx.l1a_WhB = [RB.place(f"l1a_Wh{i}", [128, 16 * 1024], BF16, i * 32768)
                   for i in range(2)]
    ctx.l1a_qtmp = RC.place("l1a_qtmp", [128, 2048], BF16, 32768)
    ctx.l1a_kfm = [RC.place("l1a_kfm0", [128, 2048], BF16, 0)] * 2
    ctx.l1a_s16 = RC.place("l1a_s16", [128, 2048], mybir.dt.float16, 4096)
    ctx.l1a_kw = [RC.place(f"l1a_kw{i}", [128, 2048], BF16, 8192 + i * 4096)
                  for i in range(2)]
    ctx.l1a_vd = RC.place("l1a_vd", [128, 16 * 512], BF16, 16384)
    ctx.l1a_av = RC.place("l1a_av", [128, 512], F32, 32768)
    ctx.l1a_ast = RC.place("l1a_ast", [128, 8], F32, 34816)
    ctx.l1a_abf = RC.place("l1a_abf", [128, 2048], BF16, 0)
    ctx.l1a_zf = RC.place("l1a_zf", [128, 2048], BF16, 4096)
    ctx.fcb_f = RC.place("l1a_fcb_f", [1, 2048], F32, 8192)
    ctx.fcb = RC.place("l1a_fcb", [1, 2048], BF16, 16384)
    ctx.ones_bf = nc.alloc_sbuf_tensor("c_ones_bf", [1, 128], BF16)
    ctx.l1a_xn_d = nc.dram_tensor("l1a_xn_d", [ntb, 128, 2048], BF16,
                                  kind="Internal").ap()
    ctx.l1a_k_d = nc.dram_tensor("l1a_k_d", [ntb, 128, 2048], BF16,
                                 kind="Internal").ap()
    ctx.l1a_v_d = nc.dram_tensor("l1a_v_d", [ntb, 128, 2048], BF16,
                                 kind="Internal").ap()
    ctx.l1a_at_d = nc.dram_tensor("l1a_at_d", [ntb, 128, 2048], F32,
                                  kind="Internal").ap()
    for n in (["l1a_Q", "l1a_PT", "l1a_Wh0", "l1a_Wh1", "l1a_qtmp",
               "l1a_kfm0", "l1a_s16", "l1a_kw0", "l1a_kw1", "l1a_vd",
               "l1a_ast", "l1a_av", "l1a_abf", "l1a_zf", "ones_bf",
               "fcb_f", "fcb"]):
        ctx.B[n] = Buf(n)
    ctx.B["l1a_kfm1"] = ctx.B["l1a_kfm0"]


def l1attn_bufs_ph1(ctx):
    return [ctx.B[n] for n in ["l1a_qtmp"]]


def l1attn_bufs_ph23(ctx):
    return [ctx.B[n] for n in ["l1a_kfm0", "l1a_s16", "l1a_kw0", "l1a_kw1",
                               "l1a_vd", "l1a_ast", "l1a_av", "l1a_qtmp"]]


def l1attn_bufs_ph4(ctx):
    return [ctx.B[n] for n in ["l1a_abf", "l1a_zf", "l1a_ast", "fcb_f",
                               "fcb"]]


def emit_l1attn(ctx, src, dst, w, b_tiles_list):
    p = ctx.p
    B = ctx.B
    src_name, src_aps = src
    dst_name, dst_aps = dst
    scale = 1.0 / (2048.0 ** 0.5)
    Q_v = ctx.l1a_Q[:].rearrange("p (d n) -> p d n", d=16)
    PT_v = ctx.l1a_PT[:].rearrange("p (m n) -> p m n", m=16)
    vd_v = ctx.l1a_vd[:].rearrange("p (m c) -> p m c", m=16)
    p.op("gpsimd", "memset", ctx.ones_bf[:], 1.0, writes=[B["ones_bf"]])

    for b, tiles in enumerate(b_tiles_list):
        ntb = len(tiles)
        nend = ntb * 128
        for i, t in enumerate(tiles):
            par = i % 2
            r = ln_to_fm(ctx, par, src_aps[t], ctx.buf(f"{src_name}_t{t}"))
            p.op("gpsimd", "dma_start", out=ctx.l1a_xn_d[i],
                 in_=ctx.xf[par][:], reads=[r["xf"]],
                 writes=[ctx.buf(f"l1axn_{i}")], dma_key=f"xnd{par}")
        for h in range(3):
            ctx.RA.transition([B["l1a_Q"]])
            ctx.RB.transition([B["l1a_Wh0"], B["l1a_Wh1"]])
            ctx.RC.transition(l1attn_bufs_ph1(ctx))
            jobs = [(mat, half) for mat in ["q", "k", "v"]
                    for half in range(2)]
            for j, (mat, half) in enumerate(jobs):
                wh = ctx.l1a_WhB[j % 2]
                bwh = B[f"l1a_Wh{j % 2}"]
                p.op("sync", "dma_start",
                     out=wh[:].rearrange("p (e c) -> p e c", e=16),
                     in_=w[mat][0][:, h, half], reads=[w[mat][1]],
                     writes=[bwh], dma_key=f"whld{j % 2}")
                for i in range(ntb):
                    par = i % 2
                    xfp, bxf = ctx.xf[par], ctx.B[f"xf{par}"]
                    p.op("sync", "dma_start", out=xfp[:],
                         in_=ctx.l1a_xn_d[i],
                         reads=[ctx.buf(f"l1axn_{i}")], writes=[bxf],
                         dma_key=f"xld{par}")
                    for o in range(2):
                        q = (2 * i + o) % 4
                        mms = [(xfp[:, ec * 128:(ec + 1) * 128],
                                wh[:, ec * 1024 + o * 512:
                                   ec * 1024 + (o + 1) * 512],
                                ctx.ps_y[:, q * 512:(q + 1) * 512],
                                [bxf, bwh]) for ec in range(EC)]
                        p.mm_group([B[f"ps_q{q}"]], mms)
                        psl = ctx.ps_y[:, q * 512:(q + 1) * 512]
                        prd = [B[f"ps_q{q}"]]
                        if mat == "q":
                            p.op("scalar", "mul",
                                 ctx.l1a_qtmp[:, o * 512:(o + 1) * 512],
                                 psl, scale,
                                 reads=prd, writes=[B["l1a_qtmp"]])
                        else:
                            p.op("scalar", "copy",
                                 ctx.l1a_qtmp[:, half * 1024 + o * 512:
                                              half * 1024 + (o + 1) * 512],
                                 psl, reads=prd, writes=[B["l1a_qtmp"]])
                    if mat == "q":
                        for jj in range(8):
                            sl = slice(jj * 128, (jj + 1) * 128)
                            p.op("tensor", "transpose", ctx.ps_tr[:, sl],
                                 ctx.l1a_qtmp[:, sl], ctx.ident[:],
                                 reads=[B["l1a_qtmp"], B["ident"]],
                                 writes=[B["ps_tr"]])
                        p.op("vector", "tensor_copy",
                             Q_v[:, half * 8:(half + 1) * 8,
                                 i * 128:(i + 1) * 128],
                             ctx.ps_tr[:, 0:1024].rearrange(
                                 "p (d n) -> p d n", d=8),
                             reads=[B["ps_tr"]], writes=[B["l1a_Q"]])
                    else:
                        dd = (ctx.l1a_k_d if mat == "k" else ctx.l1a_v_d)
                        bname = f"l1a{mat}d_{i}"
                        p.op("gpsimd", "dma_start",
                             out=dd[i][:, half * 1024:(half + 1) * 1024],
                             in_=ctx.l1a_qtmp[:, half * 1024:
                                              (half + 1) * 1024],
                             reads=[B["l1a_qtmp"]],
                             writes=[ctx.buf(bname)], dma_key="kvst")
            ctx.RB.transition([B["l1a_PT"]])
            ctx.RC.transition(l1attn_bufs_ph23(ctx))
            for mt in range(ntb):
                kpar = mt % 2
                kw, bkw = ctx.l1a_kw[kpar], B[f"l1a_kw{kpar}"]
                kfm, bkfm = ctx.l1a_kfm[kpar], B[f"l1a_kfm{kpar}"]
                p.op("sync", "dma_start", out=kw[:],
                     in_=ctx.l1a_k_d[mt], reads=[ctx.buf(f"l1akd_{mt}")],
                     writes=[bkw], dma_key=f"kld{kpar}")
                for dt in range(EC):
                    sl = slice(dt * 128, (dt + 1) * 128)
                    p.op("tensor", "transpose", ctx.ps_tr[:, sl],
                         kw[:, sl], ctx.ident[:],
                         reads=[bkw, B["ident"]], writes=[B["ps_tr"]])
                p.op("vector", "tensor_copy", kfm[:], ctx.ps_tr[:],
                     reads=[B["ps_tr"]], writes=[bkfm])
                for o in range(nend // 512):
                    mms = [(kfm[:, dt * 128:(dt + 1) * 128],
                            Q_v[:, dt, o * 512:(o + 1) * 512],
                            ctx.ps_y[:, o * 512:(o + 1) * 512],
                            [bkfm, B["l1a_Q"]]) for dt in range(EC)]
                    p.mm_group([B[f"ps_q{o}"]], mms)
                negmax, rsum, recip = (ctx.l1a_ast[:, 0:1],
                                       ctx.l1a_ast[:, 1:2],
                                       ctx.l1a_ast[:, 2:3])
                p.op("vector", "tensor_copy", ctx.l1a_s16[:, 0:nend],
                     ctx.ps_y[:, 0:nend],
                     reads=ctx.psq(0, nend), writes=[B["l1a_s16"]])
                p.op("vector", "reduce_max", negmax, ctx.l1a_s16[:, 0:nend],
                     axis=AX.X, negate=True, reads=[B["l1a_s16"]],
                     writes=[B["l1a_ast"]])
                p.op("scalar", "activation", PT_v[:, mt, 0:nend],
                     ctx.l1a_s16[:, 0:nend],
                     AF.Exp, bias=negmax, accum_out=rsum,
                     reads=[B["l1a_s16"], B["l1a_ast"]],
                     writes=[B["l1a_PT"], B["l1a_ast"]])
                p.op("vector", "reciprocal", recip, rsum,
                     reads=[B["l1a_ast"]], writes=[B["l1a_ast"]])
                p.op("vector", "tensor_scalar", out=PT_v[:, mt, 0:nend],
                     in0=PT_v[:, mt, 0:nend], scalar1=recip, scalar2=None,
                     op0=ALU.mult, reads=[B["l1a_PT"], B["l1a_ast"]],
                     writes=[B["l1a_PT"]])
            for dseg in range(4):
                p.op("sync", "dma_start",
                     out=vd_v[:, 0:ntb],
                     in_=ctx.l1a_v_d[0:ntb, :, dseg * 512:(dseg + 1) * 512]
                     .rearrange("m p c -> p m c"),
                     reads=[ctx.buf(f"l1avd_{i}") for i in range(ntb)],
                     writes=[B["l1a_vd"]], dma_key="vld")
                for nt in range(ntb):
                    q = nt % 4
                    mms = [(PT_v[:, mc, nt * 128:(nt + 1) * 128],
                            vd_v[:, mc],
                            ctx.ps_y[:, q * 512:(q + 1) * 512],
                            [B["l1a_PT"], B["l1a_vd"]]) for mc in range(ntb)]
                    p.mm_group([B[f"ps_q{q}"]], mms)
                    at_sl = ctx.l1a_at_d[nt][:, dseg * 512:(dseg + 1) * 512]
                    bat = ctx.buf(f"l1aat_{nt}")
                    p.op("vector", "tensor_copy", ctx.l1a_av[:],
                         ctx.ps_y[:, q * 512:(q + 1) * 512],
                         reads=[B[f"ps_q{q}"]], writes=[B["l1a_av"]])
                    if h == 0:
                        p.op("gpsimd", "dma_start", out=at_sl,
                             in_=ctx.l1a_av[:], reads=[B["l1a_av"]],
                             writes=[bat], dma_key="atst")
                    else:
                        p.op("gpsimd", "dma_start", out=at_sl,
                             in_=ctx.l1a_av[:], accum_op=ALU.add,
                             reads=[B["l1a_av"]],
                             writes=[bat], dma_key="atst")
        ctx.RA.transition([B["l1a_Q"]])
        ctx.RC.transition(l1attn_bufs_ph4(ctx))
        p.op("sync", "dma_start", out=ctx.fcb_f[:], in_=w["fcb"],
             writes=[B["fcb_f"]], dma_key="wbr")
        p.op("vector", "tensor_copy", ctx.fcb[:], ctx.fcb_f[:],
             reads=[B["fcb_f"]], writes=[B["fcb"]])
        p.op("sync", "dma_start", out=Q_v[:], in_=w["fcT"][0],
             reads=[w["fcT"][1]], writes=[B["l1a_Q"]], dma_key="whld0")
        for nt in range(ntb):
            t = tiles[nt]
            par = nt % 2
            xa, bxa = ctx.xt[par], ctx.B[f"xt{par}"]
            p.op("sync", "dma_start", out=xa[:], in_=ctx.l1a_at_d[nt],
                 reads=[ctx.buf(f"l1aat_{nt}")], writes=[bxa],
                 dma_key=f"atld{par}")
            p.op("vector", "tensor_scalar", out=ctx.l1a_abf[:], in0=xa[:],
                 scalar1=1.0 / 3.0, scalar2=None, op0=ALU.mult,
                 reads=[bxa], writes=[B["l1a_abf"]])
            for dt in range(EC):
                sl = slice(dt * 128, (dt + 1) * 128)
                p.op("tensor", "transpose", ctx.ps_tr[:, sl],
                     ctx.l1a_abf[:, sl], ctx.ident[:],
                     reads=[B["l1a_abf"], B["ident"]], writes=[B["ps_tr"]])
            p.op("vector", "tensor_copy", ctx.l1a_zf[:], ctx.ps_tr[:],
                 reads=[B["ps_tr"]], writes=[B["l1a_zf"]])
            for o in range(4):
                mms = [(ctx.l1a_zf[:, dt * 128:(dt + 1) * 128],
                        Q_v[:, dt, o * 512:(o + 1) * 512],
                        ctx.ps_y[:, o * 512:(o + 1) * 512],
                        [B["l1a_zf"], B["l1a_Q"]]) for dt in range(EC)]
                mms.append((ctx.ones_bf[:],
                            ctx.fcb[:, o * 512:(o + 1) * 512],
                            ctx.ps_y[:, o * 512:(o + 1) * 512],
                            [B["ones_bf"], B["fcb"]]))
                p.mm_group([B[f"ps_q{o}"]], mms)
            xb_, bxb = ctx.xt[1 - par], ctx.B[f"xt{1 - par}"]
            p.op("sync", "dma_start", out=xb_[:], in_=src_aps[t],
                 reads=[ctx.buf(f"{src_name}_t{t}")], writes=[bxb],
                 dma_key=f"xld{1 - par}")
            p.op("vector", "tensor_tensor", out=xb_[:], in0=ctx.ps_y[:],
                 in1=xb_[:], op=ALU.add, reads=ctx.psq(0, 2048) + [bxb],
                 writes=[bxb])
            p.op("gpsimd", "dma_start", out=dst_aps[t], in_=xb_[:],
                 reads=[bxb], writes=[ctx.buf(f"{dst_name}_t{t}")],
                 dma_key=f"xst{1 - par}")


def build_program(ntb, nb):
    ntiles = ntb * nb
    N = ntiles * 128
    nc = bass.Bass("TRN2", target_bir_lowering=False)
    p = Prog(nc)
    ctx = Ctx(p, nc)

    ein = lambda name, shape: nc.dram_tensor(
        name, shape, F32, kind="ExternalInput").ap()
    xin = ein("xin", [N, D])
    w0 = {"WqT": ein("W0qT", [D, D]), "WkT": ein("W0kT", [D, D]),
          "WvT": ein("W0vT", [D, D]), "WoT": ein("W0oT", [D, D])}
    a_t, b_t = ein("A_T", [D, 512]), ein("B_T", [512, D])
    wr0, br0 = ein("Wr0T", [D, 8]), ein("br0", [1, 8])
    r1 = ein("r1T", [D, 8])
    w1_, w2_, w3_ = (ein("w1", [8, D, 512]), ein("w2", [8, 512, D]),
                     ein("w3", [8, D, 512]))
    q1, k1, v1 = (ein("W1qT", [3, D, D]), ein("W1kT", [3, D, D]),
                  ein("W1vT", [3, D, D]))
    fcT, fcb = ein("fcT", [D, D]), ein("fcb", [1, D])
    xout = nc.dram_tensor("xout", [N, D], F32, kind="ExternalOutput").ap()
    xd = nc.dram_tensor("xd", [N, D], F32, kind="Internal").ap()

    xin_t = xin.rearrange("(t p) d -> t p d", p=128)
    xd_t = xd.rearrange("(t p) d -> t p d", p=128)
    xout_t = xout.rearrange("(t p) d -> t p d", p=128)
    xin_l = [xin_t[t] for t in range(ntiles)]
    xd_l = [xd_t[t] for t in range(ntiles)]
    xout_l = [xout_t[t] for t in range(ntiles)]
    b_tiles = [list(range(b * ntb, (b + 1) * ntb)) for b in range(nb)]

    alloc_l0ffn(ctx)
    alloc_l0attn(ctx, ntb)
    alloc_l1ffn(ctx, ntiles)
    alloc_l1attn(ctx, ntb)
    ctx.emit_consts()

    w1f = {"rwT": r1, "w1": [w1_[e] for e in range(8)],
           "w2": [w2_[e] for e in range(8)], "w3": [w3_[e] for e in range(8)]}
    w1a = {"WqT": [q1[h] for h in range(3)], "WkT": [k1[h] for h in range(3)],
           "WvT": [v1[h] for h in range(3)], "fcT": fcT, "fcb": fcb}

    wc = build_wcache(ctx, dict(
        l0_AT=a_t, l0_BT=b_t, l0_Wr=wr0,
        l0a_k=w0["WkT"], l0a_q=w0["WqT"], l0a_v=w0["WvT"], l0a_wo=w0["WoT"],
        l1_rw=r1, l1_w1=[w1_[e] for e in range(8)],
        l1_w2=[w2_[e] for e in range(8)], l1_w3=[w3_[e] for e in range(8)],
        l1a_q=[q1[h] for h in range(3)], l1a_k=[k1[h] for h in range(3)],
        l1a_v=[v1[h] for h in range(3)], l1a_fc=fcT))

    w0c = {"k": wc["l0a_k"], "q": wc["l0a_q"], "v": wc["l0a_v"],
           "wo": wc["l0a_wo"]}
    w1fc = {"rwT": wc["l1_rw"], "w1": wc["l1_w1"], "w2": wc["l1_w2"],
            "w3": wc["l1_w3"]}
    w1ac = {"q": wc["l1a_q"], "k": wc["l1a_k"], "v": wc["l1a_v"],
            "fcT": wc["l1a_fc"], "fcb": fcb}

    with nc.named_scope("l0f1"):
        ctx.RC.transition(l0ffn_bufs(ctx))
        load_l0ffn_weights(ctx, wc, br0)
        emit_l0ffn(ctx, ("xin", xin_l), ("xd", xd_l), ntiles)
    with nc.named_scope("l0a"):
        emit_l0attn(ctx, ("xd", xd_l), ("xd", xd_l), w0c, b_tiles)
    p.flush_bg()
    with nc.named_scope("l0f2"):
        ctx.RC.transition(l0ffn_bufs(ctx))
        load_l0ffn_weights(ctx, wc, br0)
        emit_l0ffn(ctx, ("xd", xd_l), ("xd", xd_l), ntiles)
    with nc.named_scope("l1f1"):
        emit_l1ffn(ctx, ("xd", xd_l), ("xd", xd_l), ("xd", xd_l), w1fc,
                   ntiles)
    with nc.named_scope("l1a"):
        emit_l1attn(ctx, ("xd", xd_l), ("xd", xd_l), w1ac, b_tiles)
    with nc.named_scope("l1f2"):
        emit_l1ffn(ctx, ("xd", xd_l), ("xd", xd_l), ("xout", xout_l), w1fc,
                   ntiles)
    p.wait_all("gpsimd",
               [ctx.B[f"xout_t{t}"].last_write for t in range(ntiles)])
    return nc


def build_wcache(ctx, srcs):
    nc = ctx.nc
    wc = {}

    eager = {"l0_AT", "l0_BT", "l0_Wr"}

    def seg(name, shape, jobs):
        wc[name] = mk_wseg(ctx, name, shape, jobs, eager=name in eager)

    seg("l0_AT", [128, 16, 512],
        [((lambda a, ec=ec: a[:, ec]),
          srcs["l0_AT"][ec * 128:(ec + 1) * 128, :]) for ec in range(16)])
    seg("l0_BT", [128, 4, 2048],
        [((lambda a, ec=ec, o=o: a[:, ec, o * 1024:(o + 1) * 1024]),
          srcs["l0_BT"][ec * 128:(ec + 1) * 128, o * 1024:(o + 1) * 1024])
         for ec in range(4) for o in range(2)])
    seg("l0_Wr", [128, 16, 8],
        [((lambda a, ec=ec: a[:, ec]),
          srcs["l0_Wr"][ec * 128:(ec + 1) * 128, :]) for ec in range(16)])
    for m in ["k", "q", "v"]:
        seg(f"l0a_{m}", [128, 2, 16, 1024],
            [((lambda a, hh=hh, ec=ec: a[:, hh, ec]),
              srcs[f"l0a_{m}"][ec * 128:(ec + 1) * 128,
                               hh * 1024:(hh + 1) * 1024])
             for hh in range(2) for ec in range(16)])
    seg("l0a_wo", [128, 16, 2048],
        [((lambda a, hc=hc, o=o: a[:, hc, o * 1024:(o + 1) * 1024]),
          srcs["l0a_wo"][hc * 128:(hc + 1) * 128, o * 1024:(o + 1) * 1024])
         for hc in range(16) for o in range(2)])
    seg("l1_rw", [128, 16, 8],
        [((lambda a, ec=ec: a[:, ec]),
          srcs["l1_rw"][ec * 128:(ec + 1) * 128, :]) for ec in range(16)])
    for m in ["w1", "w3"]:
        seg(f"l1_{m}", [128, 8, 16, 512],
            [((lambda a, e=e, ec=ec: a[:, e, ec]),
              srcs[f"l1_{m}"][e][ec * 128:(ec + 1) * 128, :])
             for e in range(8) for ec in range(16)])
    seg("l1_w2", [128, 8, 4, 2048],
        [((lambda a, e=e, f=f, o=o: a[:, e, f, o * 1024:(o + 1) * 1024]),
          srcs["l1_w2"][e][f * 128:(f + 1) * 128, o * 1024:(o + 1) * 1024])
         for e in range(8) for f in range(4) for o in range(2)])
    for m in ["q", "k", "v"]:
        seg(f"l1a_{m}", [128, 3, 2, 16, 1024],
            [((lambda a, h=h, hh=hh, ec=ec: a[:, h, hh, ec]),
              srcs[f"l1a_{m}"][h][ec * 128:(ec + 1) * 128,
                                  hh * 1024:(hh + 1) * 1024])
             for h in range(3) for hh in range(2) for ec in range(16)])
    seg("l1a_fc", [128, 16, 2048],
        [((lambda a, dt=dt, o=o: a[:, dt, o * 1024:(o + 1) * 1024]),
          srcs["l1a_fc"][dt * 128:(dt + 1) * 128, o * 1024:(o + 1) * 1024])
         for dt in range(16) for o in range(2)])
    return wc


def host_inputs(inputs):
    C = np.ascontiguousarray
    f = np.float32
    w = {
        "W0qT": C(inputs["L0_attn_Wq"].T.astype(f, copy=False)),
        "W0kT": C(inputs["L0_attn_Wk"].T.astype(f, copy=False)),
        "W0vT": C(inputs["L0_attn_Wv"].T.astype(f, copy=False)),
        "W0oT": C(inputs["L0_attn_Wo"].T.astype(f, copy=False)),
        "A_T": C(np.transpose(inputs["L0_ffn_A"], (2, 0, 1)).reshape(D, 512)),
        "B_T": C(np.transpose(inputs["L0_ffn_B"], (0, 2, 1)).reshape(512, D)),
        "Wr0T": C(inputs["L0_ffn_Wr"].T),
        "br0": C(inputs["L0_ffn_br"].reshape(1, 8)),
        "r1T": C(inputs["L1_ffn_router_w"].T),
        "w1": C(inputs["L1_ffn_w1"]),
        "w2": C(inputs["L1_ffn_w2"]),
        "w3": C(inputs["L1_ffn_w3"]),
        "W1qT": C(np.transpose(inputs["L1_attn_Wq"], (0, 2, 1))),
        "W1kT": C(np.transpose(inputs["L1_attn_Wk"], (0, 2, 1))),
        "W1vT": C(np.transpose(inputs["L1_attn_Wv"], (0, 2, 1))),
        "fcT": C(inputs["L1_attn_fc_w"].T),
        "fcb": C(inputs["L1_attn_fc_b"].reshape(1, D)),
    }
    return {k: np.asarray(v, dtype=np.float32) for k, v in w.items()}


def kernel(**inputs):
    x = np.asarray(inputs["L0_attn_resid_pre"], dtype=np.float32)
    w = host_inputs(inputs)
    ncares = 8
    nb = 2
    ntb = 16
    nc = build_program(ntb, nb)
    in_maps = []
    for c in range(ncares):
        m = dict(w)
        m["xin"] = np.ascontiguousarray(
            x[c * nb:(c + 1) * nb].reshape(nb * 2048, 2048))
        in_maps.append(m)
    res = run_bass_kernel_spmd(nc, in_maps, core_ids=list(range(ncares)))
    out = np.concatenate(
        [r["xout"].reshape(nb, 2048, 2048) for r in res.results], axis=0)
    return out


# version 14
# speedup vs baseline: 1.0009x; 1.0009x over previous
import numpy as np
import concourse.bass as bass
import concourse.mybir as mybir
from concourse.bass_utils import run_bass_kernel_spmd




class Buf:
    __slots__ = ("name", "last_write", "readers")

    def __init__(self, name=""):
        self.name = name
        self.last_write = None
        self.readers = []


class Prog:
    def __init__(self, nc):
        self.nc = nc
        self.sems = {}
        self.cnt = {}
        self.mark = {}
        self.bg = []

    def pump(self, k):
        for _ in range(min(k, len(self.bg))):
            self.bg.pop(0)[1]()

    def flush_bg(self):
        while self.bg:
            self.bg.pop(0)[1]()

    def flush_seg(self, buf):
        if not any(b is buf for b, _ in self.bg):
            return
        last = max(i for i, (b, _) in enumerate(self.bg) if b is buf)
        for _ in range(last + 1):
            self.bg.pop(0)[1]()

    def sem(self, key):
        if key not in self.sems:
            self.sems[key] = self.nc.alloc_semaphore(key)
            self.cnt[key] = 0
        return self.sems[key]

    def _collect(self, reads, writes):
        need = {}

        def add(tok):
            if tok is None:
                return
            k, v = tok
            if need.get(k, 0) < v:
                need[k] = v

        for b in reads:
            add(b.last_write)
        for b in writes:
            add(b.last_write)
            for t in b.readers:
                add(t)
        return need

    def _wait(self, engine, eng, need):
        m = self.mark.setdefault(engine, {})
        for k in sorted(need):
            v = need[k]
            if m.get(k, 0) < v:
                eng.wait_ge(self.sem(k), v)
                m[k] = v

    def _finish(self, inst, engine, dma, reads, writes, key=None):
        if key is None:
            key = engine + ("_dma" if dma else "_op")
        inc = 16 if dma else 1
        s = self.sem(key)
        self.cnt[key] += inc
        inst.then_inc(s, inc)
        tok = (key, self.cnt[key])
        for b in writes:
            b.last_write = tok
            b.readers = []
        for b in reads:
            b.readers.append(tok)
        return tok

    def op(self, engine, method, *args, reads=(), writes=(), dma_key=None, **kw):
        eng = getattr(self.nc, engine)
        self._wait(engine, eng, self._collect(reads, writes))
        inst = getattr(eng, method)(*args, **kw)
        dma = method.startswith("dma_start")
        return self._finish(inst, engine, dma, reads, writes,
                            key=dma_key if dma else None)

    def mm_group(self, psum_writes, mms, engine="tensor"):
        eng = getattr(self.nc, engine)
        need = self._collect([], psum_writes)
        n = len(mms)
        insts = []
        for i, (lhsT, rhs, out, rbufs) in enumerate(mms):
            nd = dict(need) if i == 0 else {}
            for b in rbufs:
                t = b.last_write
                if t is not None and nd.get(t[0], 0) < t[1]:
                    nd[t[0]] = t[1]
            self._wait(engine, eng, nd)
            insts.append(
                eng.matmul(out, lhsT, rhs, start=(i == 0), stop=(i == n - 1))
            )
        tok = self._finish(insts[-1], engine, False, [], psum_writes)
        for _, _, _, rbufs in mms:
            for b in rbufs:
                if tok not in b.readers:
                    b.readers.append(tok)
        return tok

    def barrier_token(self, engine):
        key = engine + "_op"
        return (key, self.cnt.get(key, 0))

    def wait_all(self, engine, toks):
        eng = getattr(self.nc, engine)
        need = {}
        for t in toks:
            if t and need.get(t[0], 0) < t[1]:
                need[t[0]] = t[1]
        self._wait(engine, eng, need)



F32 = mybir.dt.float32
BF16 = mybir.dt.bfloat16
AF = mybir.ActivationFunctionType
ALU = mybir.AluOpType
AX = mybir.AxisListType

D = 2048
EC = D // 128


class Region:
    def __init__(self, nc, name, nbytes):
        self.nc = nc
        self.name = name
        self.t = nc.alloc_sbuf_tensor(name, [128, nbytes], mybir.dt.uint8)
        self.addr = nc.lookup_mloc(self.t).addr
        self.nbytes = nbytes
        self.live = []

    def place(self, name, shape, dtype, rel):
        per_part = 1
        for s in shape[1:]:
            per_part *= s
        per_part *= mybir.dt.size(dtype)
        assert rel + per_part <= self.nbytes, \
            f"{name}: {rel}+{per_part} > {self.nbytes} in {self.name}"
        h = self.nc.virtual_sbuf_tensor(name, list(shape), dtype)
        self.nc._tensor(h.name, list(h.shape), h.dtype, type="SB")
        mloc = self.nc.lookup_mloc(h)
        mloc.allocated = True
        mloc.addr = self.addr + rel
        return h

    def transition(self, bufs_in):
        toks = []
        for b in self.live:
            if b.last_write is not None:
                toks.append(b.last_write)
            toks.extend(b.readers)
        for b in bufs_in:
            b.readers.extend(toks)
        self.live = list(bufs_in)


class Ctx:

    def __init__(self, p, nc):
        self.p = p
        self.nc = nc
        a = nc.alloc_sbuf_tensor
        self.B = {}
        self.ident = a("c_ident", [128, 128], BF16)
        self.ones_col = a("c_ones", [1, 128], F32)
        self.cpool = a("c_pool", [128, 1], F32)
        nc.const_aps.register(F32, 1e-5, self.cpool[:, 0:1])
        self.xt = [a(f"w_x{i}", [128, D], F32) for i in range(2)]
        self.xn = a("w_xn", [128, D], BF16)
        self.xf = [a(f"w_xf{i}", [128, D], BF16) for i in range(2)]
        self.st = [a(f"w_st{i}", [128, 32], F32) for i in range(2)]
        self.stage = [a(f"w_stg{i}", [128, 1024], F32) for i in range(2)]
        self.wbounce = [a(f"w_wb{i}", [128, 1024], BF16) for i in range(2)]
        self.ps_tr = nc.alloc_psum_tensor("ps_tr", [128, D], BF16)
        self.ps_y = nc.alloc_psum_tensor("ps_y", [128, D], F32)
        self.ps_h = nc.alloc_psum_tensor("ps_h", [128, 512], F32)
        self.ps_b = nc.alloc_psum_tensor("ps_b", [128, 512], BF16)
        for name in ["ident", "ones_col", "cpool", "xn",
                     "ps_tr", "ps_h", "ps_b"]:
            self.B[name] = Buf(name)
        for q in range(4):
            self.B[f"ps_q{q}"] = Buf(f"ps_q{q}")
        for i in range(2):
            for name in ["xt", "xf", "st", "stage", "wbounce"]:
                self.B[f"{name}{i}"] = Buf(f"{name}{i}")
        self.RA = Region(nc, "regA", 64 * 1024)
        self.RB = Region(nc, "regB", 64 * 1024)
        self.RC = Region(nc, "regC", 36 * 1024)

    def emit_consts(self):
        p = self.p
        B = self.B
        p.op("gpsimd", "memset", self.ident[:], 0.0, writes=[B["ident"]])
        p.op("gpsimd", "affine_select", out=self.ident[:], in_=self.ident[:],
             compare_op=ALU.not_equal, fill=1.0, base=0, pattern=[[-1, 128]],
             channel_multiplier=1, reads=[B["ident"]], writes=[B["ident"]])
        p.op("gpsimd", "memset", self.ones_col[:], 1.0, writes=[B["ones_col"]])
        p.op("gpsimd", "memset", self.cpool[:, 0:1], 1e-5, writes=[B["cpool"]])

    def buf(self, name):
        if name not in self.B:
            self.B[name] = Buf(name)
        return self.B[name]

    def filler(self, tag=None):
        self.p.op("tensor", "matmul", self.ps_y[0:1, 1792:1856],
                  self.ident[:, 0:1], self.ident[:, 0:64],
                  start=True, stop=True,
                  reads=[self.B["ident"]], writes=[self.B["ps_q3"]])

    def psq(self, c0, c1):
        return [self.B[f"ps_q{q}"] for q in range(4)
                if q * 512 < c1 and (q + 1) * 512 > c0]


def ln_to_fm(ctx, par, src_ap, src_buf, want_fm=True, dma_engine="sync"):
    p = ctx.p
    xt, xn, xf, st = ctx.xt[par], ctx.xn, ctx.xf[par], ctx.st[par]
    ps = ctx.ps_tr
    bxt, bxn, bxf, bst = (ctx.B[f"xt{par}"], ctx.B["xn"],
                          ctx.B[f"xf{par}"], ctx.B[f"st{par}"])
    bps = ctx.B["ps_tr"]

    p.op(dma_engine, "dma_start", out=xt[:], in_=src_ap,
         reads=[src_buf], writes=[bxt], dma_key=f"xld{par}")
    p.pump(3)
    bnst = st[:, 0:24].rearrange("p (c s) -> p c s", c=4)
    mean, var = st[:, 24:25], st[:, 25:26]
    negmean, rstd, tmp = st[:, 26:27], st[:, 27:28], st[:, 28:29]
    for c in range(4):
        p.op("vector", "bn_stats", bnst[:, c], xt[:, c * 512:(c + 1) * 512],
             reads=[bxt], writes=[bst])
    p.op("vector", "bn_aggr", st[:, 24:26], bnst[:], reads=[bst], writes=[bst])
    p.op("vector", "tensor_scalar", out=negmean, in0=mean, scalar1=-1.0,
         scalar2=None, op0=ALU.mult, reads=[bst], writes=[bst])
    p.op("scalar", "activation", tmp, var, AF.Ln, bias=1e-5, scale=1.0,
         reads=[bst, ctx.B["cpool"]], writes=[bst])
    p.op("scalar", "activation", rstd, tmp, AF.Exp, bias=0.0, scale=-0.5,
         reads=[bst], writes=[bst])
    p.op("vector", "tensor_scalar", out=xn[:], in0=xt[:], scalar1=negmean,
         scalar2=rstd, op0=ALU.add, op1=ALU.mult,
         reads=[bxt, bst], writes=[bxn])
    out = {"xt": bxt, "xn": bxn}
    if want_fm:
        for ec in range(EC):
            sl = slice(ec * 128, (ec + 1) * 128)
            p.op("tensor", "transpose", ps[:, sl], xn[:, sl], ctx.ident[:],
                 reads=[bxn, ctx.B["ident"]], writes=[bps])
        p.op("vector", "tensor_copy", xf[:], ps[:], reads=[bps], writes=[bxf])
        out["xf"] = bxf
    return out


def _stream_cast(ctx, dst_slices_and_aps, buf, scale=None):
    p = ctx.p
    for i, (dst_sl, src_ap) in enumerate(dst_slices_and_aps):
        par = i % 2
        bj = buf[i] if isinstance(buf, list) else buf
        stg, bstg = ctx.stage[par], ctx.B[f"stage{par}"]
        n = src_ap.shape[-1]
        p.op("sync", "dma_start", out=stg[:, 0:n], in_=src_ap,
             writes=[bstg], dma_key=f"wstg{par}")
        if scale is None:
            p.op("vector", "tensor_copy", dst_sl, stg[:, 0:n],
                 reads=[bstg], writes=[bj])
        else:
            p.op("vector", "tensor_scalar", out=dst_sl, in0=stg[:, 0:n],
                 scalar1=scale, scalar2=None, op0=ALU.mult,
                 reads=[bstg], writes=[bj])


def emit_wcache(ctx, jobs, seg_buf):
    p = ctx.p
    for i, (dst_sl, src_ap) in enumerate(jobs):
        par = i % 2
        stg, bstg = ctx.stage[par], ctx.B[f"stage{par}"]
        wb, bwb = ctx.wbounce[par], ctx.B[f"wbounce{par}"]
        n = src_ap.shape[-1]
        p.op("sync", "dma_start", out=stg[:, 0:n], in_=src_ap,
             writes=[bstg], dma_key=f"wstg{par}")
        p.op("scalar", "copy", wb[:, 0:n], stg[:, 0:n],
             reads=[bstg], writes=[bwb])
        p.op("gpsimd", "dma_start", out=dst_sl, in_=wb[:, 0:n],
             reads=[bwb], writes=[seg_buf], dma_key=f"wcst{par}")


def mk_wseg(ctx, name, shape, jobs, eager=True):
    nc = ctx.nc
    ap_ = nc.dram_tensor("wc_" + name, shape, BF16, kind="Internal").ap()
    b = Buf("wc_" + name)
    if eager:
        emit_wcache(ctx, [(dst(ap_), src) for dst, src in jobs], b)
    else:
        for dst, src in jobs:
            ctx.p.bg.append(
                (b, lambda d=dst, sr=src: emit_wcache(ctx, [(d(ap_), sr)],
                                                      b)))
    return (ap_, b)



def alloc_l0ffn(ctx):
    RC = ctx.RC
    ctx.l0_AT = RC.place("l0_AT", [128, 16 * 512], BF16, 0)
    ctx.l0_BT = RC.place("l0_BT", [128, 4 * 2048], BF16, 16384)
    ctx.l0_WrT = RC.place("l0_WrT", [128, 16 * 8], BF16, 32768)
    ctx.l0_brbc = RC.place("l0_brbc", [128, 8], F32, 33056)
    ctx.l0_lg = RC.place("l0_lg", [128, 24], F32, 33408)
    ctx.l0_mw = RC.place("l0_mw", [128, 8], F32, 33504)
    ctx.l0_hs = RC.place("l0_hs", [128, 512], BF16, 33568)
    ctx.l0_hf = RC.place("l0_hf", [128, 512], BF16, 34592)
    ctx.l0_brrow = ctx.nc.alloc_sbuf_tensor("l0_brrow", [1, 8], F32)
    for n in ["l0_AT", "l0_BT", "l0_WrT", "l0_brbc", "l0_lg", "l0_mw",
              "l0_hs", "l0_hf", "l0_brrow"]:
        ctx.B[n] = Buf(n)


def l0ffn_bufs(ctx):
    return [ctx.B[n] for n in ["l0_AT", "l0_BT", "l0_WrT", "l0_brbc",
                               "l0_lg", "l0_mw", "l0_hs", "l0_hf"]]


def load_l0ffn_weights(ctx, wc, ap_br):
    p = ctx.p
    B = ctx.B
    p.op("sync", "dma_start",
         out=ctx.l0_AT[:].rearrange("p (e r) -> p e r", e=16),
         in_=wc["l0_AT"][0], reads=[wc["l0_AT"][1]],
         writes=[B["l0_AT"]], dma_key="wld0")
    p.op("sync", "dma_start",
         out=ctx.l0_BT[:].rearrange("p (e r) -> p e r", e=4),
         in_=wc["l0_BT"][0], reads=[wc["l0_BT"][1]],
         writes=[B["l0_BT"]], dma_key="wld1")
    p.op("sync", "dma_start",
         out=ctx.l0_WrT[:].rearrange("p (e r) -> p e r", e=16),
         in_=wc["l0_Wr"][0], reads=[wc["l0_Wr"][1]],
         writes=[B["l0_WrT"]], dma_key="wld2")
    p.op("sync", "dma_start", out=ctx.l0_brrow[:], in_=ap_br,
         writes=[B["l0_brrow"]], dma_key="wbr")
    p.op("tensor", "matmul", ctx.ps_h[:, 0:8], ctx.ones_col[:],
         ctx.l0_brrow[:], start=True, stop=True,
         reads=[ctx.B["ones_col"], B["l0_brrow"]], writes=[B["ps_h"]])
    p.op("vector", "tensor_copy", ctx.l0_brbc[:], ctx.ps_h[:, 0:8],
         reads=[B["ps_h"]], writes=[B["l0_brbc"]])


def emit_l0ffn(ctx, src, dst, ntiles):
    p = ctx.p
    B = ctx.B
    src_name, src_aps = src
    dst_name, dst_aps = dst
    scale = 0.5 / 64.0
    lg, mw, hs, hf = ctx.l0_lg, ctx.l0_mw, ctx.l0_hs, ctx.l0_hf
    blg, bmw, bhs, bhf = B["l0_lg"], B["l0_mw"], B["l0_hs"], B["l0_hf"]

    for t in range(ntiles):
        par = t % 2
        r = ln_to_fm(ctx, par, src_aps[t], ctx.buf(f"{src_name}_t{t}"))
        bxt, bxf = r["xt"], r["xf"]
        xf = ctx.xf[par]
        ph, bph = ctx.ps_h, B["ps_h"]
        mms = [(xf[:, ec * 128:(ec + 1) * 128],
                ctx.l0_WrT[:, ec * 8:(ec + 1) * 8],
                ph[:, 0:8], [bxf, B["l0_WrT"]]) for ec in range(EC)]
        p.mm_group([bph], mms)
        p.op("vector", "tensor_tensor", out=lg[:, 0:8], in0=ph[:, 0:8],
             in1=ctx.l0_brbc[:, 0:8], op=ALU.add,
             reads=[bph, B["l0_brbc"]], writes=[blg])
        m1, negm1, esum, recip = (lg[:, 8:9], lg[:, 9:10], lg[:, 10:11],
                                  lg[:, 11:12])
        p.op("vector", "reduce_max", negm1, lg[:, 0:8], axis=AX.X, negate=True,
             reads=[blg], writes=[blg])
        p.op("vector", "tensor_scalar", out=m1, in0=negm1, scalar1=-1.0,
             scalar2=None, op0=ALU.mult, reads=[blg], writes=[blg])
        p.op("scalar", "activation", lg[:, 12:20], lg[:, 0:8], AF.Exp,
             bias=negm1, scale=1.0, accum_out=esum,
             reads=[blg], writes=[blg])
        p.op("vector", "reciprocal", recip, esum, reads=[blg], writes=[blg])
        p.op("vector", "tensor_scalar", out=mw[:], in0=lg[:, 0:8], scalar1=m1,
             scalar2=recip, op0=ALU.is_ge, op1=ALU.mult,
             reads=[blg], writes=[bmw])
        mms = [(xf[:, ec * 128:(ec + 1) * 128],
                ctx.l0_AT[:, ec * 512:(ec + 1) * 512],
                ph[:, 0:512], [bxf, B["l0_AT"]]) for ec in range(EC)]
        p.mm_group([bph], mms)
        for e in range(8):
            p.op("vector", "tensor_scalar", out=hs[:, e * 64:(e + 1) * 64],
                 in0=ph[:, e * 64:(e + 1) * 64], scalar1=mw[:, e:e + 1],
                 scalar2=None, op0=ALU.mult,
                 reads=[bph, bmw], writes=[bhs])
        for j in range(4):
            sl = slice(j * 128, (j + 1) * 128)
            p.op("tensor", "transpose", ctx.ps_b[:, sl], hs[:, sl],
                 ctx.ident[:], reads=[bhs, ctx.B["ident"]],
                 writes=[B["ps_b"]])
        p.op("vector", "tensor_copy", hf[:], ctx.ps_b[:],
             reads=[B["ps_b"]], writes=[bhf])
        py = ctx.ps_y
        for o in range(4):
            mms = [(hf[:, j * 128:(j + 1) * 128],
                    ctx.l0_BT[:, j * 2048 + o * 512: j * 2048 + (o + 1) * 512],
                    py[:, o * 512:(o + 1) * 512],
                    [bhf, B["l0_BT"]]) for j in range(4)]
            p.mm_group([B[f"ps_q{o}"]], mms)
        p.op("vector", "scalar_tensor_tensor", out=ctx.xt[par][:], in0=py[:],
             scalar=scale, in1=ctx.xt[par][:], op0=ALU.mult, op1=ALU.add,
             reads=ctx.psq(0, 2048) + [bxt], writes=[bxt])
        p.op("gpsimd", "dma_start", out=dst_aps[t], in_=ctx.xt[par][:],
             reads=[bxt], writes=[ctx.buf(f"{dst_name}_t{t}")],
             dma_key=f"xst{par}")



def _alibi_slopes16():
    start = 2.0 ** -0.5
    return [start ** (i + 1) for i in range(16)]


def alloc_l0attn(ctx, ntb):
    nc = ctx.nc
    RA, RB, RC = ctx.RA, ctx.RB, ctx.RC
    ctx.Afm = RA.place("l0a_Afm", [128, 16 * 2048], BF16, 0)
    ctx.Bv = RB.place("l0a_Bv", [128, 16 * 2048], BF16, 0)
    ctx.Wh = RC.place("l0a_Wh", [128, 16 * 1024], BF16, 0)
    ctx.s_sb = [RC.place(f"l0a_s{i}", [128, 2048], F32, i * 8192)
                for i in range(2)]
    ctx.PP = RC.place("l0a_PP", [128, 2048], BF16, 16384)
    ctx.PT = RC.place("l0a_PTs", [128, 2048], BF16, 20480)
    ctx.zfull = RC.place("l0a_z", [128, 2048], BF16, 24576)
    ctx.causal = RC.place("l0a_causal", [128, 128], F32, 28672)
    ctx.att_st = RC.place("l0a_ast", [128, 16], F32, 29184)
    ctx.iota_row = RC.place("l0a_iota", [128, D], mybir.dt.float16,
                            29760)
    ctx.zf_fm = RC.place("l0a_zf", [128, 2048], BF16, 0)
    ctx.ktmp = ctx.xn[:, 0:1024]
    ctx.qstg = ctx.xn[:, 1024:2048]
    ctx.xn_d = nc.dram_tensor("l0a_xn_d", [ntb, 128, 2048], BF16,
                              kind="Internal").ap()
    ctx.q_d = nc.dram_tensor("l0a_q_d", [ntb, 128, 2048], BF16,
                             kind="Internal").ap()
    ctx.z_d = nc.dram_tensor("l0a_z_d", [ntb, 128, 2048], BF16,
                             kind="Internal").ap()
    for n in ["Afm", "Bv", "Wh", "s_sb0", "s_sb1", "PP", "PT", "zfull",
              "zf_fm", "causal", "att_st", "iota_row"]:
        ctx.B[n] = Buf(n)
    ctx.B["ktmp"] = ctx.B["xn"]


def l0attn_bufs_phA(ctx):
    return [ctx.B[n] for n in ["Wh"]]


def l0attn_bufs_phB(ctx):
    return [ctx.B[n] for n in ["s_sb0", "s_sb1", "PP", "PT", "zfull",
                               "causal", "att_st", "iota_row"]]


def l0attn_bufs_phC(ctx):
    return [ctx.B[n] for n in ["zfull", "zf_fm"]]


def emit_l0attn_consts(ctx):
    p = ctx.p
    p.op("gpsimd", "memset", ctx.causal[:], 0.0, writes=[ctx.B["causal"]])
    p.op("gpsimd", "affine_select", out=ctx.causal[:], in_=ctx.causal[:],
         compare_op=ALU.is_ge, fill=-1e9, base=0, pattern=[[-1, 128]],
         channel_multiplier=1, reads=[ctx.B["causal"]],
         writes=[ctx.B["causal"]])


def emit_l0attn(ctx, src, dst, w, b_tiles_list):
    p = ctx.p
    B = ctx.B
    src_name, src_aps = src
    dst_name, dst_aps = dst
    slopes = _alibi_slopes16()
    Afm_v = ctx.Afm[:].rearrange("p (h k) -> p h k", h=16)
    Bv_v = ctx.Bv[:].rearrange("p (t c) -> p t c", t=16)

    for b, tiles in enumerate(b_tiles_list):
        ntb = len(tiles)
        ctx.RA.transition([B["Afm"]])
        ctx.RB.transition([B["Bv"]])
        ctx.RC.transition(l0attn_bufs_phA(ctx))
        for mat in ["k", "q", "v"]:
            qscale = (1.0 / (128.0 ** 0.5)) if mat == "q" else None
            for half in range(2):
                p.flush_seg(w[mat][1])
                p.op("sync", "dma_start",
                     out=ctx.Wh[:].rearrange("p (e c) -> p e c", e=16),
                     in_=w[mat][0][:, half], reads=[w[mat][1]],
                     writes=[B["Wh"]], dma_key="whld")
                for i in range(ntb):
                    t = tiles[i]
                    par = i % 2
                    xfp, bxf = ctx.xf[par], ctx.B[f"xf{par}"]
                    if mat == "k" and half == 0:
                        r = ln_to_fm(ctx, par, src_aps[t],
                                     ctx.buf(f"{src_name}_t{t}"))
                        p.op("gpsimd", "dma_start", out=ctx.xn_d[i],
                             in_=xfp[:], reads=[r["xf"]],
                             writes=[ctx.buf(f"xnd_{i}")],
                             dma_key=f"xnd{par}")
                    else:
                        p.op("sync", "dma_start", out=xfp[:],
                             in_=ctx.xn_d[i],
                             reads=[ctx.buf(f"xnd_{i}")], writes=[bxf],
                             dma_key=f"xld{par}")
                    p.pump(4)
                    for o in range(2):
                        q = (2 * i + o) % 4
                        mms = [(xfp[:, ec * 128:(ec + 1) * 128],
                                ctx.Wh[:, ec * 1024 + o * 512:
                                       ec * 1024 + (o + 1) * 512],
                                ctx.ps_y[:, q * 512:(q + 1) * 512],
                                [bxf, B["Wh"]]) for ec in range(EC)]
                        p.mm_group([B[f"ps_q{q}"]], mms)
                        psl = ctx.ps_y[:, q * 512:(q + 1) * 512]
                        prd = [B[f"ps_q{q}"]]
                        if mat == "v":
                            p.op("scalar", "copy",
                                 ctx.Bv[:, i * 2048 + half * 1024 + o * 512:
                                        i * 2048 + half * 1024 +
                                        (o + 1) * 512],
                                 psl, reads=prd, writes=[B["Bv"]])
                        elif qscale is None:
                            p.op("scalar", "copy",
                                 ctx.ktmp[:, o * 512:(o + 1) * 512], psl,
                                 reads=prd, writes=[B["ktmp"]])
                        else:
                            p.op("scalar", "mul",
                                 ctx.ktmp[:, o * 512:(o + 1) * 512], psl,
                                 qscale, reads=prd, writes=[B["ktmp"]])
                    if mat == "v":
                        continue
                    for j in range(8):
                        sl = slice(j * 128, (j + 1) * 128)
                        p.op("tensor", "transpose", ctx.ps_tr[:, sl],
                             ctx.ktmp[:, sl], ctx.ident[:],
                             reads=[B["ktmp"], B["ident"]],
                             writes=[B["ps_tr"]])
                    if mat == "k":
                        p.op("vector", "tensor_copy",
                             Afm_v[:, half * 8:(half + 1) * 8,
                                   i * 128:(i + 1) * 128],
                             ctx.ps_tr[:, 0:1024].rearrange(
                                 "p (h k) -> p h k", h=8),
                             reads=[B["ps_tr"]], writes=[B["Afm"]])
                    else:
                        p.op("vector", "tensor_copy", ctx.qstg,
                             ctx.ps_tr[:, 0:1024],
                             reads=[B["ps_tr"]], writes=[B["xn"]])
                        p.op("gpsimd", "dma_start",
                             out=ctx.q_d[i][:, half * 1024:(half + 1) * 1024],
                             in_=ctx.qstg,
                             reads=[B["xn"]], writes=[ctx.buf(f"qd_d{i}")],
                             dma_key="qst")
        ctx.RC.transition(l0attn_bufs_phB(ctx))
        p.op("gpsimd", "iota", ctx.iota_row[:], [[1, D]], base=0,
             channel_multiplier=0, allow_small_or_imprecise_dtypes=True,
             writes=[B["iota_row"]])
        emit_l0attn_consts(ctx)
        iters = [(qtl, h) for qtl in range(ntb) for h in range(16)]
        n_it = len(iters)


        def qd_slot(qtl):
            return ctx.xf[qtl % 2], ctx.B[f"xf{qtl % 2}"]

        def s_qload(qtl):
            qd, bqd = qd_slot(qtl)
            p.op("sync", "dma_start", out=qd[:], in_=ctx.q_d[qtl],
                 reads=[ctx.buf(f"qd_d{qtl}")], writes=[bqd],
                 dma_key=f"qld{qtl % 2}")

        def chunks_of(i):
            qtl, h = iters[i]
            kend = (qtl + 1) * 128
            out = []
            for ci, c0 in enumerate(range(0, kend, 1024)):
                c1 = min(c0 + 1024, kend)
                pairbase = (ci if kend > 1024 else i % 2) * 1024
                out.append((c0, c1, pairbase))
            return out

        def s1_scores(i):
            qtl, h = iters[i]
            qd, bqd = qd_slot(qtl)
            p.pump(1)
            if h == 0:
                s_qload(qtl)
            for c0, c1, pb in chunks_of(i):
                for s0 in range(c0, c1, 512):
                    s1 = min(s0 + 512, c1)
                    pcol = pb + (s0 - c0)
                    p.op("tensor", "matmul",
                         ctx.ps_y[:, pcol:pcol + (s1 - s0)],
                         qd[:, h * 128:(h + 1) * 128],
                         ctx.Afm[:, h * 2048 + s0:h * 2048 + s1],
                         start=True, stop=True,
                         reads=[bqd, B["Afm"]],
                         writes=ctx.psq(pcol, pcol + (s1 - s0)))

        def s2a_evac(i):
            qtl, h = iters[i]
            kend = (qtl + 1) * 128
            qb = qtl * 128
            ss, bss = ctx.s_sb[i % 2], B[f"s_sb{i % 2}"]
            for c0, c1, pb in chunks_of(i):
                if qb >= c1:
                    p.op("scalar", "copy", ss[:, c0:c1],
                         ctx.ps_y[:, pb:pb + (c1 - c0)],
                         reads=ctx.psq(pb, pb + (c1 - c0)), writes=[bss])
                    continue
                if qb > c0:
                    p.op("scalar", "copy", ss[:, c0:qb],
                         ctx.ps_y[:, pb:pb + (qb - c0)],
                         reads=ctx.psq(pb, pb + (qb - c0)), writes=[bss])
                p.op("vector", "scalar_tensor_tensor", out=ss[:, qb:kend],
                     in0=ctx.ps_y[:, pb + (qb - c0):pb + (c1 - c0)],
                     scalar=1.0, in1=ctx.causal[:],
                     op0=ALU.mult, op1=ALU.add,
                     reads=ctx.psq(pb + (qb - c0), pb + (c1 - c0)) +
                     [B["causal"]], writes=[bss])

        def s2b_stats(i):
            qtl, h = iters[i]
            kend = (qtl + 1) * 128
            ss, bss = ctx.s_sb[i % 2], B[f"s_sb{i % 2}"]
            ast = ctx.att_st[:, 8 * (i % 2):8 * (i % 2) + 8]
            p.op("vector", "scalar_tensor_tensor", out=ss[:, 0:kend],
                 in0=ctx.iota_row[:, 0:kend], scalar=slopes[h],
                 in1=ss[:, 0:kend], op0=ALU.mult, op1=ALU.add,
                 reads=[B["iota_row"], bss], writes=[bss])
            p.op("vector", "reduce_max", ast[:, 0:1], ss[:, 0:kend],
                 axis=AX.X, negate=True,
                 reads=[bss], writes=[B["att_st"]])

        def s3_exp(i):
            qtl, h = iters[i]
            kend = (qtl + 1) * 128
            ss, bss = ctx.s_sb[i % 2], B[f"s_sb{i % 2}"]
            ast = ctx.att_st[:, 8 * (i % 2):8 * (i % 2) + 8]
            p.op("scalar", "activation", ctx.PP[:, 0:kend], ss[:, 0:kend],
                 AF.Exp, bias=ast[:, 0:1], accum_out=ast[:, 1:2],
                 reads=[bss, B["att_st"]],
                 writes=[B["PP"], B["att_st"]])

        def s4567_tail(i):
            qtl, h = iters[i]
            kend = (qtl + 1) * 128
            nkc = qtl + 1
            ast = ctx.att_st[:, 8 * (i % 2):8 * (i % 2) + 8]
            for kc in range(nkc):
                sl = slice(kc * 128, (kc + 1) * 128)
                p.op("tensor", "transpose", ctx.ps_tr[:, sl],
                     ctx.PP[:, sl], ctx.ident[:],
                     reads=[B["PP"], B["ident"]], writes=[B["ps_tr"]])
            p.op("vector", "reciprocal", ast[:, 2:3], ast[:, 1:2],
                 reads=[B["att_st"]], writes=[B["att_st"]])
            p.op("scalar", "copy", ctx.PT[:, 0:kend],
                 ctx.ps_tr[:, 0:kend],
                 reads=[B["ps_tr"]], writes=[B["PT"]])
            mms = [(ctx.PT[:, kc * 128:(kc + 1) * 128],
                    Bv_v[:, kc, h * 128:(h + 1) * 128],
                    ctx.ps_h[:, 0:128], [B["PT"], B["Bv"]])
                   for kc in range(nkc)]
            p.mm_group([B["ps_h"]], mms)
            p.op("vector", "tensor_scalar",
                 out=ctx.zfull[:, h * 128:(h + 1) * 128],
                 in0=ctx.ps_h[:, 0:128], scalar1=ast[:, 2:3], scalar2=None,
                 op0=ALU.mult, reads=[B["ps_h"], B["att_st"]],
                 writes=[B["zfull"]])
            if h == 15:
                p.op("gpsimd", "dma_start", out=ctx.z_d[qtl],
                     in_=ctx.zfull[:],
                     reads=[B["zfull"]], writes=[ctx.buf(f"zd_{qtl}")],
                     dma_key=f"zst{qtl % 2}")

        s1_scores(0)
        s2a_evac(0)
        s2b_stats(0)
        s1_scores(1)
        for i in range(n_it):
            s3_exp(i)
            if i + 1 < n_it:
                s2a_evac(i + 1)
                s2b_stats(i + 1)
            if i + 2 < n_it:
                s1_scores(i + 2)
            s4567_tail(i)
        p.flush_bg()
        ctx.RC.transition(l0attn_bufs_phC(ctx))
        p.op("sync", "dma_start",
             out=Afm_v[:], in_=w["wo"][0], reads=[w["wo"][1]],
             writes=[B["Afm"]], dma_key="whld")
        for qtl in range(ntb):
            t = tiles[qtl]
            par = qtl % 2
            p.op("sync", "dma_start", out=ctx.zfull[:], in_=ctx.z_d[qtl],
                 reads=[ctx.buf(f"zd_{qtl}")], writes=[B["zfull"]],
                 dma_key="zld")
            for hc in range(EC):
                sl = slice(hc * 128, (hc + 1) * 128)
                p.op("tensor", "transpose", ctx.ps_tr[:, sl],
                     ctx.zfull[:, sl], ctx.ident[:],
                     reads=[B["zfull"], B["ident"]], writes=[B["ps_tr"]])
            p.op("vector", "tensor_copy", ctx.zf_fm[:], ctx.ps_tr[:],
                 reads=[B["ps_tr"]], writes=[B["zf_fm"]])
            for o in range(4):
                mms = [(ctx.zf_fm[:, hc * 128:(hc + 1) * 128],
                        Afm_v[:, hc, o * 512:(o + 1) * 512],
                        ctx.ps_y[:, o * 512:(o + 1) * 512],
                        [B["zf_fm"], B["Afm"]]) for hc in range(EC)]
                p.mm_group([B[f"ps_q{o}"]], mms)
            xt, bxt = ctx.xt[par], ctx.B[f"xt{par}"]
            p.op("sync", "dma_start", out=xt[:], in_=src_aps[t],
                 reads=[ctx.buf(f"{src_name}_t{t}")], writes=[bxt],
                 dma_key=f"xld{par}")
            p.op("vector", "tensor_tensor", out=xt[:], in0=ctx.ps_y[:],
                 in1=xt[:], op=ALU.add, reads=ctx.psq(0, 2048) + [bxt],
                 writes=[bxt])
            p.op("gpsimd", "dma_start", out=dst_aps[t], in_=xt[:],
                 reads=[bxt], writes=[ctx.buf(f"{dst_name}_t{t}")],
                 dma_key=f"xst{par}")



def alloc_l1ffn(ctx, ntiles):
    nc = ctx.nc
    RA, RB, RC = ctx.RA, ctx.RB, ctx.RC
    ctx.w13 = RA.place("l1f_w13", [128, 4 * 16 * 512], BF16, 0)
    ctx.w2p = RB.place("l1f_w2", [128, 2 * 4 * 2048], BF16, 0)
    ctx.l1_rwT = RC.place("l1f_rwT", [128, 16 * 8], BF16, 0)
    ctx.l1_lg = RC.place("l1f_lg", [128, 48], F32, 256)
    ctx.l1_sl = [RC.place(f"l1f_sl{i}", [128, 512], BF16, 512 + i * 1024)
                 for i in range(2)]
    ctx.l1_h = [RC.place(f"l1f_h{i}", [128, 512], BF16, 2560 + i * 1024)
                for i in range(2)]
    ctx.l1_hf = [RC.place(f"l1f_hf{i}", [128, 2 * 512], BF16,
                          4608 + i * 2048) for i in range(2)]
    ctx.l1_mw = RC.place("l1f_mw", [128, 8 * ntiles], F32, 8704)
    ctx.l1_xn_d = nc.dram_tensor("l1f_xn_d", [ntiles, 128, 2048], BF16,
                                 kind="Internal").ap()
    for n in ["l1_rwT", "l1_lg", "l1_mw", "l1_sl0", "l1_sl1", "l1_h0",
              "l1_h1", "l1_hf0", "l1_hf1"]:
        ctx.B[n] = Buf(n)
    for i in range(4):
        ctx.B[f"w13_{i}"] = Buf(f"w13_{i}")
    for i in range(2):
        ctx.B[f"w2p_{i}"] = Buf(f"w2p_{i}")


def l1ffn_bufs_p0(ctx):
    return [ctx.B[n] for n in ["l1_rwT", "l1_lg", "l1_mw"]]


def l1ffn_bufs_px(ctx):
    return [ctx.B[n] for n in ["l1_rwT", "l1_lg", "l1_mw", "l1_sl0",
                               "l1_sl1", "l1_h0", "l1_h1", "l1_hf0",
                               "l1_hf1"]]


def w13_bufs(ctx):
    return [ctx.B[f"w13_{i}"] for i in range(4)]


def w2p_bufs(ctx):
    return [ctx.B[f"w2p_{i}"] for i in range(2)]


def emit_l1ffn(ctx, src, work, dst, w, ntiles):
    p = ctx.p
    B = ctx.B
    src_name, src_aps = src
    work_name, work_aps = work
    dst_name, dst_aps = dst
    lg = ctx.l1_lg
    l, mask1, l2 = lg[:, 0:8], lg[:, 8:16], lg[:, 16:24]
    mw1, mw2 = lg[:, 24:32], lg[:, 32:40]
    negm1, m1, negm2, m2 = (lg[:, 40:41], lg[:, 41:42], lg[:, 42:43],
                            lg[:, 43:44])
    dd, e2, onep, w1s, w2s = (lg[:, 44:45], lg[:, 45:46], lg[:, 46:47],
                              lg[:, 47:48], lg[:, 43:44])

    def router_tile(t, xf, bxf):
        mms = [(xf[:, ec * 128:(ec + 1) * 128],
                ctx.l1_rwT[:, ec * 8:(ec + 1) * 8],
                ctx.ps_h[:, 0:8], [bxf, B["l1_rwT"]]) for ec in range(EC)]
        p.mm_group([B["ps_h"]], mms)
        blg = B["l1_lg"]
        p.op("vector", "tensor_copy", l, ctx.ps_h[:, 0:8],
             reads=[B["ps_h"]], writes=[blg])
        p.op("vector", "reduce_max", negm1, l, axis=AX.X, negate=True,
             reads=[blg], writes=[blg])
        p.op("vector", "tensor_scalar", out=m1, in0=negm1, scalar1=-1.0,
             scalar2=None, op0=ALU.mult, reads=[blg], writes=[blg])
        p.op("vector", "tensor_scalar", out=mask1, in0=l, scalar1=m1,
             scalar2=None, op0=ALU.is_ge, reads=[blg], writes=[blg])
        p.op("vector", "scalar_tensor_tensor", out=l2, in0=mask1,
             scalar=-1e9, in1=l, op0=ALU.mult, op1=ALU.add,
             reads=[blg], writes=[blg])
        p.op("vector", "reduce_max", negm2, l2, axis=AX.X, negate=True,
             reads=[blg], writes=[blg])
        p.op("vector", "tensor_tensor", out=dd, in0=negm2, in1=negm1,
             op=ALU.subtract, reads=[blg], writes=[blg])
        p.op("scalar", "activation", e2, dd, AF.Exp, bias=0.0, scale=-1.0,
             reads=[blg], writes=[blg])
        p.op("vector", "tensor_scalar", out=onep, in0=e2, scalar1=1.0,
             scalar2=None, op0=ALU.add, reads=[blg], writes=[blg])
        p.op("vector", "reciprocal", w1s, onep, reads=[blg], writes=[blg])
        p.op("vector", "tensor_scalar", out=mw1, in0=l, scalar1=m1,
             scalar2=w1s, op0=ALU.is_ge, op1=ALU.mult,
             reads=[blg], writes=[blg])
        p.op("vector", "tensor_scalar", out=m2, in0=negm2, scalar1=-1.0,
             scalar2=None, op0=ALU.mult, reads=[blg], writes=[blg])
        p.op("vector", "tensor_scalar", out=mw2, in0=l2, scalar1=m2,
             scalar2=None, op0=ALU.is_ge, reads=[blg], writes=[blg])
        p.op("vector", "tensor_tensor", out=m2, in0=e2, in1=w1s,
             op=ALU.mult, reads=[blg], writes=[blg])
        p.op("vector", "tensor_scalar", out=mw2, in0=mw2, scalar1=m2,
             scalar2=None, op0=ALU.mult, reads=[blg], writes=[blg])
        p.op("vector", "tensor_tensor", out=ctx.l1_mw[:, t * 8:(t + 1) * 8],
             in0=mw1, in1=mw2, op=ALU.add,
             reads=[blg], writes=[B["l1_mw"]])

    ctx.RA.transition(w13_bufs(ctx))
    ctx.RB.transition(w2p_bufs(ctx))
    ctx.RC.transition(l1ffn_bufs_px(ctx))
    rwT_v = ctx.l1_rwT[:].rearrange("p (e r) -> p e r", e=16)
    p.op("sync", "dma_start", out=rwT_v[:], in_=w["rwT"][0],
         reads=[w["rwT"][1]], writes=[B["l1_rwT"]], dma_key="wldr")
    w13_v = ctx.w13[:].rearrange("p (s e c) -> p s e c", s=4, e=16)
    w2_v = ctx.w2p[:].rearrange("p (s f c) -> p s f c", s=2, f=4)
    for pr in range(4):
        e0, e1 = 2 * pr, 2 * pr + 1
        for si, (mat, e) in enumerate([("w1", e0), ("w3", e0),
                                       ("w1", e1), ("w3", e1)]):
            p.op("sync", "dma_start", out=w13_v[:, si],
                 in_=w[mat][0][:, e], reads=[w[mat][1]],
                 writes=[B[f"w13_{si}"]], dma_key=f"wld13_{si}")
        for si, e in enumerate([e0, e1]):
            p.op("sync", "dma_start", out=w2_v[:, si],
                 in_=w["w2"][0][:, e], reads=[w["w2"][1]],
                 writes=[B[f"w2p_{si}"]], dma_key=f"wldw2_{si}")
        first = pr == 0
        last = pr == 3
        rd = src_aps if first else work_aps
        rdname = src_name if first else work_name
        wr = dst_aps if last else work_aps
        wrname = dst_name if last else work_name
        for t in range(ntiles):
            par = t % 2
            xf, bxf = ctx.xf[par], ctx.B[f"xf{par}"]
            if first:
                r = ln_to_fm(ctx, par, src_aps[t],
                             ctx.buf(f"{src_name}_t{t}"))
                p.op("gpsimd", "dma_start", out=ctx.l1_xn_d[t], in_=xf[:],
                     reads=[r["xf"]], writes=[ctx.buf(f"l1xnd_{t}")],
                     dma_key=f"xnd{par}")
                router_tile(t, xf, bxf)
            else:
                p.op("sync", "dma_start", out=xf[:], in_=ctx.l1_xn_d[t],
                     reads=[ctx.buf(f"l1xnd_{t}")], writes=[bxf],
                     dma_key=f"xld{par}")
            for s, e in enumerate([e0, e1]):
                x1ps = ctx.ps_h[:, 0:512] if s == 0 else ctx.ps_y[:, 512:1024]
                bx1 = B["ps_h"] if s == 0 else B["ps_q1"]
                x3ps = ctx.ps_y[:, 0:512] if s == 0 else ctx.ps_y[:,
                                                                 1024:1536]
                bx3 = B["ps_q0"] if s == 0 else B["ps_q2"]
                mms = [(xf[:, ec * 128:(ec + 1) * 128],
                        w13_v[:, 2 * s, ec], x1ps,
                        [bxf, B[f"w13_{2 * s}"]]) for ec in range(EC)]
                p.mm_group([bx1], mms)
                mms = [(xf[:, ec * 128:(ec + 1) * 128],
                        w13_v[:, 2 * s + 1, ec], x3ps,
                        [bxf, B[f"w13_{2 * s + 1}"]]) for ec in range(EC)]
                p.mm_group([bx3], mms)
                slt, bslt = ctx.l1_sl[par], B[f"l1_sl{par}"]
                ht, bht = ctx.l1_h[par], B[f"l1_h{par}"]
                p.op("scalar", "activation", slt[:], x1ps,
                     AF.Silu, reads=[bx1], writes=[bslt])
                p.op("vector", "scalar_tensor_tensor", out=ht[:],
                     in0=x3ps,
                     scalar=ctx.l1_mw[:, t * 8 + (e0 + s):t * 8 + (e0 + s) + 1],
                     in1=slt[:], op0=ALU.mult, op1=ALU.mult,
                     reads=[bx3, B["l1_mw"], bslt],
                     writes=[bht])
                btr = B["ps_b"] if s == 0 else B["ps_tr"]
                for j in range(4):
                    sl = slice(j * 128, (j + 1) * 128)
                    p.op("tensor", "transpose",
                         ctx.ps_b[:, sl] if s == 0 else ctx.ps_tr[:, sl],
                         ht[:, sl], ctx.ident[:],
                         reads=[bht, ctx.B["ident"]],
                         writes=[btr])
                p.op("vector", "tensor_copy",
                     ctx.l1_hf[par][:, s * 512:(s + 1) * 512],
                     ctx.ps_b[:, 0:512] if s == 0 else ctx.ps_tr[:, 0:512],
                     reads=[btr], writes=[B[f"l1_hf{par}"]])
            for o in range(4):
                mms = []
                for s in range(2):
                    for f in range(4):
                        mms.append((
                            ctx.l1_hf[par][:, s * 512 + f * 128:
                                           s * 512 + (f + 1) * 128],
                            w2_v[:, s, f, o * 512:(o + 1) * 512],
                            ctx.ps_y[:, o * 512:(o + 1) * 512],
                            [B[f"l1_hf{par}"], B[f"w2p_{s}"]]))
                p.mm_group([B[f"ps_q{o}"]], mms)
            xt, bxt = ctx.xt[par], ctx.B[f"xt{par}"]
            if not first:
                p.op("sync", "dma_start", out=xt[:], in_=rd[t],
                     reads=[ctx.buf(f"{rdname}_t{t}")], writes=[bxt],
                     dma_key=f"rmw{par}")
            p.op("vector", "scalar_tensor_tensor", out=xt[:],
                 in0=ctx.ps_y[:], scalar=0.5, in1=xt[:],
                 op0=ALU.mult, op1=ALU.add,
                 reads=ctx.psq(0, 2048) + [bxt], writes=[bxt])
            p.op("gpsimd", "dma_start", out=wr[t], in_=xt[:],
                 reads=[bxt], writes=[ctx.buf(f"{wrname}_t{t}")],
                 dma_key=f"xst{par}")



def alloc_l1attn(ctx, ntb):
    nc = ctx.nc
    RA, RB, RC = ctx.RA, ctx.RB, ctx.RC
    ctx.l1a_Q = RA.place("l1a_Q", [128, 16 * 2048], BF16, 0)
    ctx.l1a_PT = RB.place("l1a_PT", [128, 16 * 2048], BF16, 0)
    ctx.l1a_WhB = [RB.place(f"l1a_Wh{i}", [128, 16 * 1024], BF16, i * 32768)
                   for i in range(2)]
    ctx.l1a_qtmp = RC.place("l1a_qtmp", [128, 2048], BF16, 32768)
    ctx.l1a_kfm = [RC.place("l1a_kfm0", [128, 2048], BF16, 0)] * 2
    ctx.l1a_s16 = RC.place("l1a_s16", [128, 2048], mybir.dt.float16, 4096)
    ctx.l1a_kw = [RC.place(f"l1a_kw{i}", [128, 2048], BF16, 8192 + i * 4096)
                  for i in range(2)]
    ctx.l1a_vd = RC.place("l1a_vd", [128, 16 * 512], BF16, 16384)
    ctx.l1a_av = RC.place("l1a_av", [128, 512], F32, 32768)
    ctx.l1a_ast = RC.place("l1a_ast", [128, 8], F32, 34816)
    ctx.l1a_abf = RC.place("l1a_abf", [128, 2048], BF16, 0)
    ctx.l1a_zf = RC.place("l1a_zf", [128, 2048], BF16, 4096)
    ctx.fcb_f = RC.place("l1a_fcb_f", [1, 2048], F32, 8192)
    ctx.fcb = RC.place("l1a_fcb", [1, 2048], BF16, 16384)
    ctx.ones_bf = nc.alloc_sbuf_tensor("c_ones_bf", [1, 128], BF16)
    ctx.l1a_xn_d = nc.dram_tensor("l1a_xn_d", [ntb, 128, 2048], BF16,
                                  kind="Internal").ap()
    ctx.l1a_k_d = nc.dram_tensor("l1a_k_d", [ntb, 128, 2048], BF16,
                                 kind="Internal").ap()
    ctx.l1a_v_d = nc.dram_tensor("l1a_v_d", [ntb, 128, 2048], BF16,
                                 kind="Internal").ap()
    ctx.l1a_at_d = nc.dram_tensor("l1a_at_d", [ntb, 128, 2048], F32,
                                  kind="Internal").ap()
    for n in (["l1a_Q", "l1a_PT", "l1a_Wh0", "l1a_Wh1", "l1a_qtmp",
               "l1a_kfm0", "l1a_s16", "l1a_kw0", "l1a_kw1", "l1a_vd",
               "l1a_ast", "l1a_av", "l1a_abf", "l1a_zf", "ones_bf",
               "fcb_f", "fcb"]):
        ctx.B[n] = Buf(n)
    ctx.B["l1a_kfm1"] = ctx.B["l1a_kfm0"]


def l1attn_bufs_ph1(ctx):
    return [ctx.B[n] for n in ["l1a_qtmp"]]


def l1attn_bufs_ph23(ctx):
    return [ctx.B[n] for n in ["l1a_kfm0", "l1a_s16", "l1a_kw0", "l1a_kw1",
                               "l1a_vd", "l1a_ast", "l1a_av", "l1a_qtmp"]]


def l1attn_bufs_ph4(ctx):
    return [ctx.B[n] for n in ["l1a_abf", "l1a_zf", "l1a_ast", "fcb_f",
                               "fcb"]]


def emit_l1attn(ctx, src, dst, w, b_tiles_list):
    p = ctx.p
    B = ctx.B
    src_name, src_aps = src
    dst_name, dst_aps = dst
    scale = 1.0 / (2048.0 ** 0.5)
    Q_v = ctx.l1a_Q[:].rearrange("p (d n) -> p d n", d=16)
    PT_v = ctx.l1a_PT[:].rearrange("p (m n) -> p m n", m=16)
    vd_v = ctx.l1a_vd[:].rearrange("p (m c) -> p m c", m=16)
    p.op("gpsimd", "memset", ctx.ones_bf[:], 1.0, writes=[B["ones_bf"]])

    for b, tiles in enumerate(b_tiles_list):
        ntb = len(tiles)
        nend = ntb * 128
        for i, t in enumerate(tiles):
            par = i % 2
            r = ln_to_fm(ctx, par, src_aps[t], ctx.buf(f"{src_name}_t{t}"))
            p.op("gpsimd", "dma_start", out=ctx.l1a_xn_d[i],
                 in_=ctx.xf[par][:], reads=[r["xf"]],
                 writes=[ctx.buf(f"l1axn_{i}")], dma_key=f"xnd{par}")
        for h in range(3):
            ctx.RA.transition([B["l1a_Q"]])
            ctx.RB.transition([B["l1a_Wh0"], B["l1a_Wh1"]])
            ctx.RC.transition(l1attn_bufs_ph1(ctx))
            jobs = [(mat, half) for mat in ["q", "k", "v"]
                    for half in range(2)]
            for j, (mat, half) in enumerate(jobs):
                wh = ctx.l1a_WhB[j % 2]
                bwh = B[f"l1a_Wh{j % 2}"]
                p.op("sync", "dma_start",
                     out=wh[:].rearrange("p (e c) -> p e c", e=16),
                     in_=w[mat][0][:, h, half], reads=[w[mat][1]],
                     writes=[bwh], dma_key=f"whld{j % 2}")
                for i in range(ntb):
                    par = i % 2
                    xfp, bxf = ctx.xf[par], ctx.B[f"xf{par}"]
                    p.op("sync", "dma_start", out=xfp[:],
                         in_=ctx.l1a_xn_d[i],
                         reads=[ctx.buf(f"l1axn_{i}")], writes=[bxf],
                         dma_key=f"xld{par}")
                    for o in range(2):
                        q = (2 * i + o) % 4
                        mms = [(xfp[:, ec * 128:(ec + 1) * 128],
                                wh[:, ec * 1024 + o * 512:
                                   ec * 1024 + (o + 1) * 512],
                                ctx.ps_y[:, q * 512:(q + 1) * 512],
                                [bxf, bwh]) for ec in range(EC)]
                        p.mm_group([B[f"ps_q{q}"]], mms)
                        psl = ctx.ps_y[:, q * 512:(q + 1) * 512]
                        prd = [B[f"ps_q{q}"]]
                        if mat == "q":
                            p.op("scalar", "mul",
                                 ctx.l1a_qtmp[:, o * 512:(o + 1) * 512],
                                 psl, scale,
                                 reads=prd, writes=[B["l1a_qtmp"]])
                        else:
                            p.op("scalar", "copy",
                                 ctx.l1a_qtmp[:, half * 1024 + o * 512:
                                              half * 1024 + (o + 1) * 512],
                                 psl, reads=prd, writes=[B["l1a_qtmp"]])
                    if mat == "q":
                        for jj in range(8):
                            sl = slice(jj * 128, (jj + 1) * 128)
                            p.op("tensor", "transpose", ctx.ps_tr[:, sl],
                                 ctx.l1a_qtmp[:, sl], ctx.ident[:],
                                 reads=[B["l1a_qtmp"], B["ident"]],
                                 writes=[B["ps_tr"]])
                        p.op("vector", "tensor_copy",
                             Q_v[:, half * 8:(half + 1) * 8,
                                 i * 128:(i + 1) * 128],
                             ctx.ps_tr[:, 0:1024].rearrange(
                                 "p (d n) -> p d n", d=8),
                             reads=[B["ps_tr"]], writes=[B["l1a_Q"]])
                    else:
                        dd = (ctx.l1a_k_d if mat == "k" else ctx.l1a_v_d)
                        bname = f"l1a{mat}d_{i}"
                        p.op("gpsimd", "dma_start",
                             out=dd[i][:, half * 1024:(half + 1) * 1024],
                             in_=ctx.l1a_qtmp[:, half * 1024:
                                              (half + 1) * 1024],
                             reads=[B["l1a_qtmp"]],
                             writes=[ctx.buf(bname)], dma_key="kvst")
            ctx.RB.transition([B["l1a_PT"]])
            ctx.RC.transition(l1attn_bufs_ph23(ctx))
            for mt in range(ntb):
                kpar = mt % 2
                kw, bkw = ctx.l1a_kw[kpar], B[f"l1a_kw{kpar}"]
                kfm, bkfm = ctx.l1a_kfm[kpar], B[f"l1a_kfm{kpar}"]
                p.op("sync", "dma_start", out=kw[:],
                     in_=ctx.l1a_k_d[mt], reads=[ctx.buf(f"l1akd_{mt}")],
                     writes=[bkw], dma_key=f"kld{kpar}")
                for dt in range(EC):
                    sl = slice(dt * 128, (dt + 1) * 128)
                    p.op("tensor", "transpose", ctx.ps_tr[:, sl],
                         kw[:, sl], ctx.ident[:],
                         reads=[bkw, B["ident"]], writes=[B["ps_tr"]])
                p.op("vector", "tensor_copy", kfm[:], ctx.ps_tr[:],
                     reads=[B["ps_tr"]], writes=[bkfm])
                for o in range(nend // 512):
                    mms = [(kfm[:, dt * 128:(dt + 1) * 128],
                            Q_v[:, dt, o * 512:(o + 1) * 512],
                            ctx.ps_y[:, o * 512:(o + 1) * 512],
                            [bkfm, B["l1a_Q"]]) for dt in range(EC)]
                    p.mm_group([B[f"ps_q{o}"]], mms)
                negmax, rsum, recip = (ctx.l1a_ast[:, 0:1],
                                       ctx.l1a_ast[:, 1:2],
                                       ctx.l1a_ast[:, 2:3])
                p.op("vector", "tensor_copy", ctx.l1a_s16[:, 0:nend],
                     ctx.ps_y[:, 0:nend],
                     reads=ctx.psq(0, nend), writes=[B["l1a_s16"]])
                p.op("vector", "reduce_max", negmax, ctx.l1a_s16[:, 0:nend],
                     axis=AX.X, negate=True, reads=[B["l1a_s16"]],
                     writes=[B["l1a_ast"]])
                p.op("scalar", "activation", PT_v[:, mt, 0:nend],
                     ctx.l1a_s16[:, 0:nend],
                     AF.Exp, bias=negmax, accum_out=rsum,
                     reads=[B["l1a_s16"], B["l1a_ast"]],
                     writes=[B["l1a_PT"], B["l1a_ast"]])
                p.op("vector", "reciprocal", recip, rsum,
                     reads=[B["l1a_ast"]], writes=[B["l1a_ast"]])
                p.op("vector", "tensor_scalar", out=PT_v[:, mt, 0:nend],
                     in0=PT_v[:, mt, 0:nend], scalar1=recip, scalar2=None,
                     op0=ALU.mult, reads=[B["l1a_PT"], B["l1a_ast"]],
                     writes=[B["l1a_PT"]])
            for dseg in range(4):
                p.op("sync", "dma_start",
                     out=vd_v[:, 0:ntb],
                     in_=ctx.l1a_v_d[0:ntb, :, dseg * 512:(dseg + 1) * 512]
                     .rearrange("m p c -> p m c"),
                     reads=[ctx.buf(f"l1avd_{i}") for i in range(ntb)],
                     writes=[B["l1a_vd"]], dma_key="vld")
                for nt in range(ntb):
                    q = nt % 4
                    mms = [(PT_v[:, mc, nt * 128:(nt + 1) * 128],
                            vd_v[:, mc],
                            ctx.ps_y[:, q * 512:(q + 1) * 512],
                            [B["l1a_PT"], B["l1a_vd"]]) for mc in range(ntb)]
                    p.mm_group([B[f"ps_q{q}"]], mms)
                    at_sl = ctx.l1a_at_d[nt][:, dseg * 512:(dseg + 1) * 512]
                    bat = ctx.buf(f"l1aat_{nt}")
                    p.op("vector", "tensor_copy", ctx.l1a_av[:],
                         ctx.ps_y[:, q * 512:(q + 1) * 512],
                         reads=[B[f"ps_q{q}"]], writes=[B["l1a_av"]])
                    if h == 0:
                        p.op("gpsimd", "dma_start", out=at_sl,
                             in_=ctx.l1a_av[:], reads=[B["l1a_av"]],
                             writes=[bat], dma_key="atst")
                    else:
                        p.op("gpsimd", "dma_start", out=at_sl,
                             in_=ctx.l1a_av[:], accum_op=ALU.add,
                             reads=[B["l1a_av"]],
                             writes=[bat], dma_key="atst")
        ctx.RA.transition([B["l1a_Q"]])
        ctx.RC.transition(l1attn_bufs_ph4(ctx))
        p.op("sync", "dma_start", out=ctx.fcb_f[:], in_=w["fcb"],
             writes=[B["fcb_f"]], dma_key="wbr")
        p.op("vector", "tensor_copy", ctx.fcb[:], ctx.fcb_f[:],
             reads=[B["fcb_f"]], writes=[B["fcb"]])
        p.op("sync", "dma_start", out=Q_v[:], in_=w["fcT"][0],
             reads=[w["fcT"][1]], writes=[B["l1a_Q"]], dma_key="whld0")
        for nt in range(ntb):
            t = tiles[nt]
            par = nt % 2
            xa, bxa = ctx.xt[par], ctx.B[f"xt{par}"]
            p.op("sync", "dma_start", out=xa[:], in_=ctx.l1a_at_d[nt],
                 reads=[ctx.buf(f"l1aat_{nt}")], writes=[bxa],
                 dma_key=f"atld{par}")
            p.op("vector", "tensor_scalar", out=ctx.l1a_abf[:], in0=xa[:],
                 scalar1=1.0 / 3.0, scalar2=None, op0=ALU.mult,
                 reads=[bxa], writes=[B["l1a_abf"]])
            for dt in range(EC):
                sl = slice(dt * 128, (dt + 1) * 128)
                p.op("tensor", "transpose", ctx.ps_tr[:, sl],
                     ctx.l1a_abf[:, sl], ctx.ident[:],
                     reads=[B["l1a_abf"], B["ident"]], writes=[B["ps_tr"]])
            p.op("vector", "tensor_copy", ctx.l1a_zf[:], ctx.ps_tr[:],
                 reads=[B["ps_tr"]], writes=[B["l1a_zf"]])
            for o in range(4):
                mms = [(ctx.l1a_zf[:, dt * 128:(dt + 1) * 128],
                        Q_v[:, dt, o * 512:(o + 1) * 512],
                        ctx.ps_y[:, o * 512:(o + 1) * 512],
                        [B["l1a_zf"], B["l1a_Q"]]) for dt in range(EC)]
                mms.append((ctx.ones_bf[:],
                            ctx.fcb[:, o * 512:(o + 1) * 512],
                            ctx.ps_y[:, o * 512:(o + 1) * 512],
                            [B["ones_bf"], B["fcb"]]))
                p.mm_group([B[f"ps_q{o}"]], mms)
            xb_, bxb = ctx.xt[1 - par], ctx.B[f"xt{1 - par}"]
            p.op("sync", "dma_start", out=xb_[:], in_=src_aps[t],
                 reads=[ctx.buf(f"{src_name}_t{t}")], writes=[bxb],
                 dma_key=f"xld{1 - par}")
            p.op("vector", "tensor_tensor", out=xb_[:], in0=ctx.ps_y[:],
                 in1=xb_[:], op=ALU.add, reads=ctx.psq(0, 2048) + [bxb],
                 writes=[bxb])
            p.op("gpsimd", "dma_start", out=dst_aps[t], in_=xb_[:],
                 reads=[bxb], writes=[ctx.buf(f"{dst_name}_t{t}")],
                 dma_key=f"xst{1 - par}")


def build_program(ntb, nb):
    ntiles = ntb * nb
    N = ntiles * 128
    nc = bass.Bass("TRN2", target_bir_lowering=False)
    p = Prog(nc)
    ctx = Ctx(p, nc)

    ein = lambda name, shape: nc.dram_tensor(
        name, shape, F32, kind="ExternalInput").ap()
    xin = ein("xin", [N, D])
    w0 = {"WqT": ein("W0qT", [D, D]), "WkT": ein("W0kT", [D, D]),
          "WvT": ein("W0vT", [D, D]), "WoT": ein("W0oT", [D, D])}
    a_t, b_t = ein("A_T", [D, 512]), ein("B_T", [512, D])
    wr0, br0 = ein("Wr0T", [D, 8]), ein("br0", [1, 8])
    r1 = ein("r1T", [D, 8])
    w1_, w2_, w3_ = (ein("w1", [8, D, 512]), ein("w2", [8, 512, D]),
                     ein("w3", [8, D, 512]))
    q1, k1, v1 = (ein("W1qT", [3, D, D]), ein("W1kT", [3, D, D]),
                  ein("W1vT", [3, D, D]))
    fcT, fcb = ein("fcT", [D, D]), ein("fcb", [1, D])
    xout = nc.dram_tensor("xout", [N, D], F32, kind="ExternalOutput").ap()
    xd = nc.dram_tensor("xd", [N, D], F32, kind="Internal").ap()

    xin_t = xin.rearrange("(t p) d -> t p d", p=128)
    xd_t = xd.rearrange("(t p) d -> t p d", p=128)
    xout_t = xout.rearrange("(t p) d -> t p d", p=128)
    xin_l = [xin_t[t] for t in range(ntiles)]
    xd_l = [xd_t[t] for t in range(ntiles)]
    xout_l = [xout_t[t] for t in range(ntiles)]
    b_tiles = [list(range(b * ntb, (b + 1) * ntb)) for b in range(nb)]

    alloc_l0ffn(ctx)
    alloc_l0attn(ctx, ntb)
    alloc_l1ffn(ctx, ntiles)
    alloc_l1attn(ctx, ntb)
    ctx.emit_consts()

    w1f = {"rwT": r1, "w1": [w1_[e] for e in range(8)],
           "w2": [w2_[e] for e in range(8)], "w3": [w3_[e] for e in range(8)]}
    w1a = {"WqT": [q1[h] for h in range(3)], "WkT": [k1[h] for h in range(3)],
           "WvT": [v1[h] for h in range(3)], "fcT": fcT, "fcb": fcb}

    wc = build_wcache(ctx, dict(
        l0_AT=a_t, l0_BT=b_t, l0_Wr=wr0,
        l0a_k=w0["WkT"], l0a_q=w0["WqT"], l0a_v=w0["WvT"], l0a_wo=w0["WoT"],
        l1_rw=r1, l1_w1=[w1_[e] for e in range(8)],
        l1_w2=[w2_[e] for e in range(8)], l1_w3=[w3_[e] for e in range(8)],
        l1a_q=[q1[h] for h in range(3)], l1a_k=[k1[h] for h in range(3)],
        l1a_v=[v1[h] for h in range(3)], l1a_fc=fcT))

    w0c = {"k": wc["l0a_k"], "q": wc["l0a_q"], "v": wc["l0a_v"],
           "wo": wc["l0a_wo"]}
    w1fc = {"rwT": wc["l1_rw"], "w1": wc["l1_w1"], "w2": wc["l1_w2"],
            "w3": wc["l1_w3"]}
    w1ac = {"q": wc["l1a_q"], "k": wc["l1a_k"], "v": wc["l1a_v"],
            "fcT": wc["l1a_fc"], "fcb": fcb}

    with nc.named_scope("l0f1"):
        ctx.RC.transition(l0ffn_bufs(ctx))
        load_l0ffn_weights(ctx, wc, br0)
        emit_l0ffn(ctx, ("xin", xin_l), ("xd", xd_l), ntiles)
    with nc.named_scope("l0a"):
        emit_l0attn(ctx, ("xd", xd_l), ("xd", xd_l), w0c, b_tiles)
    p.flush_bg()
    with nc.named_scope("l0f2"):
        ctx.RC.transition(l0ffn_bufs(ctx))
        load_l0ffn_weights(ctx, wc, br0)
        emit_l0ffn(ctx, ("xd", xd_l), ("xd", xd_l), ntiles)
    with nc.named_scope("l1f1"):
        emit_l1ffn(ctx, ("xd", xd_l), ("xd", xd_l), ("xd", xd_l), w1fc,
                   ntiles)
    with nc.named_scope("l1a"):
        emit_l1attn(ctx, ("xd", xd_l), ("xd", xd_l), w1ac, b_tiles)
    with nc.named_scope("l1f2"):
        emit_l1ffn(ctx, ("xd", xd_l), ("xd", xd_l), ("xout", xout_l), w1fc,
                   ntiles)
    p.wait_all("gpsimd",
               [ctx.B[f"xout_t{t}"].last_write for t in range(ntiles)])
    return nc


def build_wcache(ctx, srcs):
    nc = ctx.nc
    wc = {}

    eager = {"l0_AT", "l0_BT", "l0_Wr"}

    def seg(name, shape, jobs):
        wc[name] = mk_wseg(ctx, name, shape, jobs, eager=name in eager)

    seg("l0_AT", [128, 16, 512],
        [((lambda a, ec=ec: a[:, ec]),
          srcs["l0_AT"][ec * 128:(ec + 1) * 128, :]) for ec in range(16)])
    seg("l0_BT", [128, 4, 2048],
        [((lambda a, ec=ec, o=o: a[:, ec, o * 1024:(o + 1) * 1024]),
          srcs["l0_BT"][ec * 128:(ec + 1) * 128, o * 1024:(o + 1) * 1024])
         for ec in range(4) for o in range(2)])
    seg("l0_Wr", [128, 16, 8],
        [((lambda a, ec=ec: a[:, ec]),
          srcs["l0_Wr"][ec * 128:(ec + 1) * 128, :]) for ec in range(16)])
    for m in ["k", "q", "v"]:
        seg(f"l0a_{m}", [128, 2, 16, 1024],
            [((lambda a, hh=hh, ec=ec: a[:, hh, ec]),
              srcs[f"l0a_{m}"][ec * 128:(ec + 1) * 128,
                               hh * 1024:(hh + 1) * 1024])
             for hh in range(2) for ec in range(16)])
    seg("l0a_wo", [128, 16, 2048],
        [((lambda a, hc=hc, o=o: a[:, hc, o * 1024:(o + 1) * 1024]),
          srcs["l0a_wo"][hc * 128:(hc + 1) * 128, o * 1024:(o + 1) * 1024])
         for hc in range(16) for o in range(2)])
    seg("l1_rw", [128, 16, 8],
        [((lambda a, ec=ec: a[:, ec]),
          srcs["l1_rw"][ec * 128:(ec + 1) * 128, :]) for ec in range(16)])
    for m in ["w1", "w3"]:
        seg(f"l1_{m}", [128, 8, 16, 512],
            [((lambda a, e=e, ec=ec: a[:, e, ec]),
              srcs[f"l1_{m}"][e][ec * 128:(ec + 1) * 128, :])
             for e in range(8) for ec in range(16)])
    seg("l1_w2", [128, 8, 4, 2048],
        [((lambda a, e=e, f=f, o=o: a[:, e, f, o * 1024:(o + 1) * 1024]),
          srcs["l1_w2"][e][f * 128:(f + 1) * 128, o * 1024:(o + 1) * 1024])
         for e in range(8) for f in range(4) for o in range(2)])
    for m in ["q", "k", "v"]:
        seg(f"l1a_{m}", [128, 3, 2, 16, 1024],
            [((lambda a, h=h, hh=hh, ec=ec: a[:, h, hh, ec]),
              srcs[f"l1a_{m}"][h][ec * 128:(ec + 1) * 128,
                                  hh * 1024:(hh + 1) * 1024])
             for h in range(3) for hh in range(2) for ec in range(16)])
    seg("l1a_fc", [128, 16, 2048],
        [((lambda a, dt=dt, o=o: a[:, dt, o * 1024:(o + 1) * 1024]),
          srcs["l1a_fc"][dt * 128:(dt + 1) * 128, o * 1024:(o + 1) * 1024])
         for dt in range(16) for o in range(2)])
    return wc


def host_inputs(inputs):
    C = np.ascontiguousarray
    f = np.float32
    w = {
        "W0qT": C(inputs["L0_attn_Wq"].T.astype(f, copy=False)),
        "W0kT": C(inputs["L0_attn_Wk"].T.astype(f, copy=False)),
        "W0vT": C(inputs["L0_attn_Wv"].T.astype(f, copy=False)),
        "W0oT": C(inputs["L0_attn_Wo"].T.astype(f, copy=False)),
        "A_T": C(np.transpose(inputs["L0_ffn_A"], (2, 0, 1)).reshape(D, 512)),
        "B_T": C(np.transpose(inputs["L0_ffn_B"], (0, 2, 1)).reshape(512, D)),
        "Wr0T": C(inputs["L0_ffn_Wr"].T),
        "br0": C(inputs["L0_ffn_br"].reshape(1, 8)),
        "r1T": C(inputs["L1_ffn_router_w"].T),
        "w1": C(inputs["L1_ffn_w1"]),
        "w2": C(inputs["L1_ffn_w2"]),
        "w3": C(inputs["L1_ffn_w3"]),
        "W1qT": C(np.transpose(inputs["L1_attn_Wq"], (0, 2, 1))),
        "W1kT": C(np.transpose(inputs["L1_attn_Wk"], (0, 2, 1))),
        "W1vT": C(np.transpose(inputs["L1_attn_Wv"], (0, 2, 1))),
        "fcT": C(inputs["L1_attn_fc_w"].T),
        "fcb": C(inputs["L1_attn_fc_b"].reshape(1, D)),
    }
    return {k: np.asarray(v, dtype=np.float32) for k, v in w.items()}


def kernel(**inputs):
    x = np.asarray(inputs["L0_attn_resid_pre"], dtype=np.float32)
    w = host_inputs(inputs)
    ncares = 8
    nb = 2
    ntb = 16
    nc = build_program(ntb, nb)
    in_maps = []
    for c in range(ncares):
        m = dict(w)
        m["xin"] = np.ascontiguousarray(
            x[c * nb:(c + 1) * nb].reshape(nb * 2048, 2048))
        in_maps.append(m)
    res = run_bass_kernel_spmd(nc, in_maps, core_ids=list(range(ncares)))
    out = np.concatenate(
        [r["xout"].reshape(nb, 2048, 2048) for r in res.results], axis=0)
    return out


# version 15
# speedup vs baseline: 1.0388x; 1.0379x over previous
import numpy as np
import concourse.bass as bass
import concourse.mybir as mybir
from concourse.bass_utils import run_bass_kernel_spmd




class Buf:
    __slots__ = ("name", "last_write", "readers")

    def __init__(self, name=""):
        self.name = name
        self.last_write = None
        self.readers = []


class Prog:
    def __init__(self, nc):
        self.nc = nc
        self.sems = {}
        self.cnt = {}
        self.mark = {}
        self.bg = []

    def pump(self, k):
        for _ in range(min(k, len(self.bg))):
            self.bg.pop(0)[1]()

    def flush_bg(self):
        while self.bg:
            self.bg.pop(0)[1]()

    def flush_seg(self, buf):
        if not any(b is buf for b, _ in self.bg):
            return
        last = max(i for i, (b, _) in enumerate(self.bg) if b is buf)
        for _ in range(last + 1):
            self.bg.pop(0)[1]()

    def sem(self, key):
        if key not in self.sems:
            self.sems[key] = self.nc.alloc_semaphore(key)
            self.cnt[key] = 0
        return self.sems[key]

    def _collect(self, reads, writes):
        need = {}

        def add(tok):
            if tok is None:
                return
            k, v = tok
            if need.get(k, 0) < v:
                need[k] = v

        for b in reads:
            add(b.last_write)
        for b in writes:
            add(b.last_write)
            for t in b.readers:
                add(t)
        return need

    def _wait(self, engine, eng, need):
        m = self.mark.setdefault(engine, {})
        for k in sorted(need):
            v = need[k]
            if m.get(k, 0) < v:
                eng.wait_ge(self.sem(k), v)
                m[k] = v

    def _finish(self, inst, engine, dma, reads, writes, key=None):
        if key is None:
            key = engine + ("_dma" if dma else "_op")
        inc = 16 if dma else 1
        s = self.sem(key)
        self.cnt[key] += inc
        inst.then_inc(s, inc)
        tok = (key, self.cnt[key])
        for b in writes:
            b.last_write = tok
            b.readers = []
        for b in reads:
            b.readers.append(tok)
        return tok

    def op(self, engine, method, *args, reads=(), writes=(), dma_key=None, **kw):
        eng = getattr(self.nc, engine)
        self._wait(engine, eng, self._collect(reads, writes))
        inst = getattr(eng, method)(*args, **kw)
        dma = method.startswith("dma_start")
        return self._finish(inst, engine, dma, reads, writes,
                            key=dma_key if dma else None)

    def mm_group(self, psum_writes, mms, engine="tensor"):
        eng = getattr(self.nc, engine)
        need = self._collect([], psum_writes)
        n = len(mms)
        insts = []
        for i, (lhsT, rhs, out, rbufs) in enumerate(mms):
            nd = dict(need) if i == 0 else {}
            for b in rbufs:
                t = b.last_write
                if t is not None and nd.get(t[0], 0) < t[1]:
                    nd[t[0]] = t[1]
            self._wait(engine, eng, nd)
            insts.append(
                eng.matmul(out, lhsT, rhs, start=(i == 0), stop=(i == n - 1))
            )
        tok = self._finish(insts[-1], engine, False, [], psum_writes)
        for _, _, _, rbufs in mms:
            for b in rbufs:
                if tok not in b.readers:
                    b.readers.append(tok)
        return tok

    def barrier_token(self, engine):
        key = engine + "_op"
        return (key, self.cnt.get(key, 0))

    def wait_all(self, engine, toks):
        eng = getattr(self.nc, engine)
        need = {}
        for t in toks:
            if t and need.get(t[0], 0) < t[1]:
                need[t[0]] = t[1]
        self._wait(engine, eng, need)



F32 = mybir.dt.float32
BF16 = mybir.dt.bfloat16
AF = mybir.ActivationFunctionType
ALU = mybir.AluOpType
AX = mybir.AxisListType

D = 2048
EC = D // 128


class Region:
    def __init__(self, nc, name, nbytes):
        self.nc = nc
        self.name = name
        self.t = nc.alloc_sbuf_tensor(name, [128, nbytes], mybir.dt.uint8)
        self.addr = nc.lookup_mloc(self.t).addr
        self.nbytes = nbytes
        self.live = []

    def place(self, name, shape, dtype, rel):
        per_part = 1
        for s in shape[1:]:
            per_part *= s
        per_part *= mybir.dt.size(dtype)
        assert rel + per_part <= self.nbytes, \
            f"{name}: {rel}+{per_part} > {self.nbytes} in {self.name}"
        h = self.nc.virtual_sbuf_tensor(name, list(shape), dtype)
        self.nc._tensor(h.name, list(h.shape), h.dtype, type="SB")
        mloc = self.nc.lookup_mloc(h)
        mloc.allocated = True
        mloc.addr = self.addr + rel
        return h

    def transition(self, bufs_in):
        toks = []
        for b in self.live:
            if b.last_write is not None:
                toks.append(b.last_write)
            toks.extend(b.readers)
        for b in bufs_in:
            b.readers.extend(toks)
        self.live = list(bufs_in)


class Ctx:

    def __init__(self, p, nc):
        self.p = p
        self.nc = nc
        a = nc.alloc_sbuf_tensor
        self.B = {}
        self.ident = a("c_ident", [128, 128], BF16)
        self.ones_col = a("c_ones", [1, 128], F32)
        self.cpool = a("c_pool", [128, 1], F32)
        nc.const_aps.register(F32, 1e-5, self.cpool[:, 0:1])
        self.xt = [a(f"w_x{i}", [128, D], F32) for i in range(2)]
        self.xn = a("w_xn", [128, D], BF16)
        self.xf = [a(f"w_xf{i}", [128, D], BF16) for i in range(2)]
        self.st = [a(f"w_st{i}", [128, 32], F32) for i in range(2)]
        self.stage = [a(f"w_stg{i}", [128, 1024], F32) for i in range(2)]
        self.wbounce = [a(f"w_wb{i}", [128, 1024], BF16) for i in range(2)]
        self.ps_tr = nc.alloc_psum_tensor("ps_tr", [128, D], BF16)
        self.ps_y = nc.alloc_psum_tensor("ps_y", [128, D], F32)
        self.ps_h = nc.alloc_psum_tensor("ps_h", [128, 512], F32)
        self.ps_b = nc.alloc_psum_tensor("ps_b", [128, 512], BF16)
        for name in ["ident", "ones_col", "cpool", "xn",
                     "ps_tr", "ps_h", "ps_b"]:
            self.B[name] = Buf(name)
        for q in range(4):
            self.B[f"ps_q{q}"] = Buf(f"ps_q{q}")
        for i in range(2):
            for name in ["xt", "xf", "st", "stage", "wbounce"]:
                self.B[f"{name}{i}"] = Buf(f"{name}{i}")
        self.RA = Region(nc, "regA", 64 * 1024)
        self.RB = Region(nc, "regB", 64 * 1024)
        self.RC = Region(nc, "regC", 36 * 1024)

    def emit_consts(self):
        p = self.p
        B = self.B
        p.op("gpsimd", "memset", self.ident[:], 0.0, writes=[B["ident"]])
        p.op("gpsimd", "affine_select", out=self.ident[:], in_=self.ident[:],
             compare_op=ALU.not_equal, fill=1.0, base=0, pattern=[[-1, 128]],
             channel_multiplier=1, reads=[B["ident"]], writes=[B["ident"]])
        p.op("gpsimd", "memset", self.ones_col[:], 1.0, writes=[B["ones_col"]])
        p.op("gpsimd", "memset", self.cpool[:, 0:1], 1e-5, writes=[B["cpool"]])

    def buf(self, name):
        if name not in self.B:
            self.B[name] = Buf(name)
        return self.B[name]

    def filler(self, tag=None):
        self.p.op("tensor", "matmul", self.ps_y[0:1, 1792:1856],
                  self.ident[:, 0:1], self.ident[:, 0:64],
                  start=True, stop=True,
                  reads=[self.B["ident"]], writes=[self.B["ps_q3"]])

    def psq(self, c0, c1):
        return [self.B[f"ps_q{q}"] for q in range(4)
                if q * 512 < c1 and (q + 1) * 512 > c0]


def ln_to_fm(ctx, par, src_ap, src_buf, want_fm=True, dma_engine="sync"):
    p = ctx.p
    xt, xn, xf, st = ctx.xt[par], ctx.xn, ctx.xf[par], ctx.st[par]
    ps = ctx.ps_tr
    bxt, bxn, bxf, bst = (ctx.B[f"xt{par}"], ctx.B["xn"],
                          ctx.B[f"xf{par}"], ctx.B[f"st{par}"])
    bps = ctx.B["ps_tr"]

    p.op(dma_engine, "dma_start", out=xt[:], in_=src_ap,
         reads=[src_buf], writes=[bxt], dma_key=f"xld{par}")
    p.pump(3)
    bnst = st[:, 0:24].rearrange("p (c s) -> p c s", c=4)
    mean, var = st[:, 24:25], st[:, 25:26]
    negmean, rstd, tmp = st[:, 26:27], st[:, 27:28], st[:, 28:29]
    for c in range(4):
        p.op("vector", "bn_stats", bnst[:, c], xt[:, c * 512:(c + 1) * 512],
             reads=[bxt], writes=[bst])
    p.op("vector", "bn_aggr", st[:, 24:26], bnst[:], reads=[bst], writes=[bst])
    p.op("vector", "tensor_scalar", out=negmean, in0=mean, scalar1=-1.0,
         scalar2=None, op0=ALU.mult, reads=[bst], writes=[bst])
    p.op("scalar", "activation", tmp, var, AF.Ln, bias=1e-5, scale=1.0,
         reads=[bst, ctx.B["cpool"]], writes=[bst])
    p.op("scalar", "activation", rstd, tmp, AF.Exp, bias=0.0, scale=-0.5,
         reads=[bst], writes=[bst])
    p.op("vector", "tensor_scalar", out=xn[:], in0=xt[:], scalar1=negmean,
         scalar2=rstd, op0=ALU.add, op1=ALU.mult,
         reads=[bxt, bst], writes=[bxn])
    out = {"xt": bxt, "xn": bxn}
    if want_fm:
        for ec in range(EC):
            sl = slice(ec * 128, (ec + 1) * 128)
            p.op("tensor", "transpose", ps[:, sl], xn[:, sl], ctx.ident[:],
                 reads=[bxn, ctx.B["ident"]], writes=[bps])
        p.op("vector", "tensor_copy", xf[:], ps[:], reads=[bps], writes=[bxf])
        out["xf"] = bxf
    return out


def _stream_cast(ctx, dst_slices_and_aps, buf, scale=None):
    p = ctx.p
    for i, (dst_sl, src_ap) in enumerate(dst_slices_and_aps):
        par = i % 2
        bj = buf[i] if isinstance(buf, list) else buf
        stg, bstg = ctx.stage[par], ctx.B[f"stage{par}"]
        n = src_ap.shape[-1]
        p.op("sync", "dma_start", out=stg[:, 0:n], in_=src_ap,
             writes=[bstg], dma_key=f"wstg{par}")
        if scale is None:
            p.op("vector", "tensor_copy", dst_sl, stg[:, 0:n],
                 reads=[bstg], writes=[bj])
        else:
            p.op("vector", "tensor_scalar", out=dst_sl, in0=stg[:, 0:n],
                 scalar1=scale, scalar2=None, op0=ALU.mult,
                 reads=[bstg], writes=[bj])


def emit_wcache(ctx, jobs, seg_buf):
    p = ctx.p
    for i, (dst_sl, src_ap) in enumerate(jobs):
        par = i % 2
        stg, bstg = ctx.stage[par], ctx.B[f"stage{par}"]
        wb, bwb = ctx.wbounce[par], ctx.B[f"wbounce{par}"]
        n = src_ap.shape[-1]
        p.op("sync", "dma_start", out=stg[:, 0:n], in_=src_ap,
             writes=[bstg], dma_key=f"wstg{par}")
        p.op("scalar", "copy", wb[:, 0:n], stg[:, 0:n],
             reads=[bstg], writes=[bwb])
        p.op("gpsimd", "dma_start", out=dst_sl, in_=wb[:, 0:n],
             reads=[bwb], writes=[seg_buf], dma_key=f"wcst{par}")


def mk_wseg(ctx, name, shape, jobs, eager=True):
    nc = ctx.nc
    ap_ = nc.dram_tensor("wc_" + name, shape, BF16, kind="Internal").ap()
    b = Buf("wc_" + name)
    if eager:
        emit_wcache(ctx, [(dst(ap_), src) for dst, src in jobs], b)
    else:
        for dst, src in jobs:
            ctx.p.bg.append(
                (b, lambda d=dst, sr=src: emit_wcache(ctx, [(d(ap_), sr)],
                                                      b)))
    return (ap_, b)



def alloc_l0ffn(ctx):
    RC = ctx.RC
    ctx.l0_AT = RC.place("l0_AT", [128, 16 * 512], BF16, 0)
    ctx.l0_BT = RC.place("l0_BT", [128, 4 * 2048], BF16, 16384)
    ctx.l0_WrT = RC.place("l0_WrT", [128, 16 * 8], BF16, 32768)
    ctx.l0_brbc = RC.place("l0_brbc", [128, 8], F32, 33056)
    ctx.l0_lg = RC.place("l0_lg", [128, 24], F32, 33408)
    ctx.l0_mw = RC.place("l0_mw", [128, 8], F32, 33504)
    ctx.l0_hs = RC.place("l0_hs", [128, 512], BF16, 33568)
    ctx.l0_hf = RC.place("l0_hf", [128, 512], BF16, 34592)
    ctx.l0_brrow = ctx.nc.alloc_sbuf_tensor("l0_brrow", [1, 8], F32)
    for n in ["l0_AT", "l0_BT", "l0_WrT", "l0_brbc", "l0_lg", "l0_mw",
              "l0_hs", "l0_hf", "l0_brrow"]:
        ctx.B[n] = Buf(n)


def l0ffn_bufs(ctx):
    return [ctx.B[n] for n in ["l0_AT", "l0_BT", "l0_WrT", "l0_brbc",
                               "l0_lg", "l0_mw", "l0_hs", "l0_hf"]]


def load_l0ffn_weights(ctx, wc, ap_br):
    p = ctx.p
    B = ctx.B
    p.op("sync", "dma_start",
         out=ctx.l0_AT[:].rearrange("p (e r) -> p e r", e=16),
         in_=wc["l0_AT"][0], reads=[wc["l0_AT"][1]],
         writes=[B["l0_AT"]], dma_key="wld0")
    p.op("sync", "dma_start",
         out=ctx.l0_BT[:].rearrange("p (e r) -> p e r", e=4),
         in_=wc["l0_BT"][0], reads=[wc["l0_BT"][1]],
         writes=[B["l0_BT"]], dma_key="wld1")
    p.op("sync", "dma_start",
         out=ctx.l0_WrT[:].rearrange("p (e r) -> p e r", e=16),
         in_=wc["l0_Wr"][0], reads=[wc["l0_Wr"][1]],
         writes=[B["l0_WrT"]], dma_key="wld2")
    p.op("sync", "dma_start", out=ctx.l0_brrow[:], in_=ap_br,
         writes=[B["l0_brrow"]], dma_key="wbr")
    p.op("tensor", "matmul", ctx.ps_h[:, 0:8], ctx.ones_col[:],
         ctx.l0_brrow[:], start=True, stop=True,
         reads=[ctx.B["ones_col"], B["l0_brrow"]], writes=[B["ps_h"]])
    p.op("vector", "tensor_copy", ctx.l0_brbc[:], ctx.ps_h[:, 0:8],
         reads=[B["ps_h"]], writes=[B["l0_brbc"]])


def emit_l0ffn(ctx, src, dst, ntiles):
    p = ctx.p
    B = ctx.B
    src_name, src_aps = src
    dst_name, dst_aps = dst
    scale = 0.5 / 64.0
    lg, mw, hs, hf = ctx.l0_lg, ctx.l0_mw, ctx.l0_hs, ctx.l0_hf
    blg, bmw, bhs, bhf = B["l0_lg"], B["l0_mw"], B["l0_hs"], B["l0_hf"]

    for t in range(ntiles):
        par = t % 2
        r = ln_to_fm(ctx, par, src_aps[t], ctx.buf(f"{src_name}_t{t}"))
        bxt, bxf = r["xt"], r["xf"]
        xf = ctx.xf[par]
        ph, bph = ctx.ps_h, B["ps_h"]
        mms = [(xf[:, ec * 128:(ec + 1) * 128],
                ctx.l0_WrT[:, ec * 8:(ec + 1) * 8],
                ph[:, 0:8], [bxf, B["l0_WrT"]]) for ec in range(EC)]
        p.mm_group([bph], mms)
        p.op("vector", "tensor_tensor", out=lg[:, 0:8], in0=ph[:, 0:8],
             in1=ctx.l0_brbc[:, 0:8], op=ALU.add,
             reads=[bph, B["l0_brbc"]], writes=[blg])
        m1, negm1, esum, recip = (lg[:, 8:9], lg[:, 9:10], lg[:, 10:11],
                                  lg[:, 11:12])
        p.op("vector", "reduce_max", negm1, lg[:, 0:8], axis=AX.X, negate=True,
             reads=[blg], writes=[blg])
        p.op("vector", "tensor_scalar", out=m1, in0=negm1, scalar1=-1.0,
             scalar2=None, op0=ALU.mult, reads=[blg], writes=[blg])
        p.op("scalar", "activation", lg[:, 12:20], lg[:, 0:8], AF.Exp,
             bias=negm1, scale=1.0, accum_out=esum,
             reads=[blg], writes=[blg])
        p.op("vector", "reciprocal", recip, esum, reads=[blg], writes=[blg])
        p.op("vector", "tensor_scalar", out=mw[:], in0=lg[:, 0:8], scalar1=m1,
             scalar2=recip, op0=ALU.is_ge, op1=ALU.mult,
             reads=[blg], writes=[bmw])
        mms = [(xf[:, ec * 128:(ec + 1) * 128],
                ctx.l0_AT[:, ec * 512:(ec + 1) * 512],
                ph[:, 0:512], [bxf, B["l0_AT"]]) for ec in range(EC)]
        p.mm_group([bph], mms)
        for e in range(8):
            p.op("vector", "tensor_scalar", out=hs[:, e * 64:(e + 1) * 64],
                 in0=ph[:, e * 64:(e + 1) * 64], scalar1=mw[:, e:e + 1],
                 scalar2=None, op0=ALU.mult,
                 reads=[bph, bmw], writes=[bhs])
        for j in range(4):
            sl = slice(j * 128, (j + 1) * 128)
            p.op("tensor", "transpose", ctx.ps_b[:, sl], hs[:, sl],
                 ctx.ident[:], reads=[bhs, ctx.B["ident"]],
                 writes=[B["ps_b"]])
        p.op("vector", "tensor_copy", hf[:], ctx.ps_b[:],
             reads=[B["ps_b"]], writes=[bhf])
        py = ctx.ps_y
        for o in range(4):
            mms = [(hf[:, j * 128:(j + 1) * 128],
                    ctx.l0_BT[:, j * 2048 + o * 512: j * 2048 + (o + 1) * 512],
                    py[:, o * 512:(o + 1) * 512],
                    [bhf, B["l0_BT"]]) for j in range(4)]
            p.mm_group([B[f"ps_q{o}"]], mms)
        p.op("vector", "scalar_tensor_tensor", out=ctx.xt[par][:], in0=py[:],
             scalar=scale, in1=ctx.xt[par][:], op0=ALU.mult, op1=ALU.add,
             reads=ctx.psq(0, 2048) + [bxt], writes=[bxt])
        p.op("gpsimd", "dma_start", out=dst_aps[t], in_=ctx.xt[par][:],
             reads=[bxt], writes=[ctx.buf(f"{dst_name}_t{t}")],
             dma_key=f"xst{par}")



def _alibi_slopes16():
    start = 2.0 ** -0.5
    return [start ** (i + 1) for i in range(16)]


def alloc_l0attn(ctx, ntb):
    nc = ctx.nc
    RA, RB, RC = ctx.RA, ctx.RB, ctx.RC
    ctx.Afm = RA.place("l0a_Afm", [128, 16 * 2048], BF16, 0)
    ctx.Bv = RB.place("l0a_Bv", [128, 16 * 2048], BF16, 0)
    ctx.Wh = RC.place("l0a_Wh", [128, 16 * 1024], BF16, 0)
    ctx.s_sb = [RC.place(f"l0a_s{i}", [128, 2048], F32, i * 8192)
                for i in range(2)]
    ctx.PP = RC.place("l0a_PP", [128, 2048], BF16, 16384)
    ctx.PT = RC.place("l0a_PTs", [128, 2048], BF16, 20480)
    ctx.zfull = RC.place("l0a_z", [128, 2048], BF16, 24576)
    ctx.causal = RC.place("l0a_causal", [128, 128], F32, 28672)
    ctx.att_st = RC.place("l0a_ast", [128, 16], F32, 29184)
    ctx.iota_row = RC.place("l0a_iota", [128, D], mybir.dt.float16,
                            29760)
    ctx.zf_fm = RC.place("l0a_zf", [128, 2048], BF16, 0)
    ctx.ktmp = ctx.xn[:, 0:1024]
    ctx.qstg = ctx.xn[:, 1024:2048]
    ctx.xn_d = nc.dram_tensor("l0a_xn_d", [ntb, 128, 2048], BF16,
                              kind="Internal").ap()
    ctx.q_d = nc.dram_tensor("l0a_q_d", [ntb, 128, 2048], BF16,
                             kind="Internal").ap()
    ctx.z_d = nc.dram_tensor("l0a_z_d", [ntb, 128, 2048], BF16,
                             kind="Internal").ap()
    for n in ["Afm", "Bv", "Wh", "s_sb0", "s_sb1", "PP", "PT", "zfull",
              "zf_fm", "causal", "att_st", "iota_row"]:
        ctx.B[n] = Buf(n)
    ctx.B["ktmp"] = ctx.B["xn"]


def l0attn_bufs_phA(ctx):
    return [ctx.B[n] for n in ["Wh"]]


def l0attn_bufs_phB(ctx):
    return [ctx.B[n] for n in ["s_sb0", "s_sb1", "PP", "PT", "zfull",
                               "causal", "att_st", "iota_row"]]


def l0attn_bufs_phC(ctx):
    return [ctx.B[n] for n in ["zfull", "zf_fm"]]


def emit_l0attn_consts(ctx):
    p = ctx.p
    p.op("gpsimd", "memset", ctx.causal[:], 0.0, writes=[ctx.B["causal"]])
    p.op("gpsimd", "affine_select", out=ctx.causal[:], in_=ctx.causal[:],
         compare_op=ALU.is_ge, fill=-1e9, base=0, pattern=[[-1, 128]],
         channel_multiplier=1, reads=[ctx.B["causal"]],
         writes=[ctx.B["causal"]])


def emit_l0attn(ctx, src, dst, w, b_tiles_list):
    p = ctx.p
    B = ctx.B
    src_name, src_aps = src
    dst_name, dst_aps = dst
    slopes = _alibi_slopes16()
    Afm_v = ctx.Afm[:].rearrange("p (h k) -> p h k", h=16)
    Bv_v = ctx.Bv[:].rearrange("p (t c) -> p t c", t=16)

    for b, tiles in enumerate(b_tiles_list):
        ntb = len(tiles)
        for i, t in enumerate(tiles):
            par = i % 2
            r = ln_to_fm(ctx, par, src_aps[t], ctx.buf(f"{src_name}_t{t}"))
            p.op("gpsimd", "dma_start", out=ctx.xn_d[i], in_=ctx.xf[par][:],
                 reads=[r["xf"]], writes=[ctx.buf(f"xnd_{i}")],
                 dma_key=f"xnd{par}")
        ctx.RA.transition([B["Afm"]])
        ctx.RB.transition([B["Bv"]])
        ctx.RC.transition(l0attn_bufs_phA(ctx))
        for mat in ["k", "q", "v"]:
            qscale = (1.0 / (128.0 ** 0.5)) if mat == "q" else None
            for half in range(2):
                p.flush_seg(w[mat][1])
                p.op("sync", "dma_start",
                     out=ctx.Wh[:].rearrange("p (e c) -> p e c", e=16),
                     in_=w[mat][0][:, half], reads=[w[mat][1]],
                     writes=[B["Wh"]], dma_key="whld")
                for i in range(ntb):
                    par = i % 2
                    xfp, bxf = ctx.xf[par], ctx.B[f"xf{par}"]
                    p.op("sync", "dma_start", out=xfp[:], in_=ctx.xn_d[i],
                         reads=[ctx.buf(f"xnd_{i}")], writes=[bxf],
                         dma_key=f"xld{par}")
                    p.pump(4)
                    for o in range(2):
                        q = (2 * i + o) % 4
                        mms = [(xfp[:, ec * 128:(ec + 1) * 128],
                                ctx.Wh[:, ec * 1024 + o * 512:
                                       ec * 1024 + (o + 1) * 512],
                                ctx.ps_y[:, q * 512:(q + 1) * 512],
                                [bxf, B["Wh"]]) for ec in range(EC)]
                        p.mm_group([B[f"ps_q{q}"]], mms)
                        psl = ctx.ps_y[:, q * 512:(q + 1) * 512]
                        prd = [B[f"ps_q{q}"]]
                        if mat == "v":
                            p.op("scalar", "copy",
                                 ctx.Bv[:, i * 2048 + half * 1024 + o * 512:
                                        i * 2048 + half * 1024 +
                                        (o + 1) * 512],
                                 psl, reads=prd, writes=[B["Bv"]])
                        elif qscale is None:
                            p.op("scalar", "copy",
                                 ctx.ktmp[:, o * 512:(o + 1) * 512], psl,
                                 reads=prd, writes=[B["ktmp"]])
                        else:
                            p.op("scalar", "mul",
                                 ctx.ktmp[:, o * 512:(o + 1) * 512], psl,
                                 qscale, reads=prd, writes=[B["ktmp"]])
                    if mat == "v":
                        continue
                    for j in range(8):
                        sl = slice(j * 128, (j + 1) * 128)
                        p.op("tensor", "transpose", ctx.ps_tr[:, sl],
                             ctx.ktmp[:, sl], ctx.ident[:],
                             reads=[B["ktmp"], B["ident"]],
                             writes=[B["ps_tr"]])
                    if mat == "k":
                        p.op("vector", "tensor_copy",
                             Afm_v[:, half * 8:(half + 1) * 8,
                                   i * 128:(i + 1) * 128],
                             ctx.ps_tr[:, 0:1024].rearrange(
                                 "p (h k) -> p h k", h=8),
                             reads=[B["ps_tr"]], writes=[B["Afm"]])
                    else:
                        p.op("vector", "tensor_copy", ctx.qstg,
                             ctx.ps_tr[:, 0:1024],
                             reads=[B["ps_tr"]], writes=[B["xn"]])
                        p.op("gpsimd", "dma_start",
                             out=ctx.q_d[i][:, half * 1024:(half + 1) * 1024],
                             in_=ctx.qstg,
                             reads=[B["xn"]], writes=[ctx.buf(f"qd_d{i}")],
                             dma_key="qst")
        ctx.RC.transition(l0attn_bufs_phB(ctx))
        p.op("gpsimd", "iota", ctx.iota_row[:], [[1, D]], base=0,
             channel_multiplier=0, allow_small_or_imprecise_dtypes=True,
             writes=[B["iota_row"]])
        emit_l0attn_consts(ctx)
        iters = [(qtl, h) for qtl in range(ntb) for h in range(16)]
        n_it = len(iters)


        def qd_slot(qtl):
            return ctx.xf[qtl % 2], ctx.B[f"xf{qtl % 2}"]

        def s_qload(qtl):
            qd, bqd = qd_slot(qtl)
            p.op("sync", "dma_start", out=qd[:], in_=ctx.q_d[qtl],
                 reads=[ctx.buf(f"qd_d{qtl}")], writes=[bqd],
                 dma_key=f"qld{qtl % 2}")

        def chunks_of(i):
            qtl, h = iters[i]
            kend = (qtl + 1) * 128
            out = []
            for ci, c0 in enumerate(range(0, kend, 1024)):
                c1 = min(c0 + 1024, kend)
                pairbase = (ci if kend > 1024 else i % 2) * 1024
                out.append((c0, c1, pairbase))
            return out

        def s1_scores(i):
            qtl, h = iters[i]
            qd, bqd = qd_slot(qtl)
            p.pump(1)
            if h == 0:
                s_qload(qtl)
            for c0, c1, pb in chunks_of(i):
                for s0 in range(c0, c1, 512):
                    s1 = min(s0 + 512, c1)
                    pcol = pb + (s0 - c0)
                    p.op("tensor", "matmul",
                         ctx.ps_y[:, pcol:pcol + (s1 - s0)],
                         qd[:, h * 128:(h + 1) * 128],
                         ctx.Afm[:, h * 2048 + s0:h * 2048 + s1],
                         start=True, stop=True,
                         reads=[bqd, B["Afm"]],
                         writes=ctx.psq(pcol, pcol + (s1 - s0)))

        def s2a_evac(i):
            qtl, h = iters[i]
            kend = (qtl + 1) * 128
            qb = qtl * 128
            ss, bss = ctx.s_sb[i % 2], B[f"s_sb{i % 2}"]
            for c0, c1, pb in chunks_of(i):
                if qb >= c1:
                    p.op("scalar", "copy", ss[:, c0:c1],
                         ctx.ps_y[:, pb:pb + (c1 - c0)],
                         reads=ctx.psq(pb, pb + (c1 - c0)), writes=[bss])
                    continue
                if qb > c0:
                    p.op("scalar", "copy", ss[:, c0:qb],
                         ctx.ps_y[:, pb:pb + (qb - c0)],
                         reads=ctx.psq(pb, pb + (qb - c0)), writes=[bss])
                p.op("vector", "scalar_tensor_tensor", out=ss[:, qb:kend],
                     in0=ctx.ps_y[:, pb + (qb - c0):pb + (c1 - c0)],
                     scalar=1.0, in1=ctx.causal[:],
                     op0=ALU.mult, op1=ALU.add,
                     reads=ctx.psq(pb + (qb - c0), pb + (c1 - c0)) +
                     [B["causal"]], writes=[bss])

        def s2b_stats(i):
            qtl, h = iters[i]
            kend = (qtl + 1) * 128
            ss, bss = ctx.s_sb[i % 2], B[f"s_sb{i % 2}"]
            ast = ctx.att_st[:, 8 * (i % 2):8 * (i % 2) + 8]
            p.op("vector", "scalar_tensor_tensor", out=ss[:, 0:kend],
                 in0=ctx.iota_row[:, 0:kend], scalar=slopes[h],
                 in1=ss[:, 0:kend], op0=ALU.mult, op1=ALU.add,
                 reads=[B["iota_row"], bss], writes=[bss])
            p.op("vector", "reduce_max", ast[:, 0:1], ss[:, 0:kend],
                 axis=AX.X, negate=True,
                 reads=[bss], writes=[B["att_st"]])

        def s3_exp(i):
            qtl, h = iters[i]
            kend = (qtl + 1) * 128
            ss, bss = ctx.s_sb[i % 2], B[f"s_sb{i % 2}"]
            ast = ctx.att_st[:, 8 * (i % 2):8 * (i % 2) + 8]
            p.op("scalar", "activation", ctx.PP[:, 0:kend], ss[:, 0:kend],
                 AF.Exp, bias=ast[:, 0:1], accum_out=ast[:, 1:2],
                 reads=[bss, B["att_st"]],
                 writes=[B["PP"], B["att_st"]])

        def s4567_tail(i):
            qtl, h = iters[i]
            kend = (qtl + 1) * 128
            nkc = qtl + 1
            ast = ctx.att_st[:, 8 * (i % 2):8 * (i % 2) + 8]
            for kc in range(nkc):
                sl = slice(kc * 128, (kc + 1) * 128)
                p.op("tensor", "transpose", ctx.ps_tr[:, sl],
                     ctx.PP[:, sl], ctx.ident[:],
                     reads=[B["PP"], B["ident"]], writes=[B["ps_tr"]])
            p.op("vector", "reciprocal", ast[:, 2:3], ast[:, 1:2],
                 reads=[B["att_st"]], writes=[B["att_st"]])
            p.op("scalar", "copy", ctx.PT[:, 0:kend],
                 ctx.ps_tr[:, 0:kend],
                 reads=[B["ps_tr"]], writes=[B["PT"]])
            mms = [(ctx.PT[:, kc * 128:(kc + 1) * 128],
                    Bv_v[:, kc, h * 128:(h + 1) * 128],
                    ctx.ps_h[:, 0:128], [B["PT"], B["Bv"]])
                   for kc in range(nkc)]
            p.mm_group([B["ps_h"]], mms)
            p.op("vector", "tensor_scalar",
                 out=ctx.zfull[:, h * 128:(h + 1) * 128],
                 in0=ctx.ps_h[:, 0:128], scalar1=ast[:, 2:3], scalar2=None,
                 op0=ALU.mult, reads=[B["ps_h"], B["att_st"]],
                 writes=[B["zfull"]])
            if h == 15:
                p.op("gpsimd", "dma_start", out=ctx.z_d[qtl],
                     in_=ctx.zfull[:],
                     reads=[B["zfull"]], writes=[ctx.buf(f"zd_{qtl}")],
                     dma_key=f"zst{qtl % 2}")

        s1_scores(0)
        s2a_evac(0)
        s2b_stats(0)
        s1_scores(1)
        for i in range(n_it):
            s3_exp(i)
            if i + 1 < n_it:
                s2a_evac(i + 1)
                s2b_stats(i + 1)
            if i + 2 < n_it:
                s1_scores(i + 2)
            s4567_tail(i)
        p.flush_bg()
        ctx.RC.transition(l0attn_bufs_phC(ctx))
        p.op("sync", "dma_start",
             out=Afm_v[:], in_=w["wo"][0], reads=[w["wo"][1]],
             writes=[B["Afm"]], dma_key="whld")
        for qtl in range(ntb):
            t = tiles[qtl]
            par = qtl % 2
            p.op("sync", "dma_start", out=ctx.zfull[:], in_=ctx.z_d[qtl],
                 reads=[ctx.buf(f"zd_{qtl}")], writes=[B["zfull"]],
                 dma_key="zld")
            for hc in range(EC):
                sl = slice(hc * 128, (hc + 1) * 128)
                p.op("tensor", "transpose", ctx.ps_tr[:, sl],
                     ctx.zfull[:, sl], ctx.ident[:],
                     reads=[B["zfull"], B["ident"]], writes=[B["ps_tr"]])
            p.op("vector", "tensor_copy", ctx.zf_fm[:], ctx.ps_tr[:],
                 reads=[B["ps_tr"]], writes=[B["zf_fm"]])
            for o in range(4):
                mms = [(ctx.zf_fm[:, hc * 128:(hc + 1) * 128],
                        Afm_v[:, hc, o * 512:(o + 1) * 512],
                        ctx.ps_y[:, o * 512:(o + 1) * 512],
                        [B["zf_fm"], B["Afm"]]) for hc in range(EC)]
                p.mm_group([B[f"ps_q{o}"]], mms)
            xt, bxt = ctx.xt[par], ctx.B[f"xt{par}"]
            p.op("sync", "dma_start", out=xt[:], in_=src_aps[t],
                 reads=[ctx.buf(f"{src_name}_t{t}")], writes=[bxt],
                 dma_key=f"xld{par}")
            p.op("vector", "tensor_tensor", out=xt[:], in0=ctx.ps_y[:],
                 in1=xt[:], op=ALU.add, reads=ctx.psq(0, 2048) + [bxt],
                 writes=[bxt])
            p.op("gpsimd", "dma_start", out=dst_aps[t], in_=xt[:],
                 reads=[bxt], writes=[ctx.buf(f"{dst_name}_t{t}")],
                 dma_key=f"xst{par}")



def alloc_l1ffn(ctx, ntiles):
    nc = ctx.nc
    RA, RB, RC = ctx.RA, ctx.RB, ctx.RC
    ctx.w13 = RA.place("l1f_w13", [128, 4 * 16 * 512], BF16, 0)
    ctx.w2p = RB.place("l1f_w2", [128, 2 * 4 * 2048], BF16, 0)
    ctx.l1_rwT = RC.place("l1f_rwT", [128, 16 * 8], BF16, 0)
    ctx.l1_lg = RC.place("l1f_lg", [128, 48], F32, 256)
    ctx.l1_sl = [RC.place(f"l1f_sl{i}", [128, 512], BF16, 512 + i * 1024)
                 for i in range(2)]
    ctx.l1_h = [RC.place(f"l1f_h{i}", [128, 512], BF16, 2560 + i * 1024)
                for i in range(2)]
    ctx.l1_hf = [RC.place(f"l1f_hf{i}", [128, 2 * 512], BF16,
                          4608 + i * 2048) for i in range(2)]
    ctx.l1_mw = RC.place("l1f_mw", [128, 8 * ntiles], F32, 8704)
    ctx.l1_xn_d = nc.dram_tensor("l1f_xn_d", [ntiles, 128, 2048], BF16,
                                 kind="Internal").ap()
    for n in ["l1_rwT", "l1_lg", "l1_mw", "l1_sl0", "l1_sl1", "l1_h0",
              "l1_h1", "l1_hf0", "l1_hf1"]:
        ctx.B[n] = Buf(n)
    for i in range(4):
        ctx.B[f"w13_{i}"] = Buf(f"w13_{i}")
    for i in range(2):
        ctx.B[f"w2p_{i}"] = Buf(f"w2p_{i}")


def l1ffn_bufs_p0(ctx):
    return [ctx.B[n] for n in ["l1_rwT", "l1_lg", "l1_mw"]]


def l1ffn_bufs_px(ctx):
    return [ctx.B[n] for n in ["l1_rwT", "l1_lg", "l1_mw", "l1_sl0",
                               "l1_sl1", "l1_h0", "l1_h1", "l1_hf0",
                               "l1_hf1"]]


def w13_bufs(ctx):
    return [ctx.B[f"w13_{i}"] for i in range(4)]


def w2p_bufs(ctx):
    return [ctx.B[f"w2p_{i}"] for i in range(2)]


def emit_l1ffn(ctx, src, work, dst, w, ntiles):
    p = ctx.p
    B = ctx.B
    src_name, src_aps = src
    work_name, work_aps = work
    dst_name, dst_aps = dst
    lg = ctx.l1_lg
    l, mask1, l2 = lg[:, 0:8], lg[:, 8:16], lg[:, 16:24]
    mw1, mw2 = lg[:, 24:32], lg[:, 32:40]
    negm1, m1, negm2, m2 = (lg[:, 40:41], lg[:, 41:42], lg[:, 42:43],
                            lg[:, 43:44])
    dd, e2, onep, w1s, w2s = (lg[:, 44:45], lg[:, 45:46], lg[:, 46:47],
                              lg[:, 47:48], lg[:, 43:44])

    def router_tile(t, xf, bxf):
        mms = [(xf[:, ec * 128:(ec + 1) * 128],
                ctx.l1_rwT[:, ec * 8:(ec + 1) * 8],
                ctx.ps_h[:, 0:8], [bxf, B["l1_rwT"]]) for ec in range(EC)]
        p.mm_group([B["ps_h"]], mms)
        blg = B["l1_lg"]
        p.op("vector", "tensor_copy", l, ctx.ps_h[:, 0:8],
             reads=[B["ps_h"]], writes=[blg])
        p.op("vector", "reduce_max", negm1, l, axis=AX.X, negate=True,
             reads=[blg], writes=[blg])
        p.op("vector", "tensor_scalar", out=m1, in0=negm1, scalar1=-1.0,
             scalar2=None, op0=ALU.mult, reads=[blg], writes=[blg])
        p.op("vector", "tensor_scalar", out=mask1, in0=l, scalar1=m1,
             scalar2=None, op0=ALU.is_ge, reads=[blg], writes=[blg])
        p.op("vector", "scalar_tensor_tensor", out=l2, in0=mask1,
             scalar=-1e9, in1=l, op0=ALU.mult, op1=ALU.add,
             reads=[blg], writes=[blg])
        p.op("vector", "reduce_max", negm2, l2, axis=AX.X, negate=True,
             reads=[blg], writes=[blg])
        p.op("vector", "tensor_tensor", out=dd, in0=negm2, in1=negm1,
             op=ALU.subtract, reads=[blg], writes=[blg])
        p.op("scalar", "activation", e2, dd, AF.Exp, bias=0.0, scale=-1.0,
             reads=[blg], writes=[blg])
        p.op("vector", "tensor_scalar", out=onep, in0=e2, scalar1=1.0,
             scalar2=None, op0=ALU.add, reads=[blg], writes=[blg])
        p.op("vector", "reciprocal", w1s, onep, reads=[blg], writes=[blg])
        p.op("vector", "tensor_scalar", out=mw1, in0=l, scalar1=m1,
             scalar2=w1s, op0=ALU.is_ge, op1=ALU.mult,
             reads=[blg], writes=[blg])
        p.op("vector", "tensor_scalar", out=m2, in0=negm2, scalar1=-1.0,
             scalar2=None, op0=ALU.mult, reads=[blg], writes=[blg])
        p.op("vector", "tensor_scalar", out=mw2, in0=l2, scalar1=m2,
             scalar2=None, op0=ALU.is_ge, reads=[blg], writes=[blg])
        p.op("vector", "tensor_tensor", out=m2, in0=e2, in1=w1s,
             op=ALU.mult, reads=[blg], writes=[blg])
        p.op("vector", "tensor_scalar", out=mw2, in0=mw2, scalar1=m2,
             scalar2=None, op0=ALU.mult, reads=[blg], writes=[blg])
        p.op("vector", "tensor_tensor", out=ctx.l1_mw[:, t * 8:(t + 1) * 8],
             in0=mw1, in1=mw2, op=ALU.add,
             reads=[blg], writes=[B["l1_mw"]])

    ctx.RA.transition(w13_bufs(ctx))
    ctx.RB.transition(w2p_bufs(ctx))
    ctx.RC.transition(l1ffn_bufs_px(ctx))
    rwT_v = ctx.l1_rwT[:].rearrange("p (e r) -> p e r", e=16)
    p.op("sync", "dma_start", out=rwT_v[:], in_=w["rwT"][0],
         reads=[w["rwT"][1]], writes=[B["l1_rwT"]], dma_key="wldr")
    w13_v = ctx.w13[:].rearrange("p (s e c) -> p s e c", s=4, e=16)
    w2_v = ctx.w2p[:].rearrange("p (s f c) -> p s f c", s=2, f=4)
    for pr in range(4):
        e0, e1 = 2 * pr, 2 * pr + 1
        for si, (mat, e) in enumerate([("w1", e0), ("w3", e0),
                                       ("w1", e1), ("w3", e1)]):
            p.op("sync", "dma_start", out=w13_v[:, si],
                 in_=w[mat][0][:, e], reads=[w[mat][1]],
                 writes=[B[f"w13_{si}"]], dma_key=f"wld13_{si}")
        for si, e in enumerate([e0, e1]):
            p.op("sync", "dma_start", out=w2_v[:, si],
                 in_=w["w2"][0][:, e], reads=[w["w2"][1]],
                 writes=[B[f"w2p_{si}"]], dma_key=f"wldw2_{si}")
        first = pr == 0
        last = pr == 3
        rd = src_aps if first else work_aps
        rdname = src_name if first else work_name
        wr = dst_aps if last else work_aps
        wrname = dst_name if last else work_name
        for t in range(ntiles):
            par = t % 2
            xf, bxf = ctx.xf[par], ctx.B[f"xf{par}"]
            if first:
                r = ln_to_fm(ctx, par, src_aps[t],
                             ctx.buf(f"{src_name}_t{t}"))
                p.op("gpsimd", "dma_start", out=ctx.l1_xn_d[t], in_=xf[:],
                     reads=[r["xf"]], writes=[ctx.buf(f"l1xnd_{t}")],
                     dma_key=f"xnd{par}")
                router_tile(t, xf, bxf)
            else:
                p.op("sync", "dma_start", out=xf[:], in_=ctx.l1_xn_d[t],
                     reads=[ctx.buf(f"l1xnd_{t}")], writes=[bxf],
                     dma_key=f"xld{par}")
            for s, e in enumerate([e0, e1]):
                x1ps = ctx.ps_h[:, 0:512] if s == 0 else ctx.ps_y[:, 512:1024]
                bx1 = B["ps_h"] if s == 0 else B["ps_q1"]
                x3ps = ctx.ps_y[:, 0:512] if s == 0 else ctx.ps_y[:,
                                                                 1024:1536]
                bx3 = B["ps_q0"] if s == 0 else B["ps_q2"]
                mms = [(xf[:, ec * 128:(ec + 1) * 128],
                        w13_v[:, 2 * s, ec], x1ps,
                        [bxf, B[f"w13_{2 * s}"]]) for ec in range(EC)]
                p.mm_group([bx1], mms)
                mms = [(xf[:, ec * 128:(ec + 1) * 128],
                        w13_v[:, 2 * s + 1, ec], x3ps,
                        [bxf, B[f"w13_{2 * s + 1}"]]) for ec in range(EC)]
                p.mm_group([bx3], mms)
                slt, bslt = ctx.l1_sl[par], B[f"l1_sl{par}"]
                ht, bht = ctx.l1_h[par], B[f"l1_h{par}"]
                p.op("scalar", "activation", slt[:], x1ps,
                     AF.Silu, reads=[bx1], writes=[bslt])
                p.op("vector", "scalar_tensor_tensor", out=ht[:],
                     in0=x3ps,
                     scalar=ctx.l1_mw[:, t * 8 + (e0 + s):t * 8 + (e0 + s) + 1],
                     in1=slt[:], op0=ALU.mult, op1=ALU.mult,
                     reads=[bx3, B["l1_mw"], bslt],
                     writes=[bht])
                btr = B["ps_b"] if s == 0 else B["ps_tr"]
                for j in range(4):
                    sl = slice(j * 128, (j + 1) * 128)
                    p.op("tensor", "transpose",
                         ctx.ps_b[:, sl] if s == 0 else ctx.ps_tr[:, sl],
                         ht[:, sl], ctx.ident[:],
                         reads=[bht, ctx.B["ident"]],
                         writes=[btr])
                p.op("vector", "tensor_copy",
                     ctx.l1_hf[par][:, s * 512:(s + 1) * 512],
                     ctx.ps_b[:, 0:512] if s == 0 else ctx.ps_tr[:, 0:512],
                     reads=[btr], writes=[B[f"l1_hf{par}"]])
            for o in range(4):
                mms = []
                for s in range(2):
                    for f in range(4):
                        mms.append((
                            ctx.l1_hf[par][:, s * 512 + f * 128:
                                           s * 512 + (f + 1) * 128],
                            w2_v[:, s, f, o * 512:(o + 1) * 512],
                            ctx.ps_y[:, o * 512:(o + 1) * 512],
                            [B[f"l1_hf{par}"], B[f"w2p_{s}"]]))
                p.mm_group([B[f"ps_q{o}"]], mms)
            xt, bxt = ctx.xt[par], ctx.B[f"xt{par}"]
            if not first:
                p.op("sync", "dma_start", out=xt[:], in_=rd[t],
                     reads=[ctx.buf(f"{rdname}_t{t}")], writes=[bxt],
                     dma_key=f"rmw{par}")
            p.op("vector", "scalar_tensor_tensor", out=xt[:],
                 in0=ctx.ps_y[:], scalar=0.5, in1=xt[:],
                 op0=ALU.mult, op1=ALU.add,
                 reads=ctx.psq(0, 2048) + [bxt], writes=[bxt])
            p.op("gpsimd", "dma_start", out=wr[t], in_=xt[:],
                 reads=[bxt], writes=[ctx.buf(f"{wrname}_t{t}")],
                 dma_key=f"xst{par}")



def alloc_l1attn(ctx, ntb):
    nc = ctx.nc
    RA, RB, RC = ctx.RA, ctx.RB, ctx.RC
    ctx.l1a_Q = RA.place("l1a_Q", [128, 16 * 2048], BF16, 0)
    ctx.l1a_PT = RB.place("l1a_PT", [128, 16 * 2048], BF16, 0)
    ctx.l1a_WhB = [RB.place(f"l1a_Wh{i}", [128, 16 * 1024], BF16, i * 32768)
                   for i in range(2)]
    ctx.l1a_qtmp = RC.place("l1a_qtmp", [128, 2048], BF16, 32768)
    ctx.l1a_kfm = [RC.place("l1a_kfm0", [128, 2048], BF16, 0)] * 2
    ctx.l1a_s16 = RC.place("l1a_s16", [128, 2048], mybir.dt.float16, 4096)
    ctx.l1a_kw = [RC.place(f"l1a_kw{i}", [128, 2048], BF16, 8192 + i * 4096)
                  for i in range(2)]
    ctx.l1a_vd = RC.place("l1a_vd", [128, 16 * 512], BF16, 16384)
    ctx.l1a_av = RC.place("l1a_av", [128, 512], F32, 32768)
    ctx.l1a_ast = RC.place("l1a_ast", [128, 8], F32, 34816)
    ctx.l1a_abf = RC.place("l1a_abf", [128, 2048], BF16, 0)
    ctx.l1a_zf = RC.place("l1a_zf", [128, 2048], BF16, 4096)
    ctx.fcb_f = RC.place("l1a_fcb_f", [1, 2048], F32, 8192)
    ctx.fcb = RC.place("l1a_fcb", [1, 2048], BF16, 16384)
    ctx.ones_bf = nc.alloc_sbuf_tensor("c_ones_bf", [1, 128], BF16)
    ctx.l1a_xn_d = nc.dram_tensor("l1a_xn_d", [ntb, 128, 2048], BF16,
                                  kind="Internal").ap()
    ctx.l1a_k_d = nc.dram_tensor("l1a_k_d", [ntb, 128, 2048], BF16,
                                 kind="Internal").ap()
    ctx.l1a_v_d = nc.dram_tensor("l1a_v_d", [ntb, 128, 2048], BF16,
                                 kind="Internal").ap()
    ctx.l1a_at_d = nc.dram_tensor("l1a_at_d", [ntb, 128, 2048], F32,
                                  kind="Internal").ap()
    for n in (["l1a_Q", "l1a_PT", "l1a_Wh0", "l1a_Wh1", "l1a_qtmp",
               "l1a_kfm0", "l1a_s16", "l1a_kw0", "l1a_kw1", "l1a_vd",
               "l1a_ast", "l1a_av", "l1a_abf", "l1a_zf", "ones_bf",
               "fcb_f", "fcb"]):
        ctx.B[n] = Buf(n)
    ctx.B["l1a_kfm1"] = ctx.B["l1a_kfm0"]


def l1attn_bufs_ph1(ctx):
    return [ctx.B[n] for n in ["l1a_qtmp"]]


def l1attn_bufs_ph23(ctx):
    return [ctx.B[n] for n in ["l1a_kfm0", "l1a_s16", "l1a_kw0", "l1a_kw1",
                               "l1a_vd", "l1a_ast", "l1a_av", "l1a_qtmp"]]


def l1attn_bufs_ph4(ctx):
    return [ctx.B[n] for n in ["l1a_abf", "l1a_zf", "l1a_ast", "fcb_f",
                               "fcb"]]


def emit_l1attn(ctx, src, dst, w, b_tiles_list):
    p = ctx.p
    B = ctx.B
    src_name, src_aps = src
    dst_name, dst_aps = dst
    scale = 1.0 / (2048.0 ** 0.5)
    Q_v = ctx.l1a_Q[:].rearrange("p (d n) -> p d n", d=16)
    PT_v = ctx.l1a_PT[:].rearrange("p (m n) -> p m n", m=16)
    vd_v = ctx.l1a_vd[:].rearrange("p (m c) -> p m c", m=16)
    p.op("gpsimd", "memset", ctx.ones_bf[:], 1.0, writes=[B["ones_bf"]])

    for b, tiles in enumerate(b_tiles_list):
        ntb = len(tiles)
        nend = ntb * 128
        for i, t in enumerate(tiles):
            par = i % 2
            r = ln_to_fm(ctx, par, src_aps[t], ctx.buf(f"{src_name}_t{t}"))
            p.op("gpsimd", "dma_start", out=ctx.l1a_xn_d[i],
                 in_=ctx.xf[par][:], reads=[r["xf"]],
                 writes=[ctx.buf(f"l1axn_{i}")], dma_key=f"xnd{par}")
        for h in range(3):
            ctx.RA.transition([B["l1a_Q"]])
            ctx.RB.transition([B["l1a_Wh0"], B["l1a_Wh1"]])
            ctx.RC.transition(l1attn_bufs_ph1(ctx))
            jobs = [(mat, half) for mat in ["q", "k", "v"]
                    for half in range(2)]
            for j, (mat, half) in enumerate(jobs):
                wh = ctx.l1a_WhB[j % 2]
                bwh = B[f"l1a_Wh{j % 2}"]
                p.op("sync", "dma_start",
                     out=wh[:].rearrange("p (e c) -> p e c", e=16),
                     in_=w[mat][0][:, h, half], reads=[w[mat][1]],
                     writes=[bwh], dma_key=f"whld{j % 2}")
                for i in range(ntb):
                    par = i % 2
                    xfp, bxf = ctx.xf[par], ctx.B[f"xf{par}"]
                    p.op("sync", "dma_start", out=xfp[:],
                         in_=ctx.l1a_xn_d[i],
                         reads=[ctx.buf(f"l1axn_{i}")], writes=[bxf],
                         dma_key=f"xld{par}")
                    for o in range(2):
                        q = (2 * i + o) % 4
                        mms = [(xfp[:, ec * 128:(ec + 1) * 128],
                                wh[:, ec * 1024 + o * 512:
                                   ec * 1024 + (o + 1) * 512],
                                ctx.ps_y[:, q * 512:(q + 1) * 512],
                                [bxf, bwh]) for ec in range(EC)]
                        p.mm_group([B[f"ps_q{q}"]], mms)
                        psl = ctx.ps_y[:, q * 512:(q + 1) * 512]
                        prd = [B[f"ps_q{q}"]]
                        if mat == "q":
                            p.op("scalar", "mul",
                                 ctx.l1a_qtmp[:, o * 512:(o + 1) * 512],
                                 psl, scale,
                                 reads=prd, writes=[B["l1a_qtmp"]])
                        else:
                            p.op("scalar", "copy",
                                 ctx.l1a_qtmp[:, half * 1024 + o * 512:
                                              half * 1024 + (o + 1) * 512],
                                 psl, reads=prd, writes=[B["l1a_qtmp"]])
                    if mat == "q":
                        for jj in range(8):
                            sl = slice(jj * 128, (jj + 1) * 128)
                            p.op("tensor", "transpose", ctx.ps_tr[:, sl],
                                 ctx.l1a_qtmp[:, sl], ctx.ident[:],
                                 reads=[B["l1a_qtmp"], B["ident"]],
                                 writes=[B["ps_tr"]])
                        p.op("vector", "tensor_copy",
                             Q_v[:, half * 8:(half + 1) * 8,
                                 i * 128:(i + 1) * 128],
                             ctx.ps_tr[:, 0:1024].rearrange(
                                 "p (d n) -> p d n", d=8),
                             reads=[B["ps_tr"]], writes=[B["l1a_Q"]])
                    else:
                        dd = (ctx.l1a_k_d if mat == "k" else ctx.l1a_v_d)
                        bname = f"l1a{mat}d_{i}"
                        p.op("gpsimd", "dma_start",
                             out=dd[i][:, half * 1024:(half + 1) * 1024],
                             in_=ctx.l1a_qtmp[:, half * 1024:
                                              (half + 1) * 1024],
                             reads=[B["l1a_qtmp"]],
                             writes=[ctx.buf(bname)], dma_key="kvst")
            ctx.RB.transition([B["l1a_PT"]])
            ctx.RC.transition(l1attn_bufs_ph23(ctx))
            for mt in range(ntb):
                kpar = mt % 2
                kw, bkw = ctx.l1a_kw[kpar], B[f"l1a_kw{kpar}"]
                kfm, bkfm = ctx.l1a_kfm[kpar], B[f"l1a_kfm{kpar}"]
                p.op("sync", "dma_start", out=kw[:],
                     in_=ctx.l1a_k_d[mt], reads=[ctx.buf(f"l1akd_{mt}")],
                     writes=[bkw], dma_key=f"kld{kpar}")
                for dt in range(EC):
                    sl = slice(dt * 128, (dt + 1) * 128)
                    p.op("tensor", "transpose", ctx.ps_tr[:, sl],
                         kw[:, sl], ctx.ident[:],
                         reads=[bkw, B["ident"]], writes=[B["ps_tr"]])
                p.op("vector", "tensor_copy", kfm[:], ctx.ps_tr[:],
                     reads=[B["ps_tr"]], writes=[bkfm])
                for o in range(nend // 512):
                    mms = [(kfm[:, dt * 128:(dt + 1) * 128],
                            Q_v[:, dt, o * 512:(o + 1) * 512],
                            ctx.ps_y[:, o * 512:(o + 1) * 512],
                            [bkfm, B["l1a_Q"]]) for dt in range(EC)]
                    p.mm_group([B[f"ps_q{o}"]], mms)
                negmax, rsum, recip = (ctx.l1a_ast[:, 0:1],
                                       ctx.l1a_ast[:, 1:2],
                                       ctx.l1a_ast[:, 2:3])
                p.op("vector", "tensor_copy", ctx.l1a_s16[:, 0:nend],
                     ctx.ps_y[:, 0:nend],
                     reads=ctx.psq(0, nend), writes=[B["l1a_s16"]])
                p.op("vector", "reduce_max", negmax, ctx.l1a_s16[:, 0:nend],
                     axis=AX.X, negate=True, reads=[B["l1a_s16"]],
                     writes=[B["l1a_ast"]])
                p.op("scalar", "activation", PT_v[:, mt, 0:nend],
                     ctx.l1a_s16[:, 0:nend],
                     AF.Exp, bias=negmax, accum_out=rsum,
                     reads=[B["l1a_s16"], B["l1a_ast"]],
                     writes=[B["l1a_PT"], B["l1a_ast"]])
                p.op("vector", "reciprocal", recip, rsum,
                     reads=[B["l1a_ast"]], writes=[B["l1a_ast"]])
                p.op("vector", "tensor_scalar", out=PT_v[:, mt, 0:nend],
                     in0=PT_v[:, mt, 0:nend], scalar1=recip, scalar2=None,
                     op0=ALU.mult, reads=[B["l1a_PT"], B["l1a_ast"]],
                     writes=[B["l1a_PT"]])
            for dseg in range(4):
                p.op("sync", "dma_start",
                     out=vd_v[:, 0:ntb],
                     in_=ctx.l1a_v_d[0:ntb, :, dseg * 512:(dseg + 1) * 512]
                     .rearrange("m p c -> p m c"),
                     reads=[ctx.buf(f"l1avd_{i}") for i in range(ntb)],
                     writes=[B["l1a_vd"]], dma_key="vld")
                for nt in range(ntb):
                    q = nt % 4
                    mms = [(PT_v[:, mc, nt * 128:(nt + 1) * 128],
                            vd_v[:, mc],
                            ctx.ps_y[:, q * 512:(q + 1) * 512],
                            [B["l1a_PT"], B["l1a_vd"]]) for mc in range(ntb)]
                    p.mm_group([B[f"ps_q{q}"]], mms)
                    at_sl = ctx.l1a_at_d[nt][:, dseg * 512:(dseg + 1) * 512]
                    bat = ctx.buf(f"l1aat_{nt}")
                    p.op("vector", "tensor_copy", ctx.l1a_av[:],
                         ctx.ps_y[:, q * 512:(q + 1) * 512],
                         reads=[B[f"ps_q{q}"]], writes=[B["l1a_av"]])
                    if h == 0:
                        p.op("gpsimd", "dma_start", out=at_sl,
                             in_=ctx.l1a_av[:], reads=[B["l1a_av"]],
                             writes=[bat], dma_key="atst")
                    else:
                        p.op("gpsimd", "dma_start", out=at_sl,
                             in_=ctx.l1a_av[:], accum_op=ALU.add,
                             reads=[B["l1a_av"]],
                             writes=[bat], dma_key="atst")
        ctx.RA.transition([B["l1a_Q"]])
        ctx.RC.transition(l1attn_bufs_ph4(ctx))
        p.op("sync", "dma_start", out=ctx.fcb_f[:], in_=w["fcb"],
             writes=[B["fcb_f"]], dma_key="wbr")
        p.op("vector", "tensor_copy", ctx.fcb[:], ctx.fcb_f[:],
             reads=[B["fcb_f"]], writes=[B["fcb"]])
        p.op("sync", "dma_start", out=Q_v[:], in_=w["fcT"][0],
             reads=[w["fcT"][1]], writes=[B["l1a_Q"]], dma_key="whld0")
        for nt in range(ntb):
            t = tiles[nt]
            par = nt % 2
            xa, bxa = ctx.xt[par], ctx.B[f"xt{par}"]
            p.op("sync", "dma_start", out=xa[:], in_=ctx.l1a_at_d[nt],
                 reads=[ctx.buf(f"l1aat_{nt}")], writes=[bxa],
                 dma_key=f"atld{par}")
            p.op("vector", "tensor_scalar", out=ctx.l1a_abf[:], in0=xa[:],
                 scalar1=1.0 / 3.0, scalar2=None, op0=ALU.mult,
                 reads=[bxa], writes=[B["l1a_abf"]])
            for dt in range(EC):
                sl = slice(dt * 128, (dt + 1) * 128)
                p.op("tensor", "transpose", ctx.ps_tr[:, sl],
                     ctx.l1a_abf[:, sl], ctx.ident[:],
                     reads=[B["l1a_abf"], B["ident"]], writes=[B["ps_tr"]])
            p.op("vector", "tensor_copy", ctx.l1a_zf[:], ctx.ps_tr[:],
                 reads=[B["ps_tr"]], writes=[B["l1a_zf"]])
            for o in range(4):
                mms = [(ctx.l1a_zf[:, dt * 128:(dt + 1) * 128],
                        Q_v[:, dt, o * 512:(o + 1) * 512],
                        ctx.ps_y[:, o * 512:(o + 1) * 512],
                        [B["l1a_zf"], B["l1a_Q"]]) for dt in range(EC)]
                mms.append((ctx.ones_bf[:],
                            ctx.fcb[:, o * 512:(o + 1) * 512],
                            ctx.ps_y[:, o * 512:(o + 1) * 512],
                            [B["ones_bf"], B["fcb"]]))
                p.mm_group([B[f"ps_q{o}"]], mms)
            xb_, bxb = ctx.xt[1 - par], ctx.B[f"xt{1 - par}"]
            p.op("sync", "dma_start", out=xb_[:], in_=src_aps[t],
                 reads=[ctx.buf(f"{src_name}_t{t}")], writes=[bxb],
                 dma_key=f"xld{1 - par}")
            p.op("vector", "tensor_tensor", out=xb_[:], in0=ctx.ps_y[:],
                 in1=xb_[:], op=ALU.add, reads=ctx.psq(0, 2048) + [bxb],
                 writes=[bxb])
            p.op("gpsimd", "dma_start", out=dst_aps[t], in_=xb_[:],
                 reads=[bxb], writes=[ctx.buf(f"{dst_name}_t{t}")],
                 dma_key=f"xst{1 - par}")


def build_program(ntb, nb):
    ntiles = ntb * nb
    N = ntiles * 128
    nc = bass.Bass("TRN2", target_bir_lowering=False)
    p = Prog(nc)
    ctx = Ctx(p, nc)

    ein = lambda name, shape: nc.dram_tensor(
        name, shape, F32, kind="ExternalInput").ap()
    xin = ein("xin", [N, D])
    w0 = {"WqT": ein("W0qT", [D, D]), "WkT": ein("W0kT", [D, D]),
          "WvT": ein("W0vT", [D, D]), "WoT": ein("W0oT", [D, D])}
    a_t, b_t = ein("A_T", [D, 512]), ein("B_T", [512, D])
    wr0, br0 = ein("Wr0T", [D, 8]), ein("br0", [1, 8])
    r1 = ein("r1T", [D, 8])
    w1_, w2_, w3_ = (ein("w1", [8, D, 512]), ein("w2", [8, 512, D]),
                     ein("w3", [8, D, 512]))
    q1, k1, v1 = (ein("W1qT", [3, D, D]), ein("W1kT", [3, D, D]),
                  ein("W1vT", [3, D, D]))
    fcT, fcb = ein("fcT", [D, D]), ein("fcb", [1, D])
    xout = nc.dram_tensor("xout", [N, D], F32, kind="ExternalOutput").ap()
    xd = nc.dram_tensor("xd", [N, D], F32, kind="Internal").ap()

    xin_t = xin.rearrange("(t p) d -> t p d", p=128)
    xd_t = xd.rearrange("(t p) d -> t p d", p=128)
    xout_t = xout.rearrange("(t p) d -> t p d", p=128)
    xin_l = [xin_t[t] for t in range(ntiles)]
    xd_l = [xd_t[t] for t in range(ntiles)]
    xout_l = [xout_t[t] for t in range(ntiles)]
    b_tiles = [list(range(b * ntb, (b + 1) * ntb)) for b in range(nb)]

    alloc_l0ffn(ctx)
    alloc_l0attn(ctx, ntb)
    alloc_l1ffn(ctx, ntiles)
    alloc_l1attn(ctx, ntb)
    ctx.emit_consts()

    w1f = {"rwT": r1, "w1": [w1_[e] for e in range(8)],
           "w2": [w2_[e] for e in range(8)], "w3": [w3_[e] for e in range(8)]}
    w1a = {"WqT": [q1[h] for h in range(3)], "WkT": [k1[h] for h in range(3)],
           "WvT": [v1[h] for h in range(3)], "fcT": fcT, "fcb": fcb}

    wc = build_wcache(ctx, dict(
        l0_AT=a_t, l0_BT=b_t, l0_Wr=wr0,
        l0a_k=w0["WkT"], l0a_q=w0["WqT"], l0a_v=w0["WvT"], l0a_wo=w0["WoT"],
        l1_rw=r1, l1_w1=[w1_[e] for e in range(8)],
        l1_w2=[w2_[e] for e in range(8)], l1_w3=[w3_[e] for e in range(8)],
        l1a_q=[q1[h] for h in range(3)], l1a_k=[k1[h] for h in range(3)],
        l1a_v=[v1[h] for h in range(3)], l1a_fc=fcT))

    w0c = {"k": wc["l0a_k"], "q": wc["l0a_q"], "v": wc["l0a_v"],
           "wo": wc["l0a_wo"]}
    w1fc = {"rwT": wc["l1_rw"], "w1": wc["l1_w1"], "w2": wc["l1_w2"],
            "w3": wc["l1_w3"]}
    w1ac = {"q": wc["l1a_q"], "k": wc["l1a_k"], "v": wc["l1a_v"],
            "fcT": wc["l1a_fc"], "fcb": fcb}

    with nc.named_scope("l0f1"):
        ctx.RC.transition(l0ffn_bufs(ctx))
        load_l0ffn_weights(ctx, wc, br0)
        emit_l0ffn(ctx, ("xin", xin_l), ("xd", xd_l), ntiles)
    with nc.named_scope("l0a"):
        emit_l0attn(ctx, ("xd", xd_l), ("xd", xd_l), w0c, b_tiles)
    p.flush_bg()
    with nc.named_scope("l0f2"):
        ctx.RC.transition(l0ffn_bufs(ctx))
        load_l0ffn_weights(ctx, wc, br0)
        emit_l0ffn(ctx, ("xd", xd_l), ("xd", xd_l), ntiles)
    with nc.named_scope("l1f1"):
        emit_l1ffn(ctx, ("xd", xd_l), ("xd", xd_l), ("xd", xd_l), w1fc,
                   ntiles)
    with nc.named_scope("l1a"):
        emit_l1attn(ctx, ("xd", xd_l), ("xd", xd_l), w1ac, b_tiles)
    with nc.named_scope("l1f2"):
        emit_l1ffn(ctx, ("xd", xd_l), ("xd", xd_l), ("xout", xout_l), w1fc,
                   ntiles)
    p.wait_all("gpsimd",
               [ctx.B[f"xout_t{t}"].last_write for t in range(ntiles)])
    return nc


def build_wcache(ctx, srcs):
    nc = ctx.nc
    wc = {}

    eager = {"l0_AT", "l0_BT", "l0_Wr"}

    def seg(name, shape, jobs):
        wc[name] = mk_wseg(ctx, name, shape, jobs, eager=name in eager)

    seg("l0_AT", [128, 16, 512],
        [((lambda a, ec=ec: a[:, ec]),
          srcs["l0_AT"][ec * 128:(ec + 1) * 128, :]) for ec in range(16)])
    seg("l0_BT", [128, 4, 2048],
        [((lambda a, ec=ec, o=o: a[:, ec, o * 1024:(o + 1) * 1024]),
          srcs["l0_BT"][ec * 128:(ec + 1) * 128, o * 1024:(o + 1) * 1024])
         for ec in range(4) for o in range(2)])
    seg("l0_Wr", [128, 16, 8],
        [((lambda a, ec=ec: a[:, ec]),
          srcs["l0_Wr"][ec * 128:(ec + 1) * 128, :]) for ec in range(16)])
    for m in ["k", "q", "v"]:
        seg(f"l0a_{m}", [128, 2, 16, 1024],
            [((lambda a, hh=hh, ec=ec: a[:, hh, ec]),
              srcs[f"l0a_{m}"][ec * 128:(ec + 1) * 128,
                               hh * 1024:(hh + 1) * 1024])
             for hh in range(2) for ec in range(16)])
    seg("l0a_wo", [128, 16, 2048],
        [((lambda a, hc=hc, o=o: a[:, hc, o * 1024:(o + 1) * 1024]),
          srcs["l0a_wo"][hc * 128:(hc + 1) * 128, o * 1024:(o + 1) * 1024])
         for hc in range(16) for o in range(2)])
    seg("l1_rw", [128, 16, 8],
        [((lambda a, ec=ec: a[:, ec]),
          srcs["l1_rw"][ec * 128:(ec + 1) * 128, :]) for ec in range(16)])
    for m in ["w1", "w3"]:
        seg(f"l1_{m}", [128, 8, 16, 512],
            [((lambda a, e=e, ec=ec: a[:, e, ec]),
              srcs[f"l1_{m}"][e][ec * 128:(ec + 1) * 128, :])
             for e in range(8) for ec in range(16)])
    seg("l1_w2", [128, 8, 4, 2048],
        [((lambda a, e=e, f=f, o=o: a[:, e, f, o * 1024:(o + 1) * 1024]),
          srcs["l1_w2"][e][f * 128:(f + 1) * 128, o * 1024:(o + 1) * 1024])
         for e in range(8) for f in range(4) for o in range(2)])
    for m in ["q", "k", "v"]:
        seg(f"l1a_{m}", [128, 3, 2, 16, 1024],
            [((lambda a, h=h, hh=hh, ec=ec: a[:, h, hh, ec]),
              srcs[f"l1a_{m}"][h][ec * 128:(ec + 1) * 128,
                                  hh * 1024:(hh + 1) * 1024])
             for h in range(3) for hh in range(2) for ec in range(16)])
    seg("l1a_fc", [128, 16, 2048],
        [((lambda a, dt=dt, o=o: a[:, dt, o * 1024:(o + 1) * 1024]),
          srcs["l1a_fc"][dt * 128:(dt + 1) * 128, o * 1024:(o + 1) * 1024])
         for dt in range(16) for o in range(2)])
    return wc


def host_inputs(inputs):
    C = np.ascontiguousarray
    f = np.float32
    w = {
        "W0qT": C(inputs["L0_attn_Wq"].T.astype(f, copy=False)),
        "W0kT": C(inputs["L0_attn_Wk"].T.astype(f, copy=False)),
        "W0vT": C(inputs["L0_attn_Wv"].T.astype(f, copy=False)),
        "W0oT": C(inputs["L0_attn_Wo"].T.astype(f, copy=False)),
        "A_T": C(np.transpose(inputs["L0_ffn_A"], (2, 0, 1)).reshape(D, 512)),
        "B_T": C(np.transpose(inputs["L0_ffn_B"], (0, 2, 1)).reshape(512, D)),
        "Wr0T": C(inputs["L0_ffn_Wr"].T),
        "br0": C(inputs["L0_ffn_br"].reshape(1, 8)),
        "r1T": C(inputs["L1_ffn_router_w"].T),
        "w1": C(inputs["L1_ffn_w1"]),
        "w2": C(inputs["L1_ffn_w2"]),
        "w3": C(inputs["L1_ffn_w3"]),
        "W1qT": C(np.transpose(inputs["L1_attn_Wq"], (0, 2, 1))),
        "W1kT": C(np.transpose(inputs["L1_attn_Wk"], (0, 2, 1))),
        "W1vT": C(np.transpose(inputs["L1_attn_Wv"], (0, 2, 1))),
        "fcT": C(inputs["L1_attn_fc_w"].T),
        "fcb": C(inputs["L1_attn_fc_b"].reshape(1, D)),
    }
    return {k: np.asarray(v, dtype=np.float32) for k, v in w.items()}


def kernel(**inputs):
    x = np.asarray(inputs["L0_attn_resid_pre"], dtype=np.float32)
    w = host_inputs(inputs)
    ncares = 8
    nb = 2
    ntb = 16
    nc = build_program(ntb, nb)
    in_maps = []
    for c in range(ncares):
        m = dict(w)
        m["xin"] = np.ascontiguousarray(
            x[c * nb:(c + 1) * nb].reshape(nb * 2048, 2048))
        in_maps.append(m)
    res = run_bass_kernel_spmd(nc, in_maps, core_ids=list(range(ncares)))
    out = np.concatenate(
        [r["xout"].reshape(nb, 2048, 2048) for r in res.results], axis=0)
    return out
